# version 03
# baseline (speedup 1.0000x reference)
import numpy as np
from contextlib import ExitStack

import concourse.bass as bass
import concourse.mybir as mybir
from concourse.bass_utils import run_bass_kernel_spmd

F32 = mybir.dt.float32
BF16 = mybir.dt.bfloat16
AF = mybir.ActivationFunctionType
ALU = mybir.AluOpType
AX = mybir.AxisListType.X

NCORE = 8
BATCH, L, D = 2, 8192, 1024
DC, DI, NS, DTR, NL = 512, 1024, 16, 32, 2
NE, FF = 14, 2048
T = (BATCH * L) // NCORE
HALF = 1024
NTOK = T // 128
EPS = 1e-5

_PC = {}
_pc_off = 0
def _pc(name, n):
    global _pc_off
    _PC[name] = _pc_off
    _pc_off += n
_pc("ln_g", 8); _pc("ln_b", 8)
_pc("alog", NL * 128)
_pc("Dp", NL * 8)
_pc("cb", NL * 8)
_pc("cw", NL * 32)
_pc("dtb", NL * 8)
_pc("rmsw", NL * 4)
_pc("po_b", 8)
_pc("msel", 8)
_pc("mselp", 8)
_pc("hmask", 1)
_pc("b1", NE * 16)
PCN = _pc_off


class KB:
    def __init__(self):
        self.nc = bass.Bass("TRN2", target_bir_lowering=False,
                            detect_race_conditions=False)
        self.stk = ExitStack()
        self.cnt = {}
        self.sems = {}
        self.V = self.nc.vector
        self.S = self.nc.scalar
        self.Tn = self.nc.tensor
        self.G = self.nc.gpsimd
        self.SY = self.nc.sync
        self.E = {"v": self.V, "s": self.S, "t": self.Tn, "g": self.G,
                  "y": self.SY}
        for name in ["v", "s", "t", "g", "cc"]:
            self.sems[name] = self.stk.enter_context(self.nc.semaphore(name))
            self.cnt[name] = 0
        self.pools = {}
        for pname, eng, nslot in [("dl", self.SY, 6), ("ds", self.S, 4),
                                  ("dg", self.G, 4)]:
            slots = []
            for j in range(nslot):
                sn = f"{pname}{j}"
                self.sems[sn] = self.stk.enter_context(self.nc.semaphore(sn))
                self.cnt[sn] = 0
                slots.append(sn)
            self.pools[pname] = {"eng": eng, "slots": slots, "i": 0}

    def _dma(self, pname, out_ap, in_ap):
        p = self.pools[pname]
        sn = p["slots"][p["i"] % len(p["slots"])]
        p["i"] += 1
        if self.cnt[sn] > 0:
            p["eng"].wait_ge(self.sems[sn], self.cnt[sn])
        instr = p["eng"].dma_start(out=out_ap, in_=in_ap)
        instr.then_inc(self.sems[sn], 16)
        self.cnt[sn] += 16
        return (sn, self.cnt[sn])

    def dma_quiesce(self, eng, pnames=("dl", "ds", "dg")):
        for pname in pnames:
            for sn in self.pools[pname]["slots"]:
                if self.cnt[sn] > 0:
                    eng.wait_ge(self.sems[sn], self.cnt[sn])

    _ENG_OF_SEM = {"v": "v", "s": "s", "t": "t", "g": "g"}

    def inc(self, name, instr, amt=1, commit=True):
        instr.then_inc(self.sems[name], amt)
        self.cnt[name] += amt
        if commit and name in self._ENG_OF_SEM:
            self.E[name].wait_ge(self.sems[name], self.cnt[name])
        return (name, self.cnt[name])

    def tok(self, name):
        return (name, self.cnt[name])

    def wait(self, eng, token):
        name, v = token
        if v > 0:
            eng.wait_ge(self.sems[name], v)

    def vrt(self, instr):
        t = self.inc("v", instr)
        self.wait(self.V, t)
        return t

    def bar(self, engines="vstg"):
        toks = []
        for ch in engines:
            toks.append(self.inc(ch, self.E[ch].nop()))
        for ch in engines:
            for t in toks:
                if t[0] != ch:
                    self.wait(self.E[ch], t)
        for t in toks:
            self.wait(self.SY, t)
            if "g" not in engines:
                self.wait(self.G, t)
        self.dma_quiesce(self.SY, ("ds", "dg"))
        self.dma_quiesce(self.G, ("ds", "dl"))

    def ld(self, out_ap, in_ap):
        return self._dma("dl", out_ap, in_ap)

    def stdma(self, out_ap, in_ap):
        return self._dma("ds", out_ap, in_ap)

    def gld(self, out_ap, in_ap):
        return self._dma("dg", out_ap, in_ap)

    _uid = 0

    def _u(self, name):
        KB._uid += 1
        return f"{name}_{KB._uid}"

    def sb(self, name, shape, dtype=F32, stack=None):
        return (stack or self.stk).enter_context(
            self.nc.sbuf_tensor(self._u(name), shape, dtype))

    def ps(self, name, shape, stack=None):
        return (stack or self.stk).enter_context(
            self.nc.psum_tensor(self._u(name), shape, F32))

    def col(self, name, idx=0):
        c = _PC[name] + idx
        return self.pcols[:, c:c + 1]

    def pmm_init(self):
        self.pmm = [self.ps("pmm0", [128, 512]), self.ps("pmm1", [128, 512])]
        self.pmm_free = [None, None]
        self.pmm_i = 0

    def pmm_next(self):
        i = self.pmm_i
        self.pmm_i ^= 1
        if self.pmm_free[i] is not None:
            self.wait(self.Tn, self.pmm_free[i])
        return i

    def build(self):
        nc = self.nc
        self.d_src = nc.dram_tensor("src_tm", [T, D], F32, kind="ExternalInput")
        self.d_halo = nc.dram_tensor("halo_src", [4, D], F32, kind="ExternalInput")
        self.d_pcols = nc.dram_tensor("pcols", [128, PCN], F32, kind="ExternalInput")
        self.d_idt = nc.dram_tensor("identity", [128, 128], F32, kind="ExternalInput")
        self.d_bsel = nc.dram_tensor("bsel", [48, NS * 128], F32, kind="ExternalInput")
        self.d_inWT = [nc.dram_tensor(f"inWT_p{l}", [128, 4 * FF], F32, kind="ExternalInput")
                       for l in range(NL)]
        self.d_moWT = [nc.dram_tensor(f"moWT_p{l}", [128, 8 * 512], F32, kind="ExternalInput")
                       for l in range(NL)]
        self.d_xpWT = [nc.dram_tensor(f"xpWT_p{l}", [128, 8 * 64], F32, kind="ExternalInput")
                       for l in range(NL)]
        self.d_dtWT = [nc.dram_tensor(f"dtWT_p{l}", [32, 1024], F32, kind="ExternalInput")
                       for l in range(NL)]
        self.d_poWT = nc.dram_tensor("poWT_p", [128, 8 * 1024], F32, kind="ExternalInput")
        self.d_gateWT = nc.dram_tensor("gateWT_p", [128, 8 * 14], F32, kind="ExternalInput")
        self.d_w1 = [nc.dram_tensor(f"w1_p{e}", [128, 8 * FF], F32, kind="ExternalInput")
                     for e in range(NE)]
        self.d_w2 = [nc.dram_tensor(f"w2_p{e}", [128, 16 * 1024], F32, kind="ExternalInput")
                     for e in range(NE)]
        self.d_b2r = nc.dram_tensor("b2_rows", [NE, 1024], F32, kind="ExternalInput")
        self.d_out = nc.dram_tensor("out_tm", [T, D], F32, kind="ExternalOutput")
        self.d_x1 = nc.dram_tensor("x1_d", [512, T], F32, kind="Internal")
        self.d_x2 = nc.dram_tensor("x2_d", [512, T], F32, kind="Internal")
        self.d_h = nc.dram_tensor("h_d", [512, T], F32, kind="Internal")
        self.d_xn = nc.dram_tensor("xn_d", [T, D], F32, kind="Internal")
        self.d_ag1i = nc.dram_tensor("ag1_in", [128, 256], F32, kind="Internal")
        self.d_ag1o = nc.dram_tensor("ag1_out", [NCORE * 128, 256], F32, kind="Internal")
        self.d_ag2i = nc.dram_tensor("ag2_in", [128, 12], F32, kind="Internal")
        self.d_ag2o = nc.dram_tensor("ag2_out", [NCORE * 128, 12], F32, kind="Internal")

        self.v_x1 = self.d_x1.ap().rearrange("(c p) t -> p c t", p=128)
        self.v_x2 = self.d_x2.ap().rearrange("(c p) t -> p c t", p=128)
        self.v_h = self.d_h.ap().rearrange("(c p) t -> p c t", p=128)

        self.pcols = self.sb("pcols_sb", [128, PCN])
        self.idt = self.sb("idt_sb", [128, 128])
        self.ones_c = self.sb("ones_c", [128, 1])
        self.ones_r = self.sb("ones_r", [1, 128])
        self.epsc = self.sb("epsc", [128, 1])
        self.A_sb = self.sb("A_sb", [128, NL * 128])
        self.halo_x1 = self.sb("halo_x1", [128, 12])
        self.halo_h = self.sb("halo_h", [128, 12])
        self.hstate = self.sb("hstate", [128, 128])
        self.pfull = self.sb("pfull", [128, 128])
        self.hin = self.sb("hin", [128, 128])
        self.cs = self.sb("cs_tot", [128, 8])

        self.pmm_init()

        t1 = self.ld(self.pcols[:, :], self.d_pcols[:, :])
        t2 = self.ld(self.idt[:, :], self.d_idt[:, :])
        for e in "vstg":
            self.wait(self.E[e], t1)
            self.wait(self.E[e], t2)
        self.V.memset(self.ones_c[:, :], 1.0)
        self.V.memset(self.ones_r[:, :], 1.0)
        self.V.memset(self.epsc[:, :], EPS)
        stok = self.inc("s", self.S.activation(
            self.A_sb[:, :],
            self.pcols[:, _PC["alog"]:_PC["alog"] + NL * 128], AF.Exp))
        self.wait(self.V, stok)
        vt = self.inc("v", self.V.tensor_single_scalar(
            self.A_sb[:, :], self.A_sb[:, :], -1.0, ALU.mult))
        self.wait(self.S, vt)
        self.bar()

        self.phase_ln1()
        for l in range(NL):
            self.mamba_layer(l)
        self.phase_po_ln0()
        self.phase_ffn()
        self.phase_lnf()
        self.dma_quiesce(self.SY)
        return nc

    def phase_ln1(self):
        with ExitStack() as es:
            sx = [self.sb(f"ln_sx{j}", [128, D], stack=es) for j in range(2)]
            sq = self.sb("ln_sq", [128, D], stack=es)
            xn = self.sb("ln_xn", [128, D], stack=es)
            mu = self.sb("ln_mu", [128, 1], stack=es)
            var = self.sb("ln_var", [128, 1], stack=es)
            mu2 = self.sb("ln_mu2", [128, 1], stack=es)
            rstd = self.sb("ln_rstd", [128, 1], stack=es)
            stg = [self.sb(f"ln_st{j}", [128, 8 * 128], stack=es) for j in range(2)]
            ptr = [self.ps(f"ln_ptr{j}", [128, 128], stack=es) for j in range(4)]
            ptr_free = [None, None, None, None]
            stg_free = [None, None]
            sx_free = [None, None]
            xn_free = None
            for i in range(NTOK + 1):
                halo = (i == NTOK)
                par = i % 2
                if sx_free[par] is not None:
                    for t_f in sx_free[par]:
                        self.wait(self.SY, t_f)
                if halo:
                    vt = self.inc("v", self.V.memset(sx[par][:, :], 0.0))
                    self.wait(self.SY, vt)
                    dtok = self.ld(sx[par][0:4, :], self.d_halo[:, :])
                else:
                    dtok = self.ld(sx[par][:, :], self.d_src[i * 128:(i + 1) * 128, :])
                self.wait(self.V, dtok)
                self.wait(self.S, dtok)
                self.vrt(self.V.reduce_sum(mu[:, :1], sx[par][:, :], axis=AX))
                self.vrt(self.V.tensor_single_scalar(mu[:, :1], mu[:, :1], 1.0 / D, ALU.mult))
                st_ = self.inc("s", self.S.activation(sq[:, :], sx[par][:, :], AF.Square))
                self.wait(self.V, st_)
                self.vrt(self.V.reduce_sum(var[:, :1], sq[:, :], axis=AX))
                self.vrt(self.V.tensor_single_scalar(var[:, :1], var[:, :1], 1.0 / D, ALU.mult))
                self.vrt(self.V.tensor_mul(mu2[:, :1], mu[:, :1], mu[:, :1]))
                vt = self.vrt(self.V.tensor_sub(var[:, :1], var[:, :1], mu2[:, :1]))
                self.wait(self.S, vt)
                st_ = self.inc("s", self.S.activation(rstd[:, :1], var[:, :1], AF.Sqrt,
                                                      bias=self.epsc[:, :1]))
                self.wait(self.V, st_)
                self.vrt(self.V.reciprocal(rstd[:, :1], rstd[:, :1]))
                if xn_free is not None:
                    self.wait(self.V, xn_free)
                vt = self.inc("v", self.V.tensor_scalar(
                    xn[:, :], sx[par][:, :], mu[:, :1], rstd[:, :1],
                    ALU.subtract, ALU.mult))
                sx_free[par] = [vt, self.tok("s")]
                self.wait(self.Tn, vt)
                if stg_free[par] is not None:
                    self.wait(self.S, stg_free[par])
                nch = 4 if halo else 8
                ev_last = None
                for c in range(nch):
                    p2 = c % 4
                    if ptr_free[p2] is not None:
                        self.wait(self.Tn, ptr_free[p2])
                    tt_ = self.inc("t", self.Tn.transpose(
                        ptr[p2][:, :],
                        xn[:, c * 128:(c + 1) * 128], self.idt[:, :]))
                    self.wait(self.S, tt_)
                    if halo:
                        ev = self.S.activation(
                            self.halo_x1[:, c * 3:(c + 1) * 3],
                            ptr[p2][:, 0:3],
                            AF.Identity, bias=self.col("ln_b", c),
                            scale=self.col("ln_g", c))
                    else:
                        ev = self.S.activation(
                            stg[par][:, c * 128:(c + 1) * 128],
                            ptr[p2][:, :],
                            AF.Identity, bias=self.col("ln_b", c),
                            scale=self.col("ln_g", c))
                    ev_last = self.inc("s", ev)
                    ptr_free[p2] = ev_last
                xn_free = self.inc("t", self.Tn.nop())
                if halo:
                    self.wait(self.V, ev_last)
                    self.inc("v", self.V.tensor_copy(self.halo_h[:, :], self.halo_x1[:, :]))
                else:
                    ts0 = i * 128
                    self.wait(self.S, ev_last)
                    self.stdma(self.v_x1[:, :, ts0:ts0 + 128], stg[par][:, 0:512])
                    self.stdma(self.v_x2[:, :, ts0:ts0 + 128], stg[par][:, 512:1024])
                    dtok = self.stdma(self.v_h[:, :, ts0:ts0 + 128], stg[par][:, 0:512])
                    stg_free[par] = dtok
            self.bar()

    def compute_rmsh(self, l, rmsh):
        with ExitStack() as es:
            pst = self.ps("rms_pst", [1, T], es)
            hst = [self.sb(f"rms_h{j}", [128, T], stack=es) for j in range(2)]
            sqb = [self.sb(f"rms_sq{j}", [128, T], stack=es) for j in range(2)]
            rr = self.sb("rms_rr", [1, T], stack=es)
            pb = self.ps("rms_pb", [128, 512], es)
            mm_toks = [None, None]
            sq_done = [None, None]
            for cc in range(4):
                par = cc % 2
                if sq_done[par] is not None:
                    self.wait(self.SY, sq_done[par])
                dtok = self.ld(hst[par][:, :], self.v_h[:, cc, :])
                self.wait(self.S, dtok)
                if mm_toks[par] is not None:
                    self.wait(self.S, mm_toks[par])
                st_ = self.inc("s", self.S.activation(sqb[par][:, :], hst[par][:, :], AF.Square))
                sq_done[par] = st_
                self.wait(self.Tn, st_)
                last = None
                for tt in range(4):
                    last = self.Tn.matmul(pst[:, tt * 512:(tt + 1) * 512],
                                          self.ones_c[:, :1],
                                          sqb[par][:, tt * 512:(tt + 1) * 512],
                                          start=(cc == 0), stop=(cc == 3))
                mm_toks[par] = self.inc("t", last)
            self.wait(self.V, mm_toks[(4 - 1) % 2])
            vt = self.vrt(self.V.tensor_single_scalar(rr[:1, :], pst[:1, :], 1.0 / DC, ALU.mult))
            self.wait(self.S, vt)
            st_ = self.inc("s", self.S.activation(rr[:1, :], rr[:1, :], AF.Sqrt,
                                                  bias=self.epsc[0:1, :1]))
            self.wait(self.V, st_)
            vt = self.vrt(self.V.reciprocal(rr[:1, :], rr[:1, :]))
            self.wait(self.Tn, vt)
            self.wait(self.SY, self.tok("s"))
            hs_used = [None, None]
            for tt in range(4):
                bt = self.inc("t", self.Tn.matmul(pb[:, :], self.ones_r[:1, :],
                                                  rr[:1, tt * 512:(tt + 1) * 512],
                                                  start=True, stop=True))
                vlast = None
                for cc in range(4):
                    par = cc % 2
                    if hs_used[par] is not None:
                        self.wait(self.SY, hs_used[par])
                    dtok = self.ld(hst[par][:, 0:512], self.v_h[:, cc, tt * 512:(tt + 1) * 512])
                    self.wait(self.V, dtok)
                    self.wait(self.V, bt)
                    vlast = self.inc("v", self.V.scalar_tensor_tensor(
                        rmsh[:, cc * T + tt * 512: cc * T + (tt + 1) * 512],
                        hst[par][:, 0:512], self.col("rmsw", l * 4 + cc),
                        pb[:, :], ALU.mult, ALU.mult))
                    hs_used[par] = vlast
                self.wait(self.Tn, vlast)
            self.bar()

    def halo_xcpre(self, l, inWT, hxc):
        with ExitStack() as es:
            psq = self.ps("hx_psq", [1, 4], es)
            pb = self.ps("hx_pb", [128, 4], es)
            pm = self.ps("hx_pm", [128, 4], es)
            sqh = self.sb("hx_sq", [128, 12], stack=es)
            rr = self.sb("hx_rr", [1, 4], stack=es)
            rmf = self.sb("hx_rmf", [128, 12], stack=es)
            rmh = self.sb("hx_rmh", [128, 12], BF16, stack=es)
            st_ = self.inc("s", self.S.activation(sqh[:, :], self.halo_h[:, :], AF.Square))
            self.wait(self.Tn, st_)
            last = None
            for cc in range(4):
                last = self.Tn.matmul(psq[:1, 0:3], self.ones_c[:, :1],
                                      sqh[:, cc * 3:(cc + 1) * 3],
                                      start=(cc == 0), stop=(cc == 3))
            t_ = self.inc("t", last)
            self.wait(self.V, t_)
            vt = self.vrt(self.V.tensor_single_scalar(rr[:1, 0:3], psq[:1, 0:3], 1.0 / DC, ALU.mult))
            self.wait(self.S, vt)
            st_ = self.inc("s", self.S.activation(rr[:1, 0:3], rr[:1, 0:3], AF.Sqrt,
                                                  bias=self.epsc[0:1, :1]))
            self.wait(self.V, st_)
            vt = self.vrt(self.V.reciprocal(rr[:1, 0:3], rr[:1, 0:3]))
            self.wait(self.Tn, vt)
            bt = self.inc("t", self.Tn.matmul(pb[:, 0:3], self.ones_r[:1, :],
                                              rr[:1, 0:3], start=True, stop=True))
            self.wait(self.V, bt)
            vlast = None
            for cc in range(4):
                vlast = self.inc("v", self.V.scalar_tensor_tensor(
                    rmf[:, cc * 3:(cc + 1) * 3], self.halo_h[:, cc * 3:(cc + 1) * 3],
                    self.col("rmsw", l * 4 + cc), pb[:, 0:3], ALU.mult, ALU.mult))
            self.wait(self.S, vlast)
            st_ = self.inc("s", self.S.activation(rmh[:, :], rmf[:, :], AF.Identity))
            self.wait(self.Tn, st_)
            for j in range(8):
                last = None
                for cc in range(4):
                    last = self.Tn.matmul(pm[:, 0:3],
                                          inWT[:, cc * FF + j * 128:cc * FF + (j + 1) * 128],
                                          rmh[:, cc * 3:(cc + 1) * 3],
                                          start=(cc == 0), stop=(cc == 3))
                t_ = self.inc("t", last)
                self.wait(self.S, t_)
                st_ = self.inc("s", self.S.activation(
                    hxc[:, j * 3:(j + 1) * 3], pm[:, 0:3], AF.Identity,
                    scale=self.col("hmask")))
                self.wait(self.Tn, st_)
            self.bar()

    def mamba_layer(self, l):
        with ExitStack() as es:
            inWT = self.sb("inWT", [128, 4 * FF], BF16, es)
            moWT = self.sb("moWT", [128, 8 * 512], BF16, es)
            xpWT = self.sb("xpWT", [128, 8 * 64], BF16, es)
            dtWT = self.sb("dtWT", [32, 1024], BF16, es)
            rmsh = self.sb("rmsh", [128, 4 * T], BF16, es)
            dbc = self.sb("dbc", [64, T], F32, es)
            dtbf = self.sb("dtbf", [32, T], BF16, es)
            hxc = self.sb("hxc", [128, 24], F32, es)
            self.bsel = self.sb("bsel_sb", [48, NS * 128], stack=es)
            self.ld(self.bsel[:, :], self.d_bsel[:, :])
            gtoks = [self.gld(inWT[:, :], self.d_inWT[l][:, :]),
                     self.gld(moWT[:, :], self.d_moWT[l][:, :]),
                     self.gld(xpWT[:, :], self.d_xpWT[l][:, :]),
                     self.gld(dtWT[:, :], self.d_dtWT[l][:, :])]
            self.compute_rmsh(l, rmsh)
            for e in "vst":
                for gt in gtoks:
                    self.wait(self.E[e], gt)
            self.halo_xcpre(l, inWT, hxc)
            self.mamba_pass(l, True, inWT, moWT, xpWT, dtWT, rmsh, dbc, dtbf, hxc)
            self.relay(l)
            self.mamba_pass(l, False, inWT, moWT, xpWT, dtWT, rmsh, dbc, dtbf, hxc)
            if l < NL - 1:
                self.halo_relay()
            self.bar()

    def _xc_chunk(self, l, hf, c, inWT, xcpre, cvo, out_sb, out_off, halo3):
        for tt in range(2):
            i = self.pmm_next()
            last = None
            for kc in range(4):
                last = self.Tn.matmul(
                    self.pmm[i][:, :],
                    inWT[:, kc * FF + c * 128:kc * FF + (c + 1) * 128],
                    self.rmsh_cur[:, kc * T + hf * HALF + tt * 512:
                                  kc * T + hf * HALF + (tt + 1) * 512],
                    start=(kc == 0), stop=(kc == 3))
            t_ = self.inc("t", last)
            self.wait(self.S, t_)
            ev = self.inc("s", self.S.activation(
                xcpre[:, 3 + tt * 512:3 + (tt + 1) * 512], self.pmm[i][:, :],
                AF.Identity))
            self.pmm_free[i] = ev
        self.wait(self.V, self.tok("s"))
        self.inc("v", self.V.tensor_copy(xcpre[:, 0:3], halo3[:, c * 3:(c + 1) * 3]))
        self.V.tensor_scalar(cvo[:, :], xcpre[:, 0:HALF],
                             self.col("cw", l * 32 + 0 * 8 + c), None, ALU.mult)
        for k in range(1, 4):
            self.V.scalar_tensor_tensor(cvo[:, :], xcpre[:, k:k + HALF],
                                        self.col("cw", l * 32 + k * 8 + c),
                                        cvo[:, :], ALU.mult, ALU.add)
        vt = self.inc("v", self.V.tensor_copy(halo3[:, c * 3:(c + 1) * 3],
                                              xcpre[:, 1024:1027]))
        self.wait(self.S, vt)
        st_ = self.inc("s", self.S.activation(out_sb[:, out_off:out_off + HALF],
                                              cvo[:, :], AF.Silu,
                                              bias=self.col("cb", l * 8 + c)))
        return st_

    def _delta_chunk(self, l, hf, c, dtWT, dtbf, pD, delta, scratch):
        self.wait(self.Tn, self.tok("s"))
        last = None
        for tt in range(2):
            last = self.Tn.matmul(
                pD[:, tt * 512:(tt + 1) * 512], dtWT[:, c * 128:(c + 1) * 128],
                dtbf[:, hf * HALF + tt * 512:hf * HALF + (tt + 1) * 512],
                start=True, stop=True)
        t_ = self.inc("t", last)
        self.wait(self.S, t_)
        self.S.activation(scratch[:, 0:HALF], pD[:, :], AF.Exp,
                          bias=self.col("dtb", l * 8 + c))
        st_ = self.inc("s", self.S.activation(delta[:, c * HALF:(c + 1) * HALF],
                                              scratch[:, 0:HALF], AF.Ln, bias=1.0))
        return st_

    def mamba_pass(self, l, pa, inWT, moWT, xpWT, dtWT, rmsh, dbc, dtbf, hxc):
        self.rmsh_cur = rmsh
        with ExitStack() as pes:
            halo3 = self.sb("halo3", [128, 24], stack=pes)
            xcpre = self.sb("xcpre", [128, 1032], stack=pes)
            cvo = self.sb("cvo", [128, HALF], stack=pes)
            xcf = self.sb("xcf", [128, HALF], stack=pes)
            dAb = self.sb("dAb", [128, 2 * HALF], stack=pes)
            dBxb = self.sb("dBxb", [128, 2 * HALF], stack=pes)
            hscr = self.sb("hscr", [128, HALF], stack=pes)
            bmc = self.sb("bmc", [48, HALF], stack=pes)
            if pa:
                xcbf = self.sb("xcbf", [128, HALF], BF16, stack=pes)
                state = self.hstate
                vt = self.inc("v", self.V.memset(state[:, :], 0.0))
                self.inc("v", self.V.memset(self.cs[:, :], 0.0))
            else:
                state = self.hin
            vt = self.inc("v", self.V.tensor_copy(halo3[:, :], hxc[:, :]))
            self.wait(self.S, vt)
            self.wait(self.Tn, vt)

            for hf in range(2):
                with ExitStack() as hes:
                    yh = None
                    if not pa:
                        yh = self.sb("y_h", [128, 8 * HALF], stack=hes)
                    dues = ExitStack()
                    delta = self.sb("delta_h", [128, 8 * HALF], stack=dues)
                    u = self.sb("u_h", [128, 8 * HALF], stack=dues)
                    with ExitStack() as ces:
                        pD = self.ps("pD", [128, HALF], ces)
                        if pa:
                            pdbc = self.ps("pdbc", [64, T], ces)
                            for c in range(8):
                                st_ = self._xc_chunk(l, hf, c, inWT, xcpre, cvo,
                                                     u, c * HALF, halo3)
                                self.wait(self.S, self.tok("t"))
                                cst = self.inc("s", self.S.activation(
                                    xcbf[:, :], u[:, c * HALF:(c + 1) * HALF],
                                    AF.Identity))
                                self.wait(self.Tn, cst)
                                last = None
                                for tt in range(2):
                                    last = self.Tn.matmul(
                                        pdbc[:, hf * HALF + tt * 512:
                                             hf * HALF + (tt + 1) * 512],
                                        xpWT[:, c * 64:(c + 1) * 64],
                                        xcbf[:, tt * 512:(tt + 1) * 512],
                                        start=(c == 0), stop=(c == 7))
                                t_ = self.inc("t", last)
                                self.wait(self.S, t_)
                            t_ = self.tok("t")
                            self.wait(self.S, t_)
                            self.S.activation(dbc[0:64, hf * HALF:(hf + 1) * HALF],
                                              pdbc[:, hf * HALF:(hf + 1) * HALF],
                                              AF.Identity)
                            st_ = self.inc("s", self.S.activation(
                                dtbf[:, hf * HALF:(hf + 1) * HALF],
                                dbc[0:32, hf * HALF:(hf + 1) * HALF], AF.Identity))
                            self.wait(self.Tn, st_)
                            self.wait(self.V, st_)
                            for c in range(8):
                                st_ = self._delta_chunk(l, hf, c, dtWT, dtbf, pD,
                                                        delta, cvo)
                                self.wait(self.V, st_)
                                vt = self.inc("v", self.V.tensor_mul(
                                    u[:, c * HALF:(c + 1) * HALF],
                                    delta[:, c * HALF:(c + 1) * HALF],
                                    u[:, c * HALF:(c + 1) * HALF]))
                                self.wait(self.Tn, vt)
                                self.vrt(self.V.reduce_sum(
                                    xcf[:, 0:1], delta[:, c * HALF:(c + 1) * HALF],
                                    axis=AX))
                                if hf == 0:
                                    self.vrt(self.V.tensor_copy(
                                        self.cs[:, c:c + 1], xcf[:, 0:1]))
                                else:
                                    self.vrt(self.V.tensor_add(
                                        self.cs[:, c:c + 1], self.cs[:, c:c + 1],
                                        xcf[:, 0:1]))
                        else:
                            for c in range(8):
                                st_ = self._delta_chunk(l, hf, c, dtWT, dtbf, pD,
                                                        delta, cvo)
                                xt = self._xc_chunk(l, hf, c, inWT, xcpre, cvo,
                                                    xcf, 0, halo3)
                                self.wait(self.V, xt)
                                self.V.tensor_scalar(
                                    yh[:, c * HALF:(c + 1) * HALF], xcf[:, 0:HALF],
                                    self.col("Dp", l * 8 + c), None, ALU.mult)
                                vt = self.inc("v", self.V.tensor_mul(
                                    u[:, c * HALF:(c + 1) * HALF],
                                    delta[:, c * HALF:(c + 1) * HALF],
                                    xcf[:, 0:HALF]))
                                self.wait(self.S, vt)
                                self.wait(self.Tn, vt)

                    self.wait(self.SY, self.tok("s"))
                    self.wait(self.SY, self.tok("t"))
                    dtok = self.ld(bmc[0:16, :], dbc[32:48, hf * HALF:(hf + 1) * HALF])
                    if not pa:
                        dtok = self.ld(bmc[32:48, :],
                                       dbc[48:64, hf * HALF:(hf + 1) * HALF])
                    self.wait(self.Tn, dtok)
                    with ExitStack() as nes:
                        pwB = self.ps("pwB", [128, HALF], nes)
                        pwC = self.ps("pwC", [128, HALF], nes)
                        vdone = []
                        for n in range(16):
                            if n > 0:
                                self.wait(self.Tn, vdone[n * 8 - 1])
                            last = None
                            for tt in range(2):
                                last = self.Tn.matmul(
                                    pwB[:, tt * 512:(tt + 1) * 512],
                                    self.bsel[0:16, n * 128:(n + 1) * 128],
                                    bmc[0:16, tt * 512:(tt + 1) * 512],
                                    start=True, stop=True)
                            if not pa:
                                for tt in range(2):
                                    last = self.Tn.matmul(
                                        pwC[:, tt * 512:(tt + 1) * 512],
                                        self.bsel[32:48, n * 128:(n + 1) * 128],
                                        bmc[32:48, tt * 512:(tt + 1) * 512],
                                        start=True, stop=True)
                            bev = self.inc("t", last)
                            for c in range(8):
                                idx = n * 8 + c
                                par = idx % 2
                                if idx >= 2:
                                    self.wait(self.S, vdone[idx - 2])
                                st_ = self.inc("s", self.S.activation(
                                    dAb[:, par * HALF:(par + 1) * HALF],
                                    delta[:, c * HALF:(c + 1) * HALF], AF.Exp,
                                    scale=self.A_sb[:, l * 128 + c * 16 + n:
                                                    l * 128 + c * 16 + n + 1]),
                                    commit=(idx < 2))
                                self.wait(self.V, st_)
                                self.wait(self.V, bev)
                                self.V.tensor_mul(
                                    dBxb[:, par * HALF:(par + 1) * HALF],
                                    u[:, c * HALF:(c + 1) * HALF],
                                    pwB[:, :])
                                col = c * 16 + n
                                self.V.tensor_tensor_scan(
                                    hscr[:, :], dAb[:, par * HALF:(par + 1) * HALF],
                                    dBxb[:, par * HALF:(par + 1) * HALF],
                                    state[:, col:col + 1], ALU.mult, ALU.add)
                                self.V.tensor_copy(state[:, col:col + 1],
                                                   hscr[:, HALF - 1:HALF])
                                if not pa:
                                    self.V.tensor_mul(
                                        dBxb[:, par * HALF:(par + 1) * HALF],
                                        hscr[:, :], pwC[:, :])
                                    vt = self.inc("v", self.V.tensor_add(
                                        yh[:, c * HALF:(c + 1) * HALF],
                                        yh[:, c * HALF:(c + 1) * HALF],
                                        dBxb[:, par * HALF:(par + 1) * HALF]))
                                else:
                                    vt = self.inc("v", self.V.engine_nop())
                                vdone.append(vt)
                    self.wait(self.S, self.tok("v"))
                    self.bar("vs")
                    dues.close()

                    if pa and hf == 1:
                        self.vrt(self.V.engine_nop())
                        vt = None
                        for c in range(8):
                            vt = self.inc("v", self.V.tensor_mul(
                                self.pfull[:, c * 16:(c + 1) * 16],
                                self.cs[:, c:c + 1].to_broadcast([128, 16]),
                                self.A_sb[:, l * 128 + c * 16:l * 128 + (c + 1) * 16]))
                        self.wait(self.S, vt)
                        self.inc("s", self.S.activation(self.pfull[:, :],
                                                        self.pfull[:, :], AF.Exp))

                    if not pa:
                        with ExitStack() as fes:
                            ybf = self.sb("ybf_h", [128, 8 * HALF], BF16, stack=fes)
                            hseg = self.sb("hseg", [128, 4 * HALF], stack=fes)
                            szb = self.sb("szb", [128, 512], stack=fes)
                            htok = self.ld(hseg[:, :].rearrange(
                                "p (c x) -> p c x", c=4),
                                self.v_h[:, :, hf * HALF:(hf + 1) * HALF])
                            for c in range(8):
                                for tt in range(2):
                                    i = self.pmm_next()
                                    last = None
                                    for kc in range(4):
                                        last = self.Tn.matmul(
                                            self.pmm[i][:, :],
                                            inWT[:, kc * FF + (8 + c) * 128:
                                                 kc * FF + (9 + c) * 128],
                                            rmsh[:, kc * T + hf * HALF + tt * 512:
                                                 kc * T + hf * HALF + (tt + 1) * 512],
                                            start=(kc == 0), stop=(kc == 3))
                                    t_ = self.inc("t", last)
                                    self.wait(self.S, t_)
                                    st_ = self.inc("s", self.S.activation(
                                        szb[:, :], self.pmm[i][:, :], AF.Silu))
                                    self.pmm_free[i] = st_
                                    self.wait(self.V, st_)
                                    vt = self.inc("v", self.V.tensor_mul(
                                        yh[:, c * HALF + tt * 512:
                                           c * HALF + (tt + 1) * 512],
                                        yh[:, c * HALF + tt * 512:
                                           c * HALF + (tt + 1) * 512],
                                        szb[:, :]))
                                    self.wait(self.S, vt)
                                vt = self.tok("v")
                                self.wait(self.S, vt)
                                self.inc("s", self.S.activation(
                                    ybf[:, c * HALF:(c + 1) * HALF],
                                    yh[:, c * HALF:(c + 1) * HALF], AF.Identity))
                            st_ = self.tok("s")
                            self.wait(self.Tn, st_)
                            self.wait(self.V, htok)
                            for tt in range(2):
                                for m in range(4):
                                    i = self.pmm_next()
                                    last = None
                                    for c in range(8):
                                        last = self.Tn.matmul(
                                            self.pmm[i][:, :],
                                            moWT[:, c * 512 + m * 128:
                                                 c * 512 + (m + 1) * 128],
                                            ybf[:, c * HALF + tt * 512:
                                                c * HALF + (tt + 1) * 512],
                                            start=(c == 0), stop=(c == 7))
                                    t_ = self.inc("t", last)
                                    self.wait(self.V, t_)
                                    vt = self.inc("v", self.V.tensor_add(
                                        hseg[:, m * HALF + tt * 512:
                                             m * HALF + (tt + 1) * 512],
                                        hseg[:, m * HALF + tt * 512:
                                             m * HALF + (tt + 1) * 512],
                                        self.pmm[i][:, :]))
                                    self.pmm_free[i] = vt
                            vt = self.tok("v")
                            self.wait(self.S, vt)
                            htok = self.stdma(
                                self.v_h[:, :, hf * HALF:(hf + 1) * HALF],
                                hseg[:, :].rearrange("p (c x) -> p c x", c=4))
                            self.wait(self.SY, htok)
            self.bar()

    def relay(self, l):
        vt = self.tok("v")
        self.wait(self.G, vt)
        self.wait(self.G, self.tok("s"))
        g1 = self.gld(self.d_ag1i[:, 0:128], self.hstate[:, :])
        g2 = self.gld(self.d_ag1i[:, 128:256], self.pfull[:, :])
        self.wait(self.G, g1)
        self.wait(self.G, g2)
        ct = self.inc("cc", self.G.collective_compute(
            "AllGather", ALU.bypass,
            replica_groups=[list(range(NCORE))],
            ins=[self.d_ag1i.ap().opt()],
            outs=[self.d_ag1o.ap().opt()]), 1)
        self.wait(self.G, ct)
        with ExitStack() as es:
            gat = self.sb("relay_gat", [128, NCORE * 256], stack=es)
            tmp = self.sb("relay_tmp", [128, 128], stack=es)
            gt = self.gld(gat[:, :].rearrange("p (j c) -> p j c", j=NCORE),
                          self.d_ag1o.ap().rearrange("(j p) c -> p j c", p=128))
            self.wait(self.V, gt)
            self.V.memset(self.hin[:, :], 0.0)
            for j in range(NCORE):
                self.V.tensor_scalar(tmp[:, :],
                                     gat[:, j * 256 + 128:j * 256 + 256],
                                     1.0, self.col("mselp", j),
                                     ALU.subtract, ALU.mult)
                self.V.tensor_scalar(tmp[:, :], tmp[:, :], 1.0, None, ALU.add)
                self.V.tensor_mul(self.hin[:, :], self.hin[:, :], tmp[:, :])
                self.V.tensor_scalar(tmp[:, :], gat[:, j * 256:j * 256 + 128],
                                     self.col("mselp", j), None, ALU.mult)
                self.V.tensor_add(self.hin[:, :], self.hin[:, :], tmp[:, :])
            self.vrt(self.V.engine_nop())
            self.bar()

    def halo_relay(self):
        self.dma_quiesce(self.G, ("ds",))
        gt = self.gld(self.d_ag2i[:, :], self.v_h[:, :, T - 3:T])
        self.wait(self.G, gt)
        ct = self.inc("cc", self.G.collective_compute(
            "AllGather", ALU.bypass,
            replica_groups=[list(range(NCORE))],
            ins=[self.d_ag2i.ap().opt()],
            outs=[self.d_ag2o.ap().opt()]), 1)
        self.wait(self.G, ct)
        with ExitStack() as es:
            gat = self.sb("halo_gat", [128, NCORE * 12], stack=es)
            gt = self.gld(gat[:, :].rearrange("p (j c) -> p j c", j=NCORE),
                          self.d_ag2o.ap().rearrange("(j p) c -> p j c", p=128))
            self.wait(self.V, gt)
            self.V.memset(self.halo_h[:, :], 0.0)
            vt = None
            for j in range(NCORE):
                vt = self.inc("v", self.V.scalar_tensor_tensor(
                    self.halo_h[:, :], gat[:, j * 12:(j + 1) * 12],
                    self.col("msel", j), self.halo_h[:, :], ALU.mult, ALU.add))
            self.bar()

    def phase_po_ln0(self):
        self.xffn = self.sb("xffn_bf", [128, 8 * T], BF16)
        self.acc = self.sb("acc_tm", [128, 16 * 1024], F32)
        self.coeff = self.sb("coeff_tm", [128, 16 * NE], F32)
        self.b2sb = self.sb("b2_sb", [NE, 1024], F32)
        with ExitStack() as es:
            poWT = self.sb("poWT", [128, 8 * 1024], BF16, es)
            gateWT = self.sb("gateWT", [128, 8 * 14], F32, es)
            outbf = self.sb("outbf", [128, 8 * T], BF16, es)
            stage = self.sb("po_stage", [128, T], stack=es)
            gt = self.gld(poWT[:, :], self.d_poWT[:, :])
            gt2 = self.ld(gateWT[:, :], self.d_gateWT[:, :])
            with ExitStack() as es0:
                stage2 = self.sb("po_stage2", [128, T], stack=es0)
                for c in range(4):
                    dt1 = self.ld(stage[:, :], self.v_x1[:, c, :])
                    self.wait(self.S, dt1)
                    st_ = self.inc("s", self.S.activation(
                        outbf[:, c * T:(c + 1) * T], stage[:, :], AF.Identity))
                    self.wait(self.SY, st_)
                for c in range(4):
                    dt1 = self.ld(stage[:, :], self.v_x2[:, c, :])
                    dt2 = self.ld(stage2[:, :], self.v_h[:, c, :])
                    self.wait(self.V, dt2)
                    self.wait(self.V, dt1)
                    vt = self.inc("v", self.V.tensor_add(stage2[:, :], stage2[:, :],
                                                         stage[:, :]))
                    self.wait(self.S, vt)
                    st_ = self.inc("s", self.S.activation(
                        outbf[:, (4 + c) * T:(5 + c) * T], stage2[:, :], AF.Identity))
                    self.wait(self.SY, st_)
                self.bar("vs")
            bt = self.ld(self.b2sb[:, :], self.d_b2r[:, :])
            xab = self.sb("xa_b", [128, 8 * 512], stack=es)
            sqb = self.sb("po_sq", [128, 512], stack=es)
            xnf = self.sb("po_xnf", [128, 512], stack=es)
            srcs = self.sb("po_srcs", [128, D], stack=es)
            rows = self.sb("po_rows", [1, 3 * 512], stack=es)
            with ExitStack() as pses:
                ptr = self.ps("po_ptr", [128, 512], pses)
                pgate = self.ps("po_pgate", [128, 512], pses)
                pst = self.ps("po_pst", [1, 512], pses)
                pst2 = self.ps("po_pst2", [1, 512], pses)
                pbc = self.ps("po_pbc", [128, 512], pses)
                pbc2 = self.ps("po_pbc2", [128, 512], pses)
                self.wait(self.Tn, gt)
                self.wait(self.Tn, gt2)
                self.wait(self.Tn, self.tok("s"))
                for tt in range(4):
                    t0 = tt * 512
                    for m in range(8):
                        i = self.pmm_next()
                        last = None
                        for kc in range(8):
                            last = self.Tn.matmul(
                                self.pmm[i][:, :],
                                poWT[:, kc * 1024 + m * 128:kc * 1024 + (m + 1) * 128],
                                outbf[:, kc * T + t0:kc * T + t0 + 512],
                                start=(kc == 0), stop=(kc == 7))
                        t_ = self.inc("t", last)
                        vsrc = self.v_x1 if m < 4 else self.v_x2
                        dt1 = self.ld(stage[:, 0:512], vsrc[:, m % 4, t0:t0 + 512])
                        self.wait(self.S, t_)
                        st_ = self.inc("s", self.S.activation(
                            xab[:, m * 512:(m + 1) * 512], self.pmm[i][:, :],
                            AF.Identity, bias=self.col("po_b", m)))
                        self.pmm_free[i] = st_
                        self.wait(self.V, st_)
                        self.wait(self.V, dt1)
                        vt = self.inc("v", self.V.tensor_add(
                            xab[:, m * 512:(m + 1) * 512],
                            xab[:, m * 512:(m + 1) * 512], stage[:, 0:512]))
                        self.wait(self.SY, vt)
                    for b in range(4):
                        dt1 = self.ld(srcs[:, :],
                                      self.d_src[t0 + b * 128:t0 + (b + 1) * 128, :])
                        self.wait(self.Tn, dt1)
                        for m in range(8):
                            tt_ = self.inc("t", self.Tn.transpose(
                                ptr[:, b * 128:(b + 1) * 128],
                                srcs[:, m * 128:(m + 1) * 128], self.idt[:, :]))
                            self.wait(self.V, tt_)
                            vt = self.inc("v", self.V.tensor_add(
                                xab[:, m * 512 + b * 128:m * 512 + (b + 1) * 128],
                                xab[:, m * 512 + b * 128:m * 512 + (b + 1) * 128],
                                ptr[:, b * 128:(b + 1) * 128]))
                            self.wait(self.Tn, vt)
                        self.wait(self.SY, self.tok("v"))
                    vt = self.tok("v")
                    self.wait(self.Tn, vt)
                    self.wait(self.S, vt)
                    last = None
                    for m in range(8):
                        last = self.Tn.matmul(pst[:1, :], self.ones_c[:, :1],
                                              xab[:, m * 512:(m + 1) * 512],
                                              start=(m == 0), stop=(m == 7))
                    t_sum = self.inc("t", last)
                    sq_t = None
                    for m in range(8):
                        if sq_t is not None:
                            self.wait(self.S, self.tok("t"))
                        st_ = self.inc("s", self.S.activation(
                            sqb[:, :], xab[:, m * 512:(m + 1) * 512], AF.Square))
                        self.wait(self.Tn, st_)
                        sq_t = self.inc("t", self.Tn.matmul(
                            pst2[:1, :], self.ones_c[:, :1], sqb[:, :],
                            start=(m == 0), stop=(m == 7)))
                    self.wait(self.V, t_sum)
                    self.wait(self.V, sq_t)
                    self.vrt(self.V.tensor_single_scalar(
                        rows[:1, 0:512], pst[:1, :], 1.0 / D, ALU.mult))
                    self.vrt(self.V.tensor_single_scalar(
                        rows[:1, 512:1024], pst2[:1, :], 1.0 / D, ALU.mult))
                    self.vrt(self.V.tensor_mul(rows[:1, 1024:1536],
                                               rows[:1, 0:512], rows[:1, 0:512]))
                    vt = self.vrt(self.V.tensor_sub(rows[:1, 512:1024],
                                                    rows[:1, 512:1024],
                                                    rows[:1, 1024:1536]))
                    self.wait(self.S, vt)
                    st_ = self.inc("s", self.S.activation(
                        rows[:1, 1024:1536], rows[:1, 512:1024], AF.Sqrt,
                        bias=self.epsc[0:1, :1]))
                    self.wait(self.V, st_)
                    vt = self.vrt(self.V.reciprocal(rows[:1, 1024:1536],
                                                    rows[:1, 1024:1536]))
                    self.wait(self.Tn, vt)
                    bt1 = self.inc("t", self.Tn.matmul(
                        pbc[:, :], self.ones_r[:1, :], rows[:1, 0:512],
                        start=True, stop=True))
                    bt2 = self.inc("t", self.Tn.matmul(
                        pbc2[:, :], self.ones_r[:1, :], rows[:1, 1024:1536],
                        start=True, stop=True))
                    self.wait(self.V, bt2)
                    for m in range(8):
                        self.V.tensor_sub(xnf[:, :], xab[:, m * 512:(m + 1) * 512],
                                          pbc[:, :])
                        vt = self.inc("v", self.V.tensor_mul(xnf[:, :], xnf[:, :],
                                                             pbc2[:, :]))
                        self.wait(self.S, vt)
                        self.inc("s", self.S.activation(
                            self.xffn[:, m * T + t0:m * T + t0 + 512],
                            xnf[:, :], AF.Identity))
                        self.wait(self.Tn, vt)
                        glast = self.Tn.matmul(
                            pgate[0:14, 0:512], gateWT[:, m * 14:(m + 1) * 14],
                            xnf[:, :], start=(m == 0), stop=(m == 7))
                        if m == 7:
                            self.inc("t", glast)
                        for b in range(4):
                            tt_ = self.inc("t", self.Tn.transpose(
                                ptr[:, b * 128:(b + 1) * 128],
                                xnf[:, b * 128:(b + 1) * 128], self.idt[:, :]))
                            self.wait(self.V, tt_)
                            vt2 = self.inc("v", self.V.tensor_copy(
                                self.acc[:, (tt * 4 + b) * 1024 + m * 128:
                                         (tt * 4 + b) * 1024 + (m + 1) * 128],
                                ptr[:, b * 128:(b + 1) * 128]))
                            self.wait(self.Tn, vt2)
                    vt = self.tok("v")
                    self.wait(self.S, vt)
                    for b in range(4):
                        self.stdma(self.d_xn[t0 + b * 128:t0 + (b + 1) * 128, :],
                                   self.acc[:, (tt * 4 + b) * 1024:
                                            (tt * 4 + b) * 1024 + 1024])
                    self.gating_tile(tt, stage, pgate, pbc2)
            self.bar()

    def gating_tile(self, tt, lgsb, pgf, ptf):
        t0 = tt * 512
        pg = pgf[0:14, 0:512]
        pt2 = ptf[0:128, 0:14]
        with ExitStack() as es:
            ltm = self.sb("gt_ltm", [128, 14], stack=es)
            prb = self.sb("gt_prb", [128, 14], stack=es)
            m8 = self.sb("gt_m8", [128, 8], stack=es)
            nmx = self.sb("gt_nmx", [128, 1], stack=es)
            den = self.sb("gt_den", [128, 1], stack=es)
            msk = self.sb("gt_msk", [128, 14], stack=es)
            self.wait(self.S, self.tok("t"))
            st_ = self.inc("s", self.S.activation(lgsb[0:14, 0:512], pg,
                                                  AF.Identity))
            self.wait(self.Tn, st_)
            for b in range(4):
                tt_ = self.inc("t", self.Tn.transpose(
                    pt2, lgsb[0:14, b * 128:(b + 1) * 128], self.idt[0:14, 0:14]))
                self.wait(self.V, tt_)
                self.vrt(self.V.tensor_copy(ltm[:, :], pt2))
                self.wait(self.Tn, self.tok("v"))
                self.vrt(self.V.reduce_max(nmx[:, :1], ltm[:, :], axis=AX,
                                           negate=True))
                vt = self.tok("v")
                self.wait(self.S, vt)
                st_ = self.inc("s", self.S.activation(prb[:, :], ltm[:, :], AF.Exp,
                                                      bias=nmx[:, :1]))
                self.wait(self.V, st_)
                self.vrt(self.V.max(out=m8[:, :], in_=prb[:, :]))
                self.vrt(self.V.tensor_add(den[:, :1], m8[:, 0:1], m8[:, 1:2]))
                self.vrt(self.V.reciprocal(den[:, :1], den[:, :1]))
                self.vrt(self.V.tensor_scalar(msk[:, :], prb[:, :], m8[:, 1:2],
                                              None, ALU.is_ge))
                self.V.tensor_mul(msk[:, :], msk[:, :], prb[:, :])
                vt = self.vrt(self.V.tensor_scalar(
                    self.coeff[:, (tt * 4 + b) * NE:(tt * 4 + b + 1) * NE],
                    msk[:, :], den[:, :1], None, ALU.mult))
            self.wait(self.SY, self.tok("t"))

    def phase_ffn(self):
        self.dma_quiesce(self.V, ("ds",))
        with ExitStack() as es:
            wsl = [self.sb(f"wsl{j}", [128, 16384], BF16, es) for j in range(2)]
            h1 = self.sb("ffn_h1", [128, 16 * 512], BF16, es)
            cfm = self.sb("coeff_fm", [14, 16 * 128], stack=es)
            with ExitStack() as es2:
                pt2 = self.ps("ffn_pt", [14, 128], es2)
                vt = self.tok("v")
                self.wait(self.Tn, vt)
                for tm in range(16):
                    tt_ = self.inc("t", self.Tn.transpose(
                        pt2[:, :], self.coeff[:, tm * NE:(tm + 1) * NE],
                        self.idt[:, :]))
                    self.wait(self.S, tt_)
                    st_ = self.inc("s", self.S.activation(
                        cfm[0:14, tm * 128:(tm + 1) * 128], pt2[0:14, :],
                        AF.Identity))
                    self.wait(self.Tn, st_)
                pb2 = self.ps("ffn_pb2", [128, 512], es2)
                for tm in range(16):
                    for dn in range(2):
                        t_ = self.inc("t", self.Tn.matmul(
                            pb2[:, :], cfm[0:14, tm * 128:(tm + 1) * 128],
                            self.b2sb[0:14, dn * 512:(dn + 1) * 512],
                            start=True, stop=True))
                        self.wait(self.V, t_)
                        vt = self.inc("v", self.V.tensor_add(
                            self.acc[:, tm * 1024 + dn * 512:
                                     tm * 1024 + (dn + 1) * 512],
                            self.acc[:, tm * 1024 + dn * 512:
                                     tm * 1024 + (dn + 1) * 512],
                            pb2[:, :]))
                        self.wait(self.Tn, vt)
            wtoks = [None, None]
            wuse = [None, None]
            for e in range(NE):
                for which in range(2):
                    if wuse[which] is not None:
                        self.wait(self.G, wuse[which])
                    src = self.d_w1[e] if which == 0 else self.d_w2[e]
                    wtoks[which] = self.gld(wsl[which][:, :], src[:, :])
                for sl in range(4):
                    t0 = sl * 512
                    self.wait(self.Tn, wtoks[0])
                    for fm in range(16):
                        i = self.pmm_next()
                        last = None
                        for kc in range(8):
                            last = self.Tn.matmul(
                                self.pmm[i][:, :],
                                wsl[0][:, kc * FF + fm * 128:kc * FF + (fm + 1) * 128],
                                self.xffn[:, kc * T + t0:kc * T + t0 + 512],
                                start=(kc == 0), stop=(kc == 7))
                        t_ = self.inc("t", last)
                        self.wait(self.S, t_)
                        st_ = self.inc("s", self.S.activation(
                            h1[:, fm * 512:(fm + 1) * 512], self.pmm[i][:, :],
                            AF.Gelu, bias=self.col("b1", e * 16 + fm)))
                        self.pmm_free[i] = st_
                    st_ = self.tok("s")
                    self.wait(self.Tn, st_)
                    self.wait(self.Tn, wtoks[1])
                    for tm4 in range(4):
                        tm = sl * 4 + tm4
                        for dn in range(2):
                            i = self.pmm_next()
                            last = None
                            for kf in range(16):
                                last = self.Tn.matmul(
                                    self.pmm[i][:, :],
                                    h1[:, kf * 512 + tm4 * 128:kf * 512 + (tm4 + 1) * 128],
                                    wsl[1][:, kf * 1024 + dn * 512:kf * 1024 + (dn + 1) * 512],
                                    start=(kf == 0), stop=(kf == 15))
                            t_ = self.inc("t", last)
                            self.wait(self.V, t_)
                            vt = self.inc("v", self.V.scalar_tensor_tensor(
                                self.acc[:, tm * 1024 + dn * 512:
                                         tm * 1024 + (dn + 1) * 512],
                                self.pmm[i][:, :],
                                self.coeff[:, tm * NE + e:tm * NE + e + 1],
                                self.acc[:, tm * 1024 + dn * 512:
                                         tm * 1024 + (dn + 1) * 512],
                                ALU.mult, ALU.add))
                            self.pmm_free[i] = vt
                wuse[0] = self.tok("t")
                wuse[1] = self.tok("t")
            self.bar()

    def phase_lnf(self):
        with ExitStack() as es:
            xr = [self.sb(f"lnf_xr{j}", [128, 1024], stack=es) for j in range(2)]
            sq = self.sb("lnf_sq", [128, 1024], stack=es)
            mu = self.sb("lnf_mu", [128, 1], stack=es)
            var = self.sb("lnf_var", [128, 1], stack=es)
            mu2 = self.sb("lnf_mu2", [128, 1], stack=es)
            rstd = self.sb("lnf_rstd", [128, 1], stack=es)
            ot = self.sb("lnf_out", [128, 1024], stack=es)
            xr_free = [None, None]
            ot_free = None
            for tm in range(16):
                par = tm % 2
                if xr_free[par] is not None:
                    self.wait(self.SY, xr_free[par])
                dtok = self.ld(xr[par][:, :], self.d_xn[tm * 128:(tm + 1) * 128, :])
                a = self.acc[:, tm * 1024:(tm + 1) * 1024]
                self.V.tensor_scalar(a, a, -100.0, 100.0, ALU.max, ALU.min)
                self.wait(self.V, dtok)
                vt = self.inc("v", self.V.tensor_add(a, a, xr[par][:, :]))
                xr_free[par] = vt
                self.vrt(self.V.reduce_sum(mu[:, :1], a, axis=AX))
                self.vrt(self.V.tensor_single_scalar(mu[:, :1], mu[:, :1],
                                                     1.0 / D, ALU.mult))
                self.wait(self.S, self.tok("v"))
                st_ = self.inc("s", self.S.activation(sq[:, :], a, AF.Square))
                self.wait(self.V, st_)
                self.vrt(self.V.reduce_sum(var[:, :1], sq[:, :], axis=AX))
                self.vrt(self.V.tensor_single_scalar(var[:, :1], var[:, :1],
                                                     1.0 / D, ALU.mult))
                self.vrt(self.V.tensor_mul(mu2[:, :1], mu[:, :1], mu[:, :1]))
                vt = self.vrt(self.V.tensor_sub(var[:, :1], var[:, :1], mu2[:, :1]))
                self.wait(self.S, vt)
                st_ = self.inc("s", self.S.activation(rstd[:, :1], var[:, :1],
                                                      AF.Sqrt, bias=self.epsc[:, :1]))
                self.wait(self.V, st_)
                self.vrt(self.V.reciprocal(rstd[:, :1], rstd[:, :1]))
                if ot_free is not None:
                    self.wait(self.V, ot_free)
                vt = self.inc("v", self.V.tensor_scalar(
                    ot[:, :], a, mu[:, :1], rstd[:, :1], ALU.subtract, ALU.mult))
                self.wait(self.S, vt)
                ot_free = self.stdma(self.d_out[tm * 128:(tm + 1) * 128, :],
                                     ot[:, :])


_BUILT = None


def _get_built():
    global _BUILT
    if _BUILT is None:
        kb = KB()
        kb.build()
        _BUILT = kb
    return _BUILT


def _pack_shared(I):
    f32 = lambda a: np.ascontiguousarray(np.asarray(a, dtype=np.float32))
    out = {}
    out["identity"] = np.eye(128, dtype=np.float32)
    bsel = np.zeros((48, NS * 128), np.float32)
    for n in range(NS):
        bsel[n, n * 128:(n + 1) * 128] = 1.0
        bsel[32 + n, n * 128:(n + 1) * 128] = 1.0
    out["bsel"] = bsel
    for l in range(NL):
        out[f"inWT_p{l}"] = f32(np.asarray(I["L0_attn_in_W"])[l].T
                                .reshape(4, 128, FF).transpose(1, 0, 2)
                                .reshape(128, 4 * FF))
        out[f"moWT_p{l}"] = f32(np.asarray(I["L0_attn_mo_W"])[l].T
                                .reshape(8, 128, 512).transpose(1, 0, 2)
                                .reshape(128, 8 * 512))
        out[f"xpWT_p{l}"] = f32(np.asarray(I["L0_attn_xp_W"])[l].T
                                .reshape(8, 128, 64).transpose(1, 0, 2)
                                .reshape(128, 8 * 64))
        out[f"dtWT_p{l}"] = f32(np.asarray(I["L0_attn_dt_W"])[l].T)
    out["poWT_p"] = f32(np.asarray(I["L0_attn_po_W"]).T
                        .reshape(8, 128, 1024).transpose(1, 0, 2)
                        .reshape(128, 8 * 1024))
    out["gateWT_p"] = f32(np.asarray(I["L0_ffn_gate_w"]).T
                          .reshape(8, 128, NE).transpose(1, 0, 2)
                          .reshape(128, 8 * NE))
    w1 = np.asarray(I["L0_ffn_w1"])
    w2 = np.asarray(I["L0_ffn_w2"])
    for e in range(NE):
        out[f"w1_p{e}"] = f32(w1[e].reshape(8, 128, FF).transpose(1, 0, 2)
                              .reshape(128, 8 * FF))
        out[f"w2_p{e}"] = f32(w2[e].reshape(16, 128, 1024).transpose(1, 0, 2)
                              .reshape(128, 16 * 1024))
    out["b2_rows"] = f32(I["L0_ffn_b2"])
    return out


def _pack_pcols_base(I):
    pc = np.zeros((128, PCN), dtype=np.float32)
    def put(name, arr):
        a = np.asarray(arr, dtype=np.float32)
        pc[:, _PC[name]:_PC[name] + a.shape[1]] = a
    put("ln_g", np.asarray(I["L0_attn_ln_g"]).reshape(8, 128).T)
    put("ln_b", np.asarray(I["L0_attn_ln_b"]).reshape(8, 128).T)
    alog = np.asarray(I["L0_attn_A_log"])
    put("alog", alog.reshape(NL, 8, 128, NS).transpose(2, 0, 1, 3)
        .reshape(128, NL * 128))
    put("Dp", np.asarray(I["L0_attn_Dp"]).reshape(NL, 8, 128)
        .transpose(2, 0, 1).reshape(128, NL * 8))
    put("cb", np.asarray(I["L0_attn_conv_b"]).reshape(NL, 8, 128)
        .transpose(2, 0, 1).reshape(128, NL * 8))
    cw = np.asarray(I["L0_attn_conv_w"]).reshape(NL, 8, 128, 4)
    put("cw", cw.transpose(2, 0, 3, 1).reshape(128, NL * 32))
    put("dtb", np.asarray(I["L0_attn_dt_b"]).reshape(NL, 8, 128)
        .transpose(2, 0, 1).reshape(128, NL * 8))
    put("rmsw", np.asarray(I["L0_attn_rms_w"]).reshape(NL, 4, 128)
        .transpose(2, 0, 1).reshape(128, NL * 4))
    put("po_b", np.asarray(I["L0_attn_po_b"]).reshape(8, 128).T)
    put("b1", np.asarray(I["L0_ffn_b1"]).reshape(NE, 16, 128)
        .transpose(2, 0, 1).reshape(128, NE * 16))
    return pc


def kernel(**inputs):
    kb = _get_built()
    shared = _pack_shared(inputs)
    pc_base = _pack_pcols_base(inputs)
    src = np.ascontiguousarray(np.asarray(inputs["L0_attn_src"], np.float32))
    in_maps = []
    for k in range(NCORE):
        seq, t0 = divmod(k * T, L)
        pc = pc_base.copy()
        if k % 4 != 0:
            pc[:, _PC["msel"] + (k - 1)] = 1.0
            pc[:, _PC["hmask"]] = 1.0
        for j in range(NCORE):
            if j // 4 == k // 4 and j < k:
                pc[:, _PC["mselp"] + j] = 1.0
        halo = np.zeros((4, D), np.float32)
        if k % 4 != 0:
            halo[0:3] = src[seq, t0 - 3:t0]
        m = {"src_tm": np.ascontiguousarray(src[seq, t0:t0 + T]),
             "halo_src": halo,
             "pcols": np.ascontiguousarray(pc)}
        m.update(shared)
        in_maps.append(m)
    res = run_bass_kernel_spmd(kb.nc, in_maps, core_ids=list(range(NCORE)))
    out = np.empty((BATCH, L, D), np.float32)
    for k in range(NCORE):
        seq, t0 = divmod(k * T, L)
        out[seq, t0:t0 + T] = res.results[k]["out_tm"]
    return out


# version 05
# speedup vs baseline: 1.0061x; 1.0061x over previous
import numpy as np
from contextlib import ExitStack

import concourse.bass as bass
import concourse.mybir as mybir
from concourse.bass_utils import run_bass_kernel_spmd

F32 = mybir.dt.float32
BF16 = mybir.dt.bfloat16
AF = mybir.ActivationFunctionType
ALU = mybir.AluOpType
AX = mybir.AxisListType.X

NCORE = 8
BATCH, L, D = 2, 8192, 1024
DC, DI, NS, DTR, NL = 512, 1024, 16, 32, 2
NE, FF = 14, 2048
T = (BATCH * L) // NCORE
HALF = 1024
NTOK = T // 128
EPS = 1e-5

_PC = {}
_pc_off = 0
def _pc(name, n):
    global _pc_off
    _PC[name] = _pc_off
    _pc_off += n
_pc("ln_g", 8); _pc("ln_b", 8)
_pc("alog", NL * 128)
_pc("Dp", NL * 8)
_pc("cb", NL * 8)
_pc("cw", NL * 32)
_pc("dtb", NL * 8)
_pc("rmsw", NL * 4)
_pc("po_b", 8)
_pc("msel", 8)
_pc("mselp", 8)
_pc("hmask", 1)
_pc("b1", NE * 16)
PCN = _pc_off


class KB:
    def __init__(self):
        self.nc = bass.Bass("TRN2", target_bir_lowering=False,
                            detect_race_conditions=False)
        self.stk = ExitStack()
        self.cnt = {}
        self.sems = {}
        self.V = self.nc.vector
        self.S = self.nc.scalar
        self.Tn = self.nc.tensor
        self.G = self.nc.gpsimd
        self.SY = self.nc.sync
        self.E = {"v": self.V, "s": self.S, "t": self.Tn, "g": self.G,
                  "y": self.SY}
        for name in ["v", "s", "t", "g", "cc"]:
            self.sems[name] = self.stk.enter_context(self.nc.semaphore(name))
            self.cnt[name] = 0
        self.pools = {}
        for pname, eng, nslot in [("dl", self.SY, 6), ("ds", self.S, 4),
                                  ("dg", self.G, 4)]:
            slots = []
            for j in range(nslot):
                sn = f"{pname}{j}"
                self.sems[sn] = self.stk.enter_context(self.nc.semaphore(sn))
                self.cnt[sn] = 0
                slots.append(sn)
            self.pools[pname] = {"eng": eng, "slots": slots, "i": 0}

    def _dma(self, pname, out_ap, in_ap):
        p = self.pools[pname]
        sn = p["slots"][p["i"] % len(p["slots"])]
        p["i"] += 1
        if self.cnt[sn] > 0:
            p["eng"].wait_ge(self.sems[sn], self.cnt[sn])
        instr = p["eng"].dma_start(out=out_ap, in_=in_ap)
        instr.then_inc(self.sems[sn], 16)
        self.cnt[sn] += 16
        return (sn, self.cnt[sn])

    def dma_quiesce(self, eng, pnames=("dl", "ds", "dg")):
        for pname in pnames:
            for sn in self.pools[pname]["slots"]:
                if self.cnt[sn] > 0:
                    eng.wait_ge(self.sems[sn], self.cnt[sn])

    _ENG_OF_SEM = {"v": "v", "s": "s", "t": "t", "g": "g"}

    def inc(self, name, instr, amt=1, commit=True):
        instr.then_inc(self.sems[name], amt)
        self.cnt[name] += amt
        if commit and name in self._ENG_OF_SEM:
            self.E[name].wait_ge(self.sems[name], self.cnt[name])
        return (name, self.cnt[name])

    def tok(self, name):
        return (name, self.cnt[name])

    def wait(self, eng, token):
        name, v = token
        if v > 0:
            eng.wait_ge(self.sems[name], v)

    def vrt(self, instr):
        t = self.inc("v", instr)
        self.wait(self.V, t)
        return t

    def bar(self, engines="vstg"):
        toks = []
        for ch in engines:
            toks.append(self.inc(ch, self.E[ch].nop()))
        for ch in engines:
            for t in toks:
                if t[0] != ch:
                    self.wait(self.E[ch], t)
        for t in toks:
            self.wait(self.SY, t)
            if "g" not in engines:
                self.wait(self.G, t)
        self.dma_quiesce(self.SY, ("ds", "dg"))
        self.dma_quiesce(self.G, ("ds", "dl"))

    def ld(self, out_ap, in_ap):
        return self._dma("dl", out_ap, in_ap)

    def stdma(self, out_ap, in_ap):
        return self._dma("ds", out_ap, in_ap)

    def gld(self, out_ap, in_ap):
        return self._dma("dg", out_ap, in_ap)

    _uid = 0

    def _u(self, name):
        KB._uid += 1
        return f"{name}_{KB._uid}"

    def sb(self, name, shape, dtype=F32, stack=None):
        return (stack or self.stk).enter_context(
            self.nc.sbuf_tensor(self._u(name), shape, dtype))

    def ps(self, name, shape, stack=None):
        return (stack or self.stk).enter_context(
            self.nc.psum_tensor(self._u(name), shape, F32))

    def col(self, name, idx=0):
        c = _PC[name] + idx
        return self.pcols[:, c:c + 1]

    def pmm_init(self):
        self.pmm = [self.ps("pmm0", [128, 512]), self.ps("pmm1", [128, 512])]
        self.pmm_free = [None, None]
        self.pmm_i = 0

    def pmm_next(self):
        i = self.pmm_i
        self.pmm_i ^= 1
        if self.pmm_free[i] is not None:
            self.wait(self.Tn, self.pmm_free[i])
        return i

    def build(self):
        nc = self.nc
        self.d_src = nc.dram_tensor("src_tm", [T, D], F32, kind="ExternalInput")
        self.d_halo = nc.dram_tensor("halo_src", [4, D], F32, kind="ExternalInput")
        self.d_pcols = nc.dram_tensor("pcols", [128, PCN], F32, kind="ExternalInput")
        self.d_idt = nc.dram_tensor("identity", [128, 128], F32, kind="ExternalInput")
        self.d_bsel = nc.dram_tensor("bsel", [48, NS * 128], F32, kind="ExternalInput")
        self.d_inWT = [nc.dram_tensor(f"inWT_p{l}", [128, 4 * FF], F32, kind="ExternalInput")
                       for l in range(NL)]
        self.d_moWT = [nc.dram_tensor(f"moWT_p{l}", [128, 8 * 512], F32, kind="ExternalInput")
                       for l in range(NL)]
        self.d_xpWT = [nc.dram_tensor(f"xpWT_p{l}", [128, 8 * 64], F32, kind="ExternalInput")
                       for l in range(NL)]
        self.d_dtWT = [nc.dram_tensor(f"dtWT_p{l}", [32, 1024], F32, kind="ExternalInput")
                       for l in range(NL)]
        self.d_poWT = nc.dram_tensor("poWT_p", [128, 8 * 1024], F32, kind="ExternalInput")
        self.d_gateWT = nc.dram_tensor("gateWT_p", [128, 8 * 14], F32, kind="ExternalInput")
        self.d_w1 = [nc.dram_tensor(f"w1_p{e}", [128, 8 * FF], F32, kind="ExternalInput")
                     for e in range(NE)]
        self.d_w2 = [nc.dram_tensor(f"w2_p{e}", [128, 16 * 1024], F32, kind="ExternalInput")
                     for e in range(NE)]
        self.d_b2r = nc.dram_tensor("b2_rows", [NE, 1024], F32, kind="ExternalInput")
        self.d_out = nc.dram_tensor("out_tm", [T, D], F32, kind="ExternalOutput")
        self.d_x1 = nc.dram_tensor("x1_d", [512, T], F32, kind="Internal")
        self.d_x2 = nc.dram_tensor("x2_d", [512, T], F32, kind="Internal")
        self.d_h = nc.dram_tensor("h_d", [512, T], F32, kind="Internal")
        self.d_xn = nc.dram_tensor("xn_d", [T, D], F32, kind="Internal")
        self.d_ag1i = nc.dram_tensor("ag1_in", [128, 256], F32, kind="Internal")
        self.d_ag1o = nc.dram_tensor("ag1_out", [NCORE * 128, 256], F32, kind="Internal")
        self.d_ag2i = nc.dram_tensor("ag2_in", [128, 12], F32, kind="Internal")
        self.d_ag2o = nc.dram_tensor("ag2_out", [NCORE * 128, 12], F32, kind="Internal")

        self.v_x1 = self.d_x1.ap().rearrange("(c p) t -> p c t", p=128)
        self.v_x2 = self.d_x2.ap().rearrange("(c p) t -> p c t", p=128)
        self.v_h = self.d_h.ap().rearrange("(c p) t -> p c t", p=128)

        self.pcols = self.sb("pcols_sb", [128, PCN])
        self.idt = self.sb("idt_sb", [128, 128])
        self.ones_c = self.sb("ones_c", [128, 1])
        self.ones_r = self.sb("ones_r", [1, 128])
        self.epsc = self.sb("epsc", [128, 1])
        self.A_sb = self.sb("A_sb", [128, NL * 128])
        self.halo_x1 = self.sb("halo_x1", [128, 12])
        self.halo_h = self.sb("halo_h", [128, 12])
        self.hstate = self.sb("hstate", [128, 128])
        self.pfull = self.sb("pfull", [128, 128])
        self.hin = self.sb("hin", [128, 128])
        self.cs = self.sb("cs_tot", [128, 8])

        self.pmm_init()

        t1 = self.ld(self.pcols[:, :], self.d_pcols[:, :])
        t2 = self.ld(self.idt[:, :], self.d_idt[:, :])
        for e in "vstg":
            self.wait(self.E[e], t1)
            self.wait(self.E[e], t2)
        self.V.memset(self.ones_c[:, :], 1.0)
        self.V.memset(self.ones_r[:, :], 1.0)
        self.V.memset(self.epsc[:, :], EPS)
        stok = self.inc("s", self.S.activation(
            self.A_sb[:, :],
            self.pcols[:, _PC["alog"]:_PC["alog"] + NL * 128], AF.Exp))
        self.wait(self.V, stok)
        vt = self.inc("v", self.V.tensor_single_scalar(
            self.A_sb[:, :], self.A_sb[:, :], -1.0, ALU.mult))
        self.wait(self.S, vt)
        self.bar()

        self.phase_ln1()
        for l in range(NL):
            self.mamba_layer(l)
        self.phase_po_ln0()
        self.phase_ffn()
        self.phase_lnf()
        self.dma_quiesce(self.SY)
        return nc

    def phase_ln1(self):
        with ExitStack() as es:
            sxa = self.sb("ln_sxa", [128, NTOK * D], stack=es)
            sqa = self.sb("ln_sqa", [128, NTOK * D], stack=es)
            mu = self.sb("ln_mu", [128, NTOK], stack=es)
            var = self.sb("ln_var", [128, NTOK], stack=es)
            mu2 = self.sb("ln_mu2", [128, NTOK], stack=es)
            rstd = self.sb("ln_rstd", [128, NTOK], stack=es)
            stg = [self.sb(f"ln_st{j}", [128, 8 * 128], stack=es) for j in range(2)]
            ptr = [self.ps(f"ln_ptr{j}", [128, 128], stack=es) for j in range(4)]
            dts = []
            for i in range(NTOK):
                dts.append(self.ld(sxa[:, i * D:(i + 1) * D],
                                   self.d_src[i * 128:(i + 1) * 128, :]))
            for t_ in dts:
                self.wait(self.V, t_)
                self.wait(self.S, t_)
            self.vrt(self.V.tensor_reduce(
                mu[:, :], sxa[:, :].rearrange("p (i d) -> p i d", d=D), AX, ALU.add))
            self.vrt(self.V.tensor_single_scalar(mu[:, :], mu[:, :], 1.0 / D, ALU.mult))
            st_ = self.inc("s", self.S.activation(sqa[:, :], sxa[:, :], AF.Square))
            self.wait(self.V, st_)
            self.vrt(self.V.tensor_reduce(
                var[:, :], sqa[:, :].rearrange("p (i d) -> p i d", d=D), AX, ALU.add))
            self.vrt(self.V.tensor_single_scalar(var[:, :], var[:, :], 1.0 / D, ALU.mult))
            self.vrt(self.V.tensor_mul(mu2[:, :], mu[:, :], mu[:, :]))
            vt = self.vrt(self.V.tensor_sub(var[:, :], var[:, :], mu2[:, :]))
            self.wait(self.S, vt)
            st_ = self.inc("s", self.S.activation(rstd[:, :], var[:, :], AF.Sqrt,
                                                  bias=self.epsc[:, :1]))
            self.wait(self.V, st_)
            self.vrt(self.V.reciprocal(rstd[:, :], rstd[:, :]))
            for i in range(NTOK):
                self.V.tensor_scalar(sqa[:, i * D:(i + 1) * D],
                                     sxa[:, i * D:(i + 1) * D],
                                     mu[:, i:i + 1], rstd[:, i:i + 1],
                                     ALU.subtract, ALU.mult)
            vt = self.vrt(self.V.engine_nop())
            self.wait(self.Tn, vt)
            ptr_free = [None, None, None, None]
            stg_free = [None, None]
            for i in range(NTOK):
                par = i % 2
                if stg_free[par] is not None:
                    self.wait(self.S, stg_free[par])
                ev_last = None
                for c in range(8):
                    p2 = c % 4
                    if ptr_free[p2] is not None:
                        self.wait(self.Tn, ptr_free[p2])
                    tt_ = self.inc("t", self.Tn.transpose(
                        ptr[p2][:, :], sqa[:, i * D + c * 128:i * D + (c + 1) * 128],
                        self.idt[:, :]), commit=False)
                    self.wait(self.S, tt_)
                    ev_last = self.inc("s", self.S.activation(
                        stg[par][:, c * 128:(c + 1) * 128], ptr[p2][:, :],
                        AF.Identity, bias=self.col("ln_b", c),
                        scale=self.col("ln_g", c)), commit=False)
                    ptr_free[p2] = ev_last
                ts0 = i * 128
                self.inc("s", self.S.nop())
                self.wait(self.S, self.tok("s"))
                self.stdma(self.v_x1[:, :, ts0:ts0 + 128], stg[par][:, 0:512])
                self.stdma(self.v_x2[:, :, ts0:ts0 + 128], stg[par][:, 512:1024])
                stg_free[par] = self.stdma(self.v_h[:, :, ts0:ts0 + 128],
                                           stg[par][:, 0:512])
            vt = self.inc("v", self.V.memset(sxa[:, 0:D], 0.0))
            self.wait(self.SY, vt)
            self.wait(self.SY, self.tok("s"))
            dtok = self.ld(sxa[0:4, 0:D], self.d_halo[:, :])
            self.wait(self.V, dtok)
            self.wait(self.S, dtok)
            self.vrt(self.V.reduce_sum(mu[:, 0:1], sxa[:, 0:D], axis=AX))
            self.vrt(self.V.tensor_single_scalar(mu[:, 0:1], mu[:, 0:1], 1.0 / D, ALU.mult))
            st_ = self.inc("s", self.S.activation(sqa[:, 0:D], sxa[:, 0:D], AF.Square))
            self.wait(self.V, st_)
            self.vrt(self.V.reduce_sum(var[:, 0:1], sqa[:, 0:D], axis=AX))
            self.vrt(self.V.tensor_single_scalar(var[:, 0:1], var[:, 0:1], 1.0 / D, ALU.mult))
            self.vrt(self.V.tensor_mul(mu2[:, 0:1], mu[:, 0:1], mu[:, 0:1]))
            vt = self.vrt(self.V.tensor_sub(var[:, 0:1], var[:, 0:1], mu2[:, 0:1]))
            self.wait(self.S, vt)
            st_ = self.inc("s", self.S.activation(rstd[:, 0:1], var[:, 0:1], AF.Sqrt,
                                                  bias=self.epsc[:, :1]))
            self.wait(self.V, st_)
            self.vrt(self.V.reciprocal(rstd[:, 0:1], rstd[:, 0:1]))
            vt = self.inc("v", self.V.tensor_scalar(
                sqa[:, 0:D], sxa[:, 0:D], mu[:, 0:1], rstd[:, 0:1],
                ALU.subtract, ALU.mult))
            self.wait(self.Tn, vt)
            ev_last = None
            for c in range(4):
                p2 = c % 4
                if ptr_free[p2] is not None:
                    self.wait(self.Tn, ptr_free[p2])
                tt_ = self.inc("t", self.Tn.transpose(
                    ptr[p2][:, :], sqa[:, c * 128:(c + 1) * 128], self.idt[:, :]))
                self.wait(self.S, tt_)
                ev_last = self.inc("s", self.S.activation(
                    self.halo_x1[:, c * 3:(c + 1) * 3], ptr[p2][:, 0:3],
                    AF.Identity, bias=self.col("ln_b", c),
                    scale=self.col("ln_g", c)))
                ptr_free[p2] = ev_last
            self.wait(self.V, ev_last)
            self.inc("v", self.V.tensor_copy(self.halo_h[:, :], self.halo_x1[:, :]))
            self.bar()

    def compute_rmsh(self, l, rmsh):
        with ExitStack() as es:
            pst = self.ps("rms_pst", [1, T], es)
            hst = [self.sb(f"rms_h{j}", [128, T], stack=es) for j in range(2)]
            sqb = [self.sb(f"rms_sq{j}", [128, T], stack=es) for j in range(2)]
            rr = self.sb("rms_rr", [1, T], stack=es)
            pb = self.ps("rms_pb", [128, 512], es)
            mm_toks = [None, None]
            sq_done = [None, None]
            for cc in range(4):
                par = cc % 2
                if sq_done[par] is not None:
                    self.wait(self.SY, sq_done[par])
                dtok = self.ld(hst[par][:, :], self.v_h[:, cc, :])
                self.wait(self.S, dtok)
                if mm_toks[par] is not None:
                    self.wait(self.S, mm_toks[par])
                st_ = self.inc("s", self.S.activation(sqb[par][:, :], hst[par][:, :], AF.Square))
                sq_done[par] = st_
                self.wait(self.Tn, st_)
                last = None
                for tt in range(4):
                    last = self.Tn.matmul(pst[:, tt * 512:(tt + 1) * 512],
                                          self.ones_c[:, :1],
                                          sqb[par][:, tt * 512:(tt + 1) * 512],
                                          start=(cc == 0), stop=(cc == 3))
                mm_toks[par] = self.inc("t", last)
            self.wait(self.V, mm_toks[(4 - 1) % 2])
            vt = self.vrt(self.V.tensor_single_scalar(rr[:1, :], pst[:1, :], 1.0 / DC, ALU.mult))
            self.wait(self.S, vt)
            st_ = self.inc("s", self.S.activation(rr[:1, :], rr[:1, :], AF.Sqrt,
                                                  bias=self.epsc[0:1, :1]))
            self.wait(self.V, st_)
            vt = self.vrt(self.V.reciprocal(rr[:1, :], rr[:1, :]))
            self.wait(self.Tn, vt)
            self.wait(self.SY, self.tok("s"))
            hs_used = [None, None]
            for tt in range(4):
                bt = self.inc("t", self.Tn.matmul(pb[:, :], self.ones_r[:1, :],
                                                  rr[:1, tt * 512:(tt + 1) * 512],
                                                  start=True, stop=True))
                vlast = None
                for cc in range(4):
                    par = cc % 2
                    if hs_used[par] is not None:
                        self.wait(self.SY, hs_used[par])
                    dtok = self.ld(hst[par][:, 0:512], self.v_h[:, cc, tt * 512:(tt + 1) * 512])
                    self.wait(self.V, dtok)
                    self.wait(self.V, bt)
                    vlast = self.inc("v", self.V.scalar_tensor_tensor(
                        rmsh[:, cc * T + tt * 512: cc * T + (tt + 1) * 512],
                        hst[par][:, 0:512], self.col("rmsw", l * 4 + cc),
                        pb[:, :], ALU.mult, ALU.mult))
                    hs_used[par] = vlast
                self.wait(self.Tn, vlast)
            self.bar()

    def halo_xcpre(self, l, inWT, hxc):
        with ExitStack() as es:
            psq = self.ps("hx_psq", [1, 4], es)
            pb = self.ps("hx_pb", [128, 4], es)
            pm = self.ps("hx_pm", [128, 4], es)
            sqh = self.sb("hx_sq", [128, 12], stack=es)
            rr = self.sb("hx_rr", [1, 4], stack=es)
            rmf = self.sb("hx_rmf", [128, 12], stack=es)
            rmh = self.sb("hx_rmh", [128, 12], BF16, stack=es)
            st_ = self.inc("s", self.S.activation(sqh[:, :], self.halo_h[:, :], AF.Square))
            self.wait(self.Tn, st_)
            last = None
            for cc in range(4):
                last = self.Tn.matmul(psq[:1, 0:3], self.ones_c[:, :1],
                                      sqh[:, cc * 3:(cc + 1) * 3],
                                      start=(cc == 0), stop=(cc == 3))
            t_ = self.inc("t", last)
            self.wait(self.V, t_)
            vt = self.vrt(self.V.tensor_single_scalar(rr[:1, 0:3], psq[:1, 0:3], 1.0 / DC, ALU.mult))
            self.wait(self.S, vt)
            st_ = self.inc("s", self.S.activation(rr[:1, 0:3], rr[:1, 0:3], AF.Sqrt,
                                                  bias=self.epsc[0:1, :1]))
            self.wait(self.V, st_)
            vt = self.vrt(self.V.reciprocal(rr[:1, 0:3], rr[:1, 0:3]))
            self.wait(self.Tn, vt)
            bt = self.inc("t", self.Tn.matmul(pb[:, 0:3], self.ones_r[:1, :],
                                              rr[:1, 0:3], start=True, stop=True))
            self.wait(self.V, bt)
            vlast = None
            for cc in range(4):
                vlast = self.inc("v", self.V.scalar_tensor_tensor(
                    rmf[:, cc * 3:(cc + 1) * 3], self.halo_h[:, cc * 3:(cc + 1) * 3],
                    self.col("rmsw", l * 4 + cc), pb[:, 0:3], ALU.mult, ALU.mult))
            self.wait(self.S, vlast)
            st_ = self.inc("s", self.S.activation(rmh[:, :], rmf[:, :], AF.Identity))
            self.wait(self.Tn, st_)
            for j in range(8):
                last = None
                for cc in range(4):
                    last = self.Tn.matmul(pm[:, 0:3],
                                          inWT[:, cc * FF + j * 128:cc * FF + (j + 1) * 128],
                                          rmh[:, cc * 3:(cc + 1) * 3],
                                          start=(cc == 0), stop=(cc == 3))
                t_ = self.inc("t", last)
                self.wait(self.S, t_)
                st_ = self.inc("s", self.S.activation(
                    hxc[:, j * 3:(j + 1) * 3], pm[:, 0:3], AF.Identity,
                    scale=self.col("hmask")))
                self.wait(self.Tn, st_)
            self.bar()

    def mamba_layer(self, l):
        with ExitStack() as es:
            inWT = self.sb("inWT", [128, 4 * FF], BF16, es)
            moWT = self.sb("moWT", [128, 8 * 512], BF16, es)
            xpWT = self.sb("xpWT", [128, 8 * 64], BF16, es)
            dtWT = self.sb("dtWT", [32, 1024], BF16, es)
            rmsh = self.sb("rmsh", [128, 4 * T], BF16, es)
            dbc = self.sb("dbc", [64, T], F32, es)
            dtbf = self.sb("dtbf", [32, T], BF16, es)
            hxc = self.sb("hxc", [128, 24], F32, es)
            self.bsel = self.sb("bsel_sb", [48, NS * 128], stack=es)
            self.ld(self.bsel[:, :], self.d_bsel[:, :])
            gtoks = [self.gld(inWT[:, :], self.d_inWT[l][:, :]),
                     self.gld(moWT[:, :], self.d_moWT[l][:, :]),
                     self.gld(xpWT[:, :], self.d_xpWT[l][:, :]),
                     self.gld(dtWT[:, :], self.d_dtWT[l][:, :])]
            self.compute_rmsh(l, rmsh)
            for e in "vst":
                for gt in gtoks:
                    self.wait(self.E[e], gt)
            self.halo_xcpre(l, inWT, hxc)
            self.mamba_pass(l, True, inWT, moWT, xpWT, dtWT, rmsh, dbc, dtbf, hxc)
            self.relay(l)
            self.mamba_pass(l, False, inWT, moWT, xpWT, dtWT, rmsh, dbc, dtbf, hxc)
            if l < NL - 1:
                self.halo_relay()
            self.bar()

    def _xc_chunk(self, l, hf, c, inWT, xcpre, cvo, out_sb, out_off, halo3):
        for tt in range(2):
            i = self.pmm_next()
            last = None
            for kc in range(4):
                last = self.Tn.matmul(
                    self.pmm[i][:, :],
                    inWT[:, kc * FF + c * 128:kc * FF + (c + 1) * 128],
                    self.rmsh_cur[:, kc * T + hf * HALF + tt * 512:
                                  kc * T + hf * HALF + (tt + 1) * 512],
                    start=(kc == 0), stop=(kc == 3))
            t_ = self.inc("t", last)
            self.wait(self.S, t_)
            ev = self.inc("s", self.S.activation(
                xcpre[:, 3 + tt * 512:3 + (tt + 1) * 512], self.pmm[i][:, :],
                AF.Identity))
            self.pmm_free[i] = ev
        self.wait(self.V, self.tok("s"))
        self.inc("v", self.V.tensor_copy(xcpre[:, 0:3], halo3[:, c * 3:(c + 1) * 3]))
        self.V.tensor_scalar(cvo[:, :], xcpre[:, 0:HALF],
                             self.col("cw", l * 32 + 0 * 8 + c), None, ALU.mult)
        for k in range(1, 4):
            self.V.scalar_tensor_tensor(cvo[:, :], xcpre[:, k:k + HALF],
                                        self.col("cw", l * 32 + k * 8 + c),
                                        cvo[:, :], ALU.mult, ALU.add)
        vt = self.inc("v", self.V.tensor_copy(halo3[:, c * 3:(c + 1) * 3],
                                              xcpre[:, 1024:1027]))
        self.wait(self.S, vt)
        st_ = self.inc("s", self.S.activation(out_sb[:, out_off:out_off + HALF],
                                              cvo[:, :], AF.Silu,
                                              bias=self.col("cb", l * 8 + c)))
        return st_

    def _delta_chunk(self, l, hf, c, dtWT, dtbf, pD, delta, scratch):
        self.wait(self.Tn, self.tok("s"))
        last = None
        for tt in range(2):
            last = self.Tn.matmul(
                pD[:, tt * 512:(tt + 1) * 512], dtWT[:, c * 128:(c + 1) * 128],
                dtbf[:, hf * HALF + tt * 512:hf * HALF + (tt + 1) * 512],
                start=True, stop=True)
        t_ = self.inc("t", last)
        self.wait(self.S, t_)
        self.S.activation(scratch[:, 0:HALF], pD[:, :], AF.Exp,
                          bias=self.col("dtb", l * 8 + c))
        st_ = self.inc("s", self.S.activation(delta[:, c * HALF:(c + 1) * HALF],
                                              scratch[:, 0:HALF], AF.Ln, bias=1.0))
        return st_

    def mamba_pass(self, l, pa, inWT, moWT, xpWT, dtWT, rmsh, dbc, dtbf, hxc):
        self.rmsh_cur = rmsh
        with ExitStack() as pes:
            halo3 = self.sb("halo3", [128, 24], stack=pes)
            xcpre = self.sb("xcpre", [128, 1032], stack=pes)
            cvo = self.sb("cvo", [128, HALF], stack=pes)
            xcf = self.sb("xcf", [128, HALF], stack=pes)
            dAb = self.sb("dAb", [128, 2 * HALF], stack=pes)
            dBxb = self.sb("dBxb", [128, 2 * HALF], stack=pes)
            hscr = self.sb("hscr", [128, HALF], stack=pes)
            bmc = self.sb("bmc", [48, HALF], stack=pes)
            Bmb = self.sb("Bmb", [128, HALF], stack=pes)
            if pa:
                xcbf = self.sb("xcbf", [128, HALF], BF16, stack=pes)
                state = self.hstate
                vt = self.inc("v", self.V.memset(state[:, :], 0.0))
                self.inc("v", self.V.memset(self.cs[:, :], 0.0))
            else:
                state = self.hin
            vt = self.inc("v", self.V.tensor_copy(halo3[:, :], hxc[:, :]))
            self.wait(self.S, vt)
            self.wait(self.Tn, vt)

            for hf in range(2):
                with ExitStack() as hes:
                    yh = None
                    if not pa:
                        yh = self.sb("y_h", [128, 8 * HALF], stack=hes)
                    dues = ExitStack()
                    delta = self.sb("delta_h", [128, 8 * HALF], stack=dues)
                    u = self.sb("u_h", [128, 8 * HALF], stack=dues)
                    with ExitStack() as ces:
                        pD = self.ps("pD", [128, HALF], ces)
                        if pa:
                            pdbc = self.ps("pdbc", [64, T], ces)
                            for c in range(8):
                                st_ = self._xc_chunk(l, hf, c, inWT, xcpre, cvo,
                                                     u, c * HALF, halo3)
                                self.wait(self.S, self.tok("t"))
                                cst = self.inc("s", self.S.activation(
                                    xcbf[:, :], u[:, c * HALF:(c + 1) * HALF],
                                    AF.Identity))
                                self.wait(self.Tn, cst)
                                last = None
                                for tt in range(2):
                                    last = self.Tn.matmul(
                                        pdbc[:, hf * HALF + tt * 512:
                                             hf * HALF + (tt + 1) * 512],
                                        xpWT[:, c * 64:(c + 1) * 64],
                                        xcbf[:, tt * 512:(tt + 1) * 512],
                                        start=(c == 0), stop=(c == 7))
                                t_ = self.inc("t", last)
                                self.wait(self.S, t_)
                            t_ = self.tok("t")
                            self.wait(self.S, t_)
                            self.S.activation(dbc[0:64, hf * HALF:(hf + 1) * HALF],
                                              pdbc[:, hf * HALF:(hf + 1) * HALF],
                                              AF.Identity)
                            st_ = self.inc("s", self.S.activation(
                                dtbf[:, hf * HALF:(hf + 1) * HALF],
                                dbc[0:32, hf * HALF:(hf + 1) * HALF], AF.Identity))
                            self.wait(self.Tn, st_)
                            self.wait(self.V, st_)
                            for c in range(8):
                                st_ = self._delta_chunk(l, hf, c, dtWT, dtbf, pD,
                                                        delta, cvo)
                                self.wait(self.V, st_)
                                vt = self.inc("v", self.V.tensor_mul(
                                    u[:, c * HALF:(c + 1) * HALF],
                                    delta[:, c * HALF:(c + 1) * HALF],
                                    u[:, c * HALF:(c + 1) * HALF]))
                                self.wait(self.Tn, vt)
                                self.vrt(self.V.reduce_sum(
                                    xcf[:, 0:1], delta[:, c * HALF:(c + 1) * HALF],
                                    axis=AX))
                                if hf == 0:
                                    self.vrt(self.V.tensor_copy(
                                        self.cs[:, c:c + 1], xcf[:, 0:1]))
                                else:
                                    self.vrt(self.V.tensor_add(
                                        self.cs[:, c:c + 1], self.cs[:, c:c + 1],
                                        xcf[:, 0:1]))
                        else:
                            for c in range(8):
                                st_ = self._delta_chunk(l, hf, c, dtWT, dtbf, pD,
                                                        delta, cvo)
                                xt = self._xc_chunk(l, hf, c, inWT, xcpre, cvo,
                                                    xcf, 0, halo3)
                                self.wait(self.V, xt)
                                self.V.tensor_scalar(
                                    yh[:, c * HALF:(c + 1) * HALF], xcf[:, 0:HALF],
                                    self.col("Dp", l * 8 + c), None, ALU.mult)
                                vt = self.inc("v", self.V.tensor_mul(
                                    u[:, c * HALF:(c + 1) * HALF],
                                    delta[:, c * HALF:(c + 1) * HALF],
                                    xcf[:, 0:HALF]))
                                self.wait(self.S, vt)
                                self.wait(self.Tn, vt)

                    self.wait(self.SY, self.tok("s"))
                    self.wait(self.SY, self.tok("t"))
                    dtok = self.ld(bmc[0:16, :], dbc[32:48, hf * HALF:(hf + 1) * HALF])
                    if not pa:
                        dtok = self.ld(bmc[32:48, :],
                                       dbc[48:64, hf * HALF:(hf + 1) * HALF])
                    self.wait(self.Tn, dtok)
                    with ExitStack() as nes:
                        pwB = self.ps("pwB", [128, HALF], nes)
                        pwC = self.ps("pwC", [128, HALF], nes)
                        vdone = []
                        gdone = []
                        for n in range(16):
                            npar = n % 2
                            if n > 0:
                                self.wait(self.Tn, vdone[n * 8 - 1])
                            last = None
                            for tt in range(2):
                                last = self.Tn.matmul(
                                    pwB[:, tt * 512:(tt + 1) * 512],
                                    self.bsel[0:16, n * 128:(n + 1) * 128],
                                    bmc[0:16, tt * 512:(tt + 1) * 512],
                                    start=True, stop=True)
                            bt = self.inc("t", last)
                            if not pa:
                                self.wait(self.S, bt)
                                if n >= 1:
                                    self.wait(self.S, gdone[n * 8 - 1])
                                bev = self.inc("s", self.S.activation(
                                    Bmb[:, :], pwB[:, :], AF.Identity))
                            if not pa:
                                self.wait(self.Tn, bev)
                                last = None
                                for tt in range(2):
                                    last = self.Tn.matmul(
                                        pwC[:, tt * 512:(tt + 1) * 512],
                                        self.bsel[32:48, n * 128:(n + 1) * 128],
                                        bmc[32:48, tt * 512:(tt + 1) * 512],
                                        start=True, stop=True)
                                cev = self.inc("t", last)
                            for c in range(8):
                                idx = n * 8 + c
                                par = idx % 2
                                if idx >= 2:
                                    self.wait(self.S, vdone[idx - 2])
                                st_ = self.inc("s", self.S.activation(
                                    dAb[:, par * HALF:(par + 1) * HALF],
                                    delta[:, c * HALF:(c + 1) * HALF], AF.Exp,
                                    scale=self.A_sb[:, l * 128 + c * 16 + n:
                                                    l * 128 + c * 16 + n + 1]),
                                    commit=(idx < 2))
                                if not pa:
                                    if idx >= 2:
                                        self.wait(self.G, vdone[idx - 2])
                                    self.wait(self.G, bev)
                                    gt_ = self.inc("g", self.G.tensor_mul(
                                        dBxb[:, par * HALF:(par + 1) * HALF],
                                        u[:, c * HALF:(c + 1) * HALF],
                                        Bmb[:, :]), commit=(idx < 2))
                                    gdone.append(gt_)
                                self.wait(self.V, st_)
                                if not pa:
                                    self.wait(self.V, gt_)
                                else:
                                    self.wait(self.V, bt)
                                    self.V.tensor_mul(
                                        dBxb[:, par * HALF:(par + 1) * HALF],
                                        u[:, c * HALF:(c + 1) * HALF],
                                        pwB[:, :])
                                col = c * 16 + n
                                self.V.tensor_tensor_scan(
                                    hscr[:, :], dAb[:, par * HALF:(par + 1) * HALF],
                                    dBxb[:, par * HALF:(par + 1) * HALF],
                                    state[:, col:col + 1], ALU.mult, ALU.add)
                                self.V.tensor_copy(state[:, col:col + 1],
                                                   hscr[:, HALF - 1:HALF])
                                if not pa:
                                    self.wait(self.V, cev)
                                    self.V.tensor_mul(
                                        dBxb[:, par * HALF:(par + 1) * HALF],
                                        hscr[:, :], pwC[:, :])
                                    vt = self.inc("v", self.V.tensor_add(
                                        yh[:, c * HALF:(c + 1) * HALF],
                                        yh[:, c * HALF:(c + 1) * HALF],
                                        dBxb[:, par * HALF:(par + 1) * HALF]))
                                else:
                                    vt = self.inc("v", self.V.engine_nop())
                                vdone.append(vt)
                    self.wait(self.S, self.tok("v"))
                    self.bar("vs")
                    dues.close()

                    if pa and hf == 1:
                        self.vrt(self.V.engine_nop())
                        vt = None
                        for c in range(8):
                            vt = self.inc("v", self.V.tensor_mul(
                                self.pfull[:, c * 16:(c + 1) * 16],
                                self.cs[:, c:c + 1].to_broadcast([128, 16]),
                                self.A_sb[:, l * 128 + c * 16:l * 128 + (c + 1) * 16]))
                        self.wait(self.S, vt)
                        self.inc("s", self.S.activation(self.pfull[:, :],
                                                        self.pfull[:, :], AF.Exp))

                    if not pa:
                        with ExitStack() as fes:
                            ybf = self.sb("ybf_h", [128, 8 * HALF], BF16, stack=fes)
                            hseg = self.sb("hseg", [128, 4 * HALF], stack=fes)
                            szb = self.sb("szb", [128, 512], stack=fes)
                            htok = self.ld(hseg[:, :].rearrange(
                                "p (c x) -> p c x", c=4),
                                self.v_h[:, :, hf * HALF:(hf + 1) * HALF])
                            for c in range(8):
                                for tt in range(2):
                                    i = self.pmm_next()
                                    last = None
                                    for kc in range(4):
                                        last = self.Tn.matmul(
                                            self.pmm[i][:, :],
                                            inWT[:, kc * FF + (8 + c) * 128:
                                                 kc * FF + (9 + c) * 128],
                                            rmsh[:, kc * T + hf * HALF + tt * 512:
                                                 kc * T + hf * HALF + (tt + 1) * 512],
                                            start=(kc == 0), stop=(kc == 3))
                                    t_ = self.inc("t", last)
                                    self.wait(self.S, t_)
                                    st_ = self.inc("s", self.S.activation(
                                        szb[:, :], self.pmm[i][:, :], AF.Silu))
                                    self.pmm_free[i] = st_
                                    self.wait(self.V, st_)
                                    vt = self.inc("v", self.V.tensor_mul(
                                        yh[:, c * HALF + tt * 512:
                                           c * HALF + (tt + 1) * 512],
                                        yh[:, c * HALF + tt * 512:
                                           c * HALF + (tt + 1) * 512],
                                        szb[:, :]))
                                    self.wait(self.S, vt)
                                vt = self.tok("v")
                                self.wait(self.S, vt)
                                self.inc("s", self.S.activation(
                                    ybf[:, c * HALF:(c + 1) * HALF],
                                    yh[:, c * HALF:(c + 1) * HALF], AF.Identity))
                            st_ = self.tok("s")
                            self.wait(self.Tn, st_)
                            self.wait(self.V, htok)
                            for tt in range(2):
                                for m in range(4):
                                    i = self.pmm_next()
                                    last = None
                                    for c in range(8):
                                        last = self.Tn.matmul(
                                            self.pmm[i][:, :],
                                            moWT[:, c * 512 + m * 128:
                                                 c * 512 + (m + 1) * 128],
                                            ybf[:, c * HALF + tt * 512:
                                                c * HALF + (tt + 1) * 512],
                                            start=(c == 0), stop=(c == 7))
                                    t_ = self.inc("t", last)
                                    self.wait(self.V, t_)
                                    vt = self.inc("v", self.V.tensor_add(
                                        hseg[:, m * HALF + tt * 512:
                                             m * HALF + (tt + 1) * 512],
                                        hseg[:, m * HALF + tt * 512:
                                             m * HALF + (tt + 1) * 512],
                                        self.pmm[i][:, :]))
                                    self.pmm_free[i] = vt
                            vt = self.tok("v")
                            self.wait(self.S, vt)
                            htok = self.stdma(
                                self.v_h[:, :, hf * HALF:(hf + 1) * HALF],
                                hseg[:, :].rearrange("p (c x) -> p c x", c=4))
                            self.wait(self.SY, htok)
            self.bar()

    def relay(self, l):
        vt = self.tok("v")
        self.wait(self.G, vt)
        self.wait(self.G, self.tok("s"))
        g1 = self.gld(self.d_ag1i[:, 0:128], self.hstate[:, :])
        g2 = self.gld(self.d_ag1i[:, 128:256], self.pfull[:, :])
        self.wait(self.G, g1)
        self.wait(self.G, g2)
        ct = self.inc("cc", self.G.collective_compute(
            "AllGather", ALU.bypass,
            replica_groups=[list(range(NCORE))],
            ins=[self.d_ag1i.ap().opt()],
            outs=[self.d_ag1o.ap().opt()]), 1)
        self.wait(self.G, ct)
        with ExitStack() as es:
            gat = self.sb("relay_gat", [128, NCORE * 256], stack=es)
            tmp = self.sb("relay_tmp", [128, 128], stack=es)
            gt = self.gld(gat[:, :].rearrange("p (j c) -> p j c", j=NCORE),
                          self.d_ag1o.ap().rearrange("(j p) c -> p j c", p=128))
            self.wait(self.V, gt)
            self.V.memset(self.hin[:, :], 0.0)
            for j in range(NCORE):
                self.V.tensor_scalar(tmp[:, :],
                                     gat[:, j * 256 + 128:j * 256 + 256],
                                     1.0, self.col("mselp", j),
                                     ALU.subtract, ALU.mult)
                self.V.tensor_scalar(tmp[:, :], tmp[:, :], 1.0, None, ALU.add)
                self.V.tensor_mul(self.hin[:, :], self.hin[:, :], tmp[:, :])
                self.V.tensor_scalar(tmp[:, :], gat[:, j * 256:j * 256 + 128],
                                     self.col("mselp", j), None, ALU.mult)
                self.V.tensor_add(self.hin[:, :], self.hin[:, :], tmp[:, :])
            self.vrt(self.V.engine_nop())
            self.bar()

    def halo_relay(self):
        self.dma_quiesce(self.G, ("ds",))
        gt = self.gld(self.d_ag2i[:, :], self.v_h[:, :, T - 3:T])
        self.wait(self.G, gt)
        ct = self.inc("cc", self.G.collective_compute(
            "AllGather", ALU.bypass,
            replica_groups=[list(range(NCORE))],
            ins=[self.d_ag2i.ap().opt()],
            outs=[self.d_ag2o.ap().opt()]), 1)
        self.wait(self.G, ct)
        with ExitStack() as es:
            gat = self.sb("halo_gat", [128, NCORE * 12], stack=es)
            gt = self.gld(gat[:, :].rearrange("p (j c) -> p j c", j=NCORE),
                          self.d_ag2o.ap().rearrange("(j p) c -> p j c", p=128))
            self.wait(self.V, gt)
            self.V.memset(self.halo_h[:, :], 0.0)
            vt = None
            for j in range(NCORE):
                vt = self.inc("v", self.V.scalar_tensor_tensor(
                    self.halo_h[:, :], gat[:, j * 12:(j + 1) * 12],
                    self.col("msel", j), self.halo_h[:, :], ALU.mult, ALU.add))
            self.bar()

    def phase_po_ln0(self):
        self.xffn = self.sb("xffn_bf", [128, 8 * T], BF16)
        self.acc = self.sb("acc_tm", [128, 16 * 1024], F32)
        self.coeff = self.sb("coeff_tm", [128, 16 * NE], F32)
        self.b2sb = self.sb("b2_sb", [NE, 1024], F32)
        with ExitStack() as es:
            poWT = self.sb("poWT", [128, 8 * 1024], BF16, es)
            gateWT = self.sb("gateWT", [128, 8 * 14], F32, es)
            outbf = self.sb("outbf", [128, 8 * T], BF16, es)
            stage = self.sb("po_stage", [128, T], stack=es)
            gt = self.gld(poWT[:, :], self.d_poWT[:, :])
            gt2 = self.ld(gateWT[:, :], self.d_gateWT[:, :])
            with ExitStack() as es0:
                stage2 = self.sb("po_stage2", [128, T], stack=es0)
                for c in range(4):
                    dt1 = self.ld(stage[:, :], self.v_x1[:, c, :])
                    self.wait(self.S, dt1)
                    st_ = self.inc("s", self.S.activation(
                        outbf[:, c * T:(c + 1) * T], stage[:, :], AF.Identity))
                    self.wait(self.SY, st_)
                for c in range(4):
                    dt1 = self.ld(stage[:, :], self.v_x2[:, c, :])
                    dt2 = self.ld(stage2[:, :], self.v_h[:, c, :])
                    self.wait(self.V, dt2)
                    self.wait(self.V, dt1)
                    vt = self.inc("v", self.V.tensor_add(stage2[:, :], stage2[:, :],
                                                         stage[:, :]))
                    self.wait(self.S, vt)
                    st_ = self.inc("s", self.S.activation(
                        outbf[:, (4 + c) * T:(5 + c) * T], stage2[:, :], AF.Identity))
                    self.wait(self.SY, st_)
                self.bar("vs")
            bt = self.ld(self.b2sb[:, :], self.d_b2r[:, :])
            xab = self.sb("xa_b", [128, 8 * 512], stack=es)
            sqb = self.sb("po_sq", [128, 512], stack=es)
            xnf = self.sb("po_xnf", [128, 512], stack=es)
            srcs = self.sb("po_srcs", [128, D], stack=es)
            rows = self.sb("po_rows", [1, 3 * 512], stack=es)
            with ExitStack() as pses:
                ptr = self.ps("po_ptr", [128, 512], pses)
                pgate = self.ps("po_pgate", [128, 512], pses)
                pst = self.ps("po_pst", [1, 512], pses)
                pst2 = self.ps("po_pst2", [1, 512], pses)
                pbc = self.ps("po_pbc", [128, 512], pses)
                pbc2 = self.ps("po_pbc2", [128, 512], pses)
                self.wait(self.Tn, gt)
                self.wait(self.Tn, gt2)
                self.wait(self.Tn, self.tok("s"))
                for tt in range(4):
                    t0 = tt * 512
                    for m in range(8):
                        i = self.pmm_next()
                        last = None
                        for kc in range(8):
                            last = self.Tn.matmul(
                                self.pmm[i][:, :],
                                poWT[:, kc * 1024 + m * 128:kc * 1024 + (m + 1) * 128],
                                outbf[:, kc * T + t0:kc * T + t0 + 512],
                                start=(kc == 0), stop=(kc == 7))
                        t_ = self.inc("t", last)
                        vsrc = self.v_x1 if m < 4 else self.v_x2
                        dt1 = self.ld(stage[:, 0:512], vsrc[:, m % 4, t0:t0 + 512])
                        self.wait(self.S, t_)
                        st_ = self.inc("s", self.S.activation(
                            xab[:, m * 512:(m + 1) * 512], self.pmm[i][:, :],
                            AF.Identity, bias=self.col("po_b", m)))
                        self.pmm_free[i] = st_
                        self.wait(self.V, st_)
                        self.wait(self.V, dt1)
                        vt = self.inc("v", self.V.tensor_add(
                            xab[:, m * 512:(m + 1) * 512],
                            xab[:, m * 512:(m + 1) * 512], stage[:, 0:512]))
                        self.wait(self.SY, vt)
                    for b in range(4):
                        dt1 = self.ld(srcs[:, :],
                                      self.d_src[t0 + b * 128:t0 + (b + 1) * 128, :])
                        self.wait(self.Tn, dt1)
                        for m in range(8):
                            tt_ = self.inc("t", self.Tn.transpose(
                                ptr[:, b * 128:(b + 1) * 128],
                                srcs[:, m * 128:(m + 1) * 128], self.idt[:, :]))
                            self.wait(self.V, tt_)
                            vt = self.inc("v", self.V.tensor_add(
                                xab[:, m * 512 + b * 128:m * 512 + (b + 1) * 128],
                                xab[:, m * 512 + b * 128:m * 512 + (b + 1) * 128],
                                ptr[:, b * 128:(b + 1) * 128]))
                            self.wait(self.Tn, vt)
                        self.wait(self.SY, self.tok("v"))
                    vt = self.tok("v")
                    self.wait(self.Tn, vt)
                    self.wait(self.S, vt)
                    last = None
                    for m in range(8):
                        last = self.Tn.matmul(pst[:1, :], self.ones_c[:, :1],
                                              xab[:, m * 512:(m + 1) * 512],
                                              start=(m == 0), stop=(m == 7))
                    t_sum = self.inc("t", last)
                    sq_t = None
                    for m in range(8):
                        if sq_t is not None:
                            self.wait(self.S, self.tok("t"))
                        st_ = self.inc("s", self.S.activation(
                            sqb[:, :], xab[:, m * 512:(m + 1) * 512], AF.Square))
                        self.wait(self.Tn, st_)
                        sq_t = self.inc("t", self.Tn.matmul(
                            pst2[:1, :], self.ones_c[:, :1], sqb[:, :],
                            start=(m == 0), stop=(m == 7)))
                    self.wait(self.V, t_sum)
                    self.wait(self.V, sq_t)
                    self.vrt(self.V.tensor_single_scalar(
                        rows[:1, 0:512], pst[:1, :], 1.0 / D, ALU.mult))
                    self.vrt(self.V.tensor_single_scalar(
                        rows[:1, 512:1024], pst2[:1, :], 1.0 / D, ALU.mult))
                    self.vrt(self.V.tensor_mul(rows[:1, 1024:1536],
                                               rows[:1, 0:512], rows[:1, 0:512]))
                    vt = self.vrt(self.V.tensor_sub(rows[:1, 512:1024],
                                                    rows[:1, 512:1024],
                                                    rows[:1, 1024:1536]))
                    self.wait(self.S, vt)
                    st_ = self.inc("s", self.S.activation(
                        rows[:1, 1024:1536], rows[:1, 512:1024], AF.Sqrt,
                        bias=self.epsc[0:1, :1]))
                    self.wait(self.V, st_)
                    vt = self.vrt(self.V.reciprocal(rows[:1, 1024:1536],
                                                    rows[:1, 1024:1536]))
                    self.wait(self.Tn, vt)
                    bt1 = self.inc("t", self.Tn.matmul(
                        pbc[:, :], self.ones_r[:1, :], rows[:1, 0:512],
                        start=True, stop=True))
                    bt2 = self.inc("t", self.Tn.matmul(
                        pbc2[:, :], self.ones_r[:1, :], rows[:1, 1024:1536],
                        start=True, stop=True))
                    self.wait(self.V, bt2)
                    for m in range(8):
                        self.V.tensor_sub(xnf[:, :], xab[:, m * 512:(m + 1) * 512],
                                          pbc[:, :])
                        vt = self.inc("v", self.V.tensor_mul(xnf[:, :], xnf[:, :],
                                                             pbc2[:, :]))
                        self.wait(self.S, vt)
                        self.inc("s", self.S.activation(
                            self.xffn[:, m * T + t0:m * T + t0 + 512],
                            xnf[:, :], AF.Identity))
                        self.wait(self.Tn, vt)
                        glast = self.Tn.matmul(
                            pgate[0:14, 0:512], gateWT[:, m * 14:(m + 1) * 14],
                            xnf[:, :], start=(m == 0), stop=(m == 7))
                        if m == 7:
                            self.inc("t", glast)
                        for b in range(4):
                            tt_ = self.inc("t", self.Tn.transpose(
                                ptr[:, b * 128:(b + 1) * 128],
                                xnf[:, b * 128:(b + 1) * 128], self.idt[:, :]))
                            self.wait(self.V, tt_)
                            vt2 = self.inc("v", self.V.tensor_copy(
                                self.acc[:, (tt * 4 + b) * 1024 + m * 128:
                                         (tt * 4 + b) * 1024 + (m + 1) * 128],
                                ptr[:, b * 128:(b + 1) * 128]))
                            self.wait(self.Tn, vt2)
                    vt = self.tok("v")
                    self.wait(self.S, vt)
                    for b in range(4):
                        self.stdma(self.d_xn[t0 + b * 128:t0 + (b + 1) * 128, :],
                                   self.acc[:, (tt * 4 + b) * 1024:
                                            (tt * 4 + b) * 1024 + 1024])
                    self.gating_tile(tt, stage, pgate, pbc2)
            self.bar()

    def gating_tile(self, tt, lgsb, pgf, ptf):
        t0 = tt * 512
        pg = pgf[0:14, 0:512]
        pt2 = ptf[0:128, 0:14]
        with ExitStack() as es:
            ltm = self.sb("gt_ltm", [128, 14], stack=es)
            prb = self.sb("gt_prb", [128, 14], stack=es)
            m8 = self.sb("gt_m8", [128, 8], stack=es)
            nmx = self.sb("gt_nmx", [128, 1], stack=es)
            den = self.sb("gt_den", [128, 1], stack=es)
            msk = self.sb("gt_msk", [128, 14], stack=es)
            self.wait(self.S, self.tok("t"))
            st_ = self.inc("s", self.S.activation(lgsb[0:14, 0:512], pg,
                                                  AF.Identity))
            self.wait(self.Tn, st_)
            for b in range(4):
                tt_ = self.inc("t", self.Tn.transpose(
                    pt2, lgsb[0:14, b * 128:(b + 1) * 128], self.idt[0:14, 0:14]))
                self.wait(self.V, tt_)
                self.vrt(self.V.tensor_copy(ltm[:, :], pt2))
                self.wait(self.Tn, self.tok("v"))
                self.vrt(self.V.reduce_max(nmx[:, :1], ltm[:, :], axis=AX,
                                           negate=True))
                vt = self.tok("v")
                self.wait(self.S, vt)
                st_ = self.inc("s", self.S.activation(prb[:, :], ltm[:, :], AF.Exp,
                                                      bias=nmx[:, :1]))
                self.wait(self.V, st_)
                self.vrt(self.V.max(out=m8[:, :], in_=prb[:, :]))
                self.vrt(self.V.tensor_add(den[:, :1], m8[:, 0:1], m8[:, 1:2]))
                self.vrt(self.V.reciprocal(den[:, :1], den[:, :1]))
                self.vrt(self.V.tensor_scalar(msk[:, :], prb[:, :], m8[:, 1:2],
                                              None, ALU.is_ge))
                self.V.tensor_mul(msk[:, :], msk[:, :], prb[:, :])
                vt = self.vrt(self.V.tensor_scalar(
                    self.coeff[:, (tt * 4 + b) * NE:(tt * 4 + b + 1) * NE],
                    msk[:, :], den[:, :1], None, ALU.mult))
            self.wait(self.SY, self.tok("t"))

    def phase_ffn(self):
        self.dma_quiesce(self.V, ("ds",))
        with ExitStack() as es:
            cfm_es = ExitStack()
            cfm = self.sb("coeff_fm", [14, 16 * 128], stack=cfm_es)
            with ExitStack() as es2:
                pt2 = self.ps("ffn_pt", [14, 128], es2)
                vt = self.tok("v")
                self.wait(self.Tn, vt)
                for tm in range(16):
                    tt_ = self.inc("t", self.Tn.transpose(
                        pt2[:, :], self.coeff[:, tm * NE:(tm + 1) * NE],
                        self.idt[:, :]))
                    self.wait(self.S, tt_)
                    st_ = self.inc("s", self.S.activation(
                        cfm[0:14, tm * 128:(tm + 1) * 128], pt2[0:14, :],
                        AF.Identity))
                    self.wait(self.Tn, st_)
                pb2 = self.ps("ffn_pb2", [128, 512], es2)
                for tm in range(16):
                    for dn in range(2):
                        t_ = self.inc("t", self.Tn.matmul(
                            pb2[:, :], cfm[0:14, tm * 128:(tm + 1) * 128],
                            self.b2sb[0:14, dn * 512:(dn + 1) * 512],
                            start=True, stop=True))
                        self.wait(self.V, t_)
                        vt = self.inc("v", self.V.tensor_add(
                            self.acc[:, tm * 1024 + dn * 512:
                                     tm * 1024 + (dn + 1) * 512],
                            self.acc[:, tm * 1024 + dn * 512:
                                     tm * 1024 + (dn + 1) * 512],
                            pb2[:, :]))
                        self.wait(self.Tn, vt)
            self.bar("vst")
            cfm_es.close()
            wsl = [self.sb(f"wsl{j}", [128, 16384], BF16, es) for j in range(2)]
            h1 = self.sb("ffn_h1", [128, 16 * 1024], BF16, es)
            pW = [self.ps(f"ffn_pw{j}", [128, 512], es) for j in range(2)]
            wtoks = [self.gld(wsl[0][:, :], self.d_w1[0][:, :]),
                     self.gld(wsl[1][:, :], self.d_w2[0][:, :])]
            w1_use = None
            w2_use = None
            for e in range(NE):
                self.wait(self.Tn, wtoks[0])
                for sp in range(2):
                    for fm in range(16):
                        i = self.pmm_next()
                        if self.pmm_free[i ^ 1] is not None:
                            pass
                        last = None
                        for kc in range(8):
                            lhs = wsl[0][:, kc * FF + fm * 128:kc * FF + (fm + 1) * 128]
                            self.Tn.matmul(
                                self.pmm[i][:, :], lhs,
                                self.xffn[:, kc * T + sp * 1024:kc * T + sp * 1024 + 512],
                                start=(kc == 0), stop=(kc == 7))
                            last = self.Tn.matmul(
                                pW[i][:, :], lhs,
                                self.xffn[:, kc * T + sp * 1024 + 512:
                                          kc * T + sp * 1024 + 1024],
                                start=(kc == 0), stop=(kc == 7))
                        t_ = self.inc("t", last)
                        self.wait(self.S, t_)
                        self.S.activation(
                            h1[:, fm * 1024:fm * 1024 + 512], self.pmm[i][:, :],
                            AF.Gelu, bias=self.col("b1", e * 16 + fm))
                        st_ = self.inc("s", self.S.activation(
                            h1[:, fm * 1024 + 512:(fm + 1) * 1024], pW[i][:, :],
                            AF.Gelu, bias=self.col("b1", e * 16 + fm)))
                        self.pmm_free[i] = st_
                    if sp == 1:
                        w1_use = self.tok("t")
                        if e + 1 < NE:
                            self.wait(self.G, w1_use)
                            wtoks[0] = self.gld(wsl[0][:, :], self.d_w1[e + 1][:, :])
                    st_ = self.tok("s")
                    self.wait(self.Tn, st_)
                    self.wait(self.Tn, wtoks[1])
                    w2_free = [None, None]
                    w2_banks = [self.pmm[0], self.pmm[1], pW[0], pW[1]]
                    for tm4 in range(8):
                        tm = sp * 8 + tm4
                        b0 = w2_banks[(tm4 % 2) * 2]
                        b1 = w2_banks[(tm4 % 2) * 2 + 1]
                        if tm4 >= 2:
                            self.wait(self.Tn, w2_free[tm4 % 2])
                        else:
                            if self.pmm_free[0] is not None:
                                self.wait(self.Tn, self.pmm_free[0])
                            if self.pmm_free[1] is not None:
                                self.wait(self.Tn, self.pmm_free[1])
                        last = None
                        for kf in range(16):
                            lhs = h1[:, kf * 1024 + tm4 * 128:kf * 1024 + (tm4 + 1) * 128]
                            self.Tn.matmul(
                                b0[:, :], lhs,
                                wsl[1][:, kf * 1024:kf * 1024 + 512],
                                start=(kf == 0), stop=(kf == 15))
                            last = self.Tn.matmul(
                                b1[:, :], lhs,
                                wsl[1][:, kf * 1024 + 512:(kf + 1) * 1024],
                                start=(kf == 0), stop=(kf == 15))
                        t_ = self.inc("t", last)
                        self.wait(self.V, t_)
                        self.V.scalar_tensor_tensor(
                            self.acc[:, tm * 1024:tm * 1024 + 512], b0[:, :],
                            self.coeff[:, tm * NE + e:tm * NE + e + 1],
                            self.acc[:, tm * 1024:tm * 1024 + 512],
                            ALU.mult, ALU.add)
                        vt = self.inc("v", self.V.scalar_tensor_tensor(
                            self.acc[:, tm * 1024 + 512:(tm + 1) * 1024], b1[:, :],
                            self.coeff[:, tm * NE + e:tm * NE + e + 1],
                            self.acc[:, tm * 1024 + 512:(tm + 1) * 1024],
                            ALU.mult, ALU.add))
                        w2_free[tm4 % 2] = vt
                    self.pmm_free = [vt, vt]
                    self.pmm_i = 0
                w2_use = self.tok("t")
                if e + 1 < NE:
                    self.wait(self.G, w2_use)
                    wtoks[1] = self.gld(wsl[1][:, :], self.d_w2[e + 1][:, :])
            self.bar()

    def phase_lnf(self):
        with ExitStack() as es:
            xna = self.sb("lnf_xna", [128, NTOK * 1024], stack=es)
            mu = self.sb("lnf_mu", [128, NTOK], stack=es)
            var = self.sb("lnf_var", [128, NTOK], stack=es)
            mu2 = self.sb("lnf_mu2", [128, NTOK], stack=es)
            rstd = self.sb("lnf_rstd", [128, NTOK], stack=es)
            ot = [self.sb(f"lnf_out{j}", [128, 1024], stack=es) for j in range(2)]
            dts = []
            for i in range(NTOK):
                dts.append(self.ld(xna[:, i * 1024:(i + 1) * 1024],
                                   self.d_xn[i * 128:(i + 1) * 128, :]))
            a = self.acc[:, :]
            self.V.tensor_scalar(a, a, -100.0, 100.0, ALU.max, ALU.min)
            for t_ in dts:
                self.wait(self.V, t_)
            vt = self.inc("v", self.V.tensor_add(xna[:, :], xna[:, :], a))
            self.wait(self.S, vt)
            self.vrt(self.V.tensor_reduce(
                mu[:, :], xna[:, :].rearrange("p (i d) -> p i d", d=1024), AX,
                ALU.add))
            self.vrt(self.V.tensor_single_scalar(mu[:, :], mu[:, :], 1.0 / D,
                                                 ALU.mult))
            st_ = self.inc("s", self.S.activation(self.acc[:, :], xna[:, :],
                                                  AF.Square))
            self.wait(self.V, st_)
            self.vrt(self.V.tensor_reduce(
                var[:, :], self.acc[:, :].rearrange("p (i d) -> p i d", d=1024),
                AX, ALU.add))
            self.vrt(self.V.tensor_single_scalar(var[:, :], var[:, :], 1.0 / D,
                                                 ALU.mult))
            self.vrt(self.V.tensor_mul(mu2[:, :], mu[:, :], mu[:, :]))
            vt = self.vrt(self.V.tensor_sub(var[:, :], var[:, :], mu2[:, :]))
            self.wait(self.S, vt)
            st_ = self.inc("s", self.S.activation(rstd[:, :], var[:, :], AF.Sqrt,
                                                  bias=self.epsc[:, :1]))
            self.wait(self.V, st_)
            self.vrt(self.V.reciprocal(rstd[:, :], rstd[:, :]))
            ot_free = [None, None]
            for tm in range(NTOK):
                par = tm % 2
                if ot_free[par] is not None:
                    self.wait(self.V, ot_free[par])
                vt = self.inc("v", self.V.tensor_scalar(
                    ot[par][:, :], xna[:, tm * 1024:(tm + 1) * 1024],
                    mu[:, tm:tm + 1], rstd[:, tm:tm + 1],
                    ALU.subtract, ALU.mult))
                self.wait(self.S, vt)
                ot_free[par] = self.stdma(self.d_out[tm * 128:(tm + 1) * 128, :],
                                          ot[par][:, :])


_BUILT = None


def _get_built():
    global _BUILT
    if _BUILT is None:
        kb = KB()
        kb.build()
        _BUILT = kb
    return _BUILT


def _pack_shared(I):
    f32 = lambda a: np.ascontiguousarray(np.asarray(a, dtype=np.float32))
    out = {}
    out["identity"] = np.eye(128, dtype=np.float32)
    bsel = np.zeros((48, NS * 128), np.float32)
    for n in range(NS):
        bsel[n, n * 128:(n + 1) * 128] = 1.0
        bsel[32 + n, n * 128:(n + 1) * 128] = 1.0
    out["bsel"] = bsel
    for l in range(NL):
        out[f"inWT_p{l}"] = f32(np.asarray(I["L0_attn_in_W"])[l].T
                                .reshape(4, 128, FF).transpose(1, 0, 2)
                                .reshape(128, 4 * FF))
        out[f"moWT_p{l}"] = f32(np.asarray(I["L0_attn_mo_W"])[l].T
                                .reshape(8, 128, 512).transpose(1, 0, 2)
                                .reshape(128, 8 * 512))
        out[f"xpWT_p{l}"] = f32(np.asarray(I["L0_attn_xp_W"])[l].T
                                .reshape(8, 128, 64).transpose(1, 0, 2)
                                .reshape(128, 8 * 64))
        out[f"dtWT_p{l}"] = f32(np.asarray(I["L0_attn_dt_W"])[l].T)
    out["poWT_p"] = f32(np.asarray(I["L0_attn_po_W"]).T
                        .reshape(8, 128, 1024).transpose(1, 0, 2)
                        .reshape(128, 8 * 1024))
    out["gateWT_p"] = f32(np.asarray(I["L0_ffn_gate_w"]).T
                          .reshape(8, 128, NE).transpose(1, 0, 2)
                          .reshape(128, 8 * NE))
    w1 = np.asarray(I["L0_ffn_w1"])
    w2 = np.asarray(I["L0_ffn_w2"])
    for e in range(NE):
        out[f"w1_p{e}"] = f32(w1[e].reshape(8, 128, FF).transpose(1, 0, 2)
                              .reshape(128, 8 * FF))
        out[f"w2_p{e}"] = f32(w2[e].reshape(16, 128, 1024).transpose(1, 0, 2)
                              .reshape(128, 16 * 1024))
    out["b2_rows"] = f32(I["L0_ffn_b2"])
    return out


def _pack_pcols_base(I):
    pc = np.zeros((128, PCN), dtype=np.float32)
    def put(name, arr):
        a = np.asarray(arr, dtype=np.float32)
        pc[:, _PC[name]:_PC[name] + a.shape[1]] = a
    put("ln_g", np.asarray(I["L0_attn_ln_g"]).reshape(8, 128).T)
    put("ln_b", np.asarray(I["L0_attn_ln_b"]).reshape(8, 128).T)
    alog = np.asarray(I["L0_attn_A_log"])
    put("alog", alog.reshape(NL, 8, 128, NS).transpose(2, 0, 1, 3)
        .reshape(128, NL * 128))
    put("Dp", np.asarray(I["L0_attn_Dp"]).reshape(NL, 8, 128)
        .transpose(2, 0, 1).reshape(128, NL * 8))
    put("cb", np.asarray(I["L0_attn_conv_b"]).reshape(NL, 8, 128)
        .transpose(2, 0, 1).reshape(128, NL * 8))
    cw = np.asarray(I["L0_attn_conv_w"]).reshape(NL, 8, 128, 4)
    put("cw", cw.transpose(2, 0, 3, 1).reshape(128, NL * 32))
    put("dtb", np.asarray(I["L0_attn_dt_b"]).reshape(NL, 8, 128)
        .transpose(2, 0, 1).reshape(128, NL * 8))
    put("rmsw", np.asarray(I["L0_attn_rms_w"]).reshape(NL, 4, 128)
        .transpose(2, 0, 1).reshape(128, NL * 4))
    put("po_b", np.asarray(I["L0_attn_po_b"]).reshape(8, 128).T)
    put("b1", np.asarray(I["L0_ffn_b1"]).reshape(NE, 16, 128)
        .transpose(2, 0, 1).reshape(128, NE * 16))
    return pc


def kernel(**inputs):
    kb = _get_built()
    shared = _pack_shared(inputs)
    pc_base = _pack_pcols_base(inputs)
    src = np.ascontiguousarray(np.asarray(inputs["L0_attn_src"], np.float32))
    in_maps = []
    for k in range(NCORE):
        seq, t0 = divmod(k * T, L)
        pc = pc_base.copy()
        if k % 4 != 0:
            pc[:, _PC["msel"] + (k - 1)] = 1.0
            pc[:, _PC["hmask"]] = 1.0
        for j in range(NCORE):
            if j // 4 == k // 4 and j < k:
                pc[:, _PC["mselp"] + j] = 1.0
        halo = np.zeros((4, D), np.float32)
        if k % 4 != 0:
            halo[0:3] = src[seq, t0 - 3:t0]
        m = {"src_tm": np.ascontiguousarray(src[seq, t0:t0 + T]),
             "halo_src": halo,
             "pcols": np.ascontiguousarray(pc)}
        m.update(shared)
        in_maps.append(m)
    res = run_bass_kernel_spmd(kb.nc, in_maps, core_ids=list(range(NCORE)))
    out = np.empty((BATCH, L, D), np.float32)
    for k in range(NCORE):
        seq, t0 = divmod(k * T, L)
        out[seq, t0:t0 + T] = res.results[k]["out_tm"]
    return out


# version 06
# speedup vs baseline: 1.0408x; 1.0344x over previous
import numpy as np
from contextlib import ExitStack

import concourse.bass as bass
import concourse.mybir as mybir
from concourse.bass_utils import run_bass_kernel_spmd

F32 = mybir.dt.float32
BF16 = mybir.dt.bfloat16
AF = mybir.ActivationFunctionType
ALU = mybir.AluOpType
AX = mybir.AxisListType.X

NCORE = 8
BATCH, L, D = 2, 8192, 1024
DC, DI, NS, DTR, NL = 512, 1024, 16, 32, 2
NE, FF = 14, 2048
T = (BATCH * L) // NCORE
HALF = 1024
NTOK = T // 128
EPS = 1e-5

_PC = {}
_pc_off = 0
def _pc(name, n):
    global _pc_off
    _PC[name] = _pc_off
    _pc_off += n
_pc("ln_g", 8); _pc("ln_b", 8)
_pc("alog", NL * 128)
_pc("Dp", NL * 8)
_pc("cb", NL * 8)
_pc("cw", NL * 32)
_pc("dtb", NL * 8)
_pc("rmsw", NL * 4)
_pc("po_b", 8)
_pc("msel", 8)
_pc("mselp", 8)
_pc("hmask", 1)
_pc("b1", NE * 16)
PCN = _pc_off


class KB:
    def __init__(self):
        self.nc = bass.Bass("TRN2", target_bir_lowering=False,
                            detect_race_conditions=False)
        self.stk = ExitStack()
        self.cnt = {}
        self.sems = {}
        self.V = self.nc.vector
        self.S = self.nc.scalar
        self.Tn = self.nc.tensor
        self.G = self.nc.gpsimd
        self.SY = self.nc.sync
        self.E = {"v": self.V, "s": self.S, "t": self.Tn, "g": self.G,
                  "y": self.SY}
        for name in ["v", "s", "t", "g", "cc"]:
            self.sems[name] = self.stk.enter_context(self.nc.semaphore(name))
            self.cnt[name] = 0
        self.pools = {}
        for pname, eng, nslot in [("dl", self.SY, 6), ("ds", self.S, 4),
                                  ("dg", self.G, 4)]:
            slots = []
            for j in range(nslot):
                sn = f"{pname}{j}"
                self.sems[sn] = self.stk.enter_context(self.nc.semaphore(sn))
                self.cnt[sn] = 0
                slots.append(sn)
            self.pools[pname] = {"eng": eng, "slots": slots, "i": 0}

    def _dma(self, pname, out_ap, in_ap):
        p = self.pools[pname]
        sn = p["slots"][p["i"] % len(p["slots"])]
        p["i"] += 1
        if self.cnt[sn] > 0:
            p["eng"].wait_ge(self.sems[sn], self.cnt[sn])
        instr = p["eng"].dma_start(out=out_ap, in_=in_ap)
        instr.then_inc(self.sems[sn], 16)
        self.cnt[sn] += 16
        return (sn, self.cnt[sn])

    def dma_quiesce(self, eng, pnames=("dl", "ds", "dg")):
        for pname in pnames:
            for sn in self.pools[pname]["slots"]:
                if self.cnt[sn] > 0:
                    eng.wait_ge(self.sems[sn], self.cnt[sn])

    _ENG_OF_SEM = {"v": "v", "s": "s", "t": "t", "g": "g"}

    def inc(self, name, instr, amt=1, commit=True):
        instr.then_inc(self.sems[name], amt)
        self.cnt[name] += amt
        if commit and name in self._ENG_OF_SEM:
            self.E[name].wait_ge(self.sems[name], self.cnt[name])
        return (name, self.cnt[name])

    def tok(self, name):
        return (name, self.cnt[name])

    def wait(self, eng, token):
        name, v = token
        if v > 0:
            eng.wait_ge(self.sems[name], v)

    def vrt(self, instr):
        t = self.inc("v", instr)
        self.wait(self.V, t)
        return t

    def bar(self, engines="vstg"):
        toks = []
        for ch in engines:
            toks.append(self.inc(ch, self.E[ch].nop()))
        for ch in engines:
            for t in toks:
                if t[0] != ch:
                    self.wait(self.E[ch], t)
        for t in toks:
            self.wait(self.SY, t)
            if "g" not in engines:
                self.wait(self.G, t)
        self.dma_quiesce(self.SY, ("ds", "dg"))
        self.dma_quiesce(self.G, ("ds", "dl"))

    def ld(self, out_ap, in_ap):
        return self._dma("dl", out_ap, in_ap)

    def stdma(self, out_ap, in_ap):
        return self._dma("ds", out_ap, in_ap)

    def gld(self, out_ap, in_ap):
        return self._dma("dg", out_ap, in_ap)

    _uid = 0

    def _u(self, name):
        KB._uid += 1
        return f"{name}_{KB._uid}"

    def sb(self, name, shape, dtype=F32, stack=None):
        return (stack or self.stk).enter_context(
            self.nc.sbuf_tensor(self._u(name), shape, dtype))

    def ps(self, name, shape, stack=None):
        return (stack or self.stk).enter_context(
            self.nc.psum_tensor(self._u(name), shape, F32))

    def col(self, name, idx=0):
        c = _PC[name] + idx
        return self.pcols[:, c:c + 1]

    def pmm_init(self):
        self.pmm = [self.ps("pmm0", [128, 512]), self.ps("pmm1", [128, 512])]
        self.pmm_free = [None, None]
        self.pmm_i = 0

    def pmm_next(self):
        i = self.pmm_i
        self.pmm_i ^= 1
        if self.pmm_free[i] is not None:
            self.wait(self.Tn, self.pmm_free[i])
        return i

    def build(self):
        nc = self.nc
        self.d_src = nc.dram_tensor("src_tm", [T, D], F32, kind="ExternalInput")
        self.d_halo = nc.dram_tensor("halo_src", [4, D], F32, kind="ExternalInput")
        self.d_pcols = nc.dram_tensor("pcols", [128, PCN], F32, kind="ExternalInput")
        self.d_idt = nc.dram_tensor("identity", [128, 128], F32, kind="ExternalInput")
        self.d_bsel = nc.dram_tensor("bsel", [48, NS * 128], F32, kind="ExternalInput")
        self.d_inWT = [nc.dram_tensor(f"inWT_p{l}", [128, 4 * FF], F32, kind="ExternalInput")
                       for l in range(NL)]
        self.d_moWT = [nc.dram_tensor(f"moWT_p{l}", [128, 8 * 512], F32, kind="ExternalInput")
                       for l in range(NL)]
        self.d_xpWT = [nc.dram_tensor(f"xpWT_p{l}", [128, 8 * 64], F32, kind="ExternalInput")
                       for l in range(NL)]
        self.d_dtWT = [nc.dram_tensor(f"dtWT_p{l}", [32, 1024], F32, kind="ExternalInput")
                       for l in range(NL)]
        self.d_poWT = nc.dram_tensor("poWT_p", [128, 8 * 1024], F32, kind="ExternalInput")
        self.d_gateWT = nc.dram_tensor("gateWT_p", [128, 8 * 14], F32, kind="ExternalInput")
        self.d_w1 = [nc.dram_tensor(f"w1_p{e}", [128, 8 * FF], F32, kind="ExternalInput")
                     for e in range(NE)]
        self.d_w2 = [nc.dram_tensor(f"w2_p{e}", [128, 16 * 1024], F32, kind="ExternalInput")
                     for e in range(NE)]
        self.d_b2r = nc.dram_tensor("b2_rows", [NE, 1024], F32, kind="ExternalInput")
        self.d_out = nc.dram_tensor("out_tm", [T, D], F32, kind="ExternalOutput")
        self.d_x1 = nc.dram_tensor("x1_d", [512, T], F32, kind="Internal")
        self.d_x2 = nc.dram_tensor("x2_d", [512, T], F32, kind="Internal")
        self.d_h = nc.dram_tensor("h_d", [512, T], F32, kind="Internal")
        self.d_xn = nc.dram_tensor("xn_d", [T, D], F32, kind="Internal")
        self.d_ag1i = nc.dram_tensor("ag1_in", [128, 256], F32, kind="Internal")
        self.d_ag1o = nc.dram_tensor("ag1_out", [NCORE * 128, 256], F32, kind="Internal")
        self.d_ag2i = nc.dram_tensor("ag2_in", [128, 12], F32, kind="Internal")
        self.d_ag2o = nc.dram_tensor("ag2_out", [NCORE * 128, 12], F32, kind="Internal")

        self.v_x1 = self.d_x1.ap().rearrange("(c p) t -> p c t", p=128)
        self.v_x2 = self.d_x2.ap().rearrange("(c p) t -> p c t", p=128)
        self.v_h = self.d_h.ap().rearrange("(c p) t -> p c t", p=128)

        self.pcols = self.sb("pcols_sb", [128, PCN])
        self.idt = self.sb("idt_sb", [128, 128])
        self.ones_c = self.sb("ones_c", [128, 1])
        self.ones_r = self.sb("ones_r", [1, 128])
        self.epsc = self.sb("epsc", [128, 1])
        self.A_sb = self.sb("A_sb", [128, NL * 128])
        self.halo_x1 = self.sb("halo_x1", [128, 12])
        self.halo_h = self.sb("halo_h", [128, 12])
        self.hstate = self.sb("hstate", [128, 128])
        self.pfull = self.sb("pfull", [128, 128])
        self.hin = self.sb("hin", [128, 128])
        self.cs = self.sb("cs_tot", [128, 8])

        self.pmm_init()

        t1 = self.ld(self.pcols[:, :], self.d_pcols[:, :])
        t2 = self.ld(self.idt[:, :], self.d_idt[:, :])
        for e in "vstg":
            self.wait(self.E[e], t1)
            self.wait(self.E[e], t2)
        self.V.memset(self.ones_c[:, :], 1.0)
        self.V.memset(self.ones_r[:, :], 1.0)
        self.V.memset(self.epsc[:, :], EPS)
        stok = self.inc("s", self.S.activation(
            self.A_sb[:, :],
            self.pcols[:, _PC["alog"]:_PC["alog"] + NL * 128], AF.Exp))
        self.wait(self.V, stok)
        vt = self.inc("v", self.V.tensor_single_scalar(
            self.A_sb[:, :], self.A_sb[:, :], -1.0, ALU.mult))
        self.wait(self.S, vt)
        self.bar()

        self.phase_ln1()
        for l in range(NL):
            self.mamba_layer(l)
        self.phase_po_ln0()
        self.phase_ffn()
        self.phase_lnf()
        self.dma_quiesce(self.SY)
        return nc

    def phase_ln1(self):
        with ExitStack() as es:
            sxa = self.sb("ln_sxa", [128, NTOK * D], stack=es)
            sqa = self.sb("ln_sqa", [128, NTOK * D], stack=es)
            mu = self.sb("ln_mu", [128, NTOK], stack=es)
            var = self.sb("ln_var", [128, NTOK], stack=es)
            mu2 = self.sb("ln_mu2", [128, NTOK], stack=es)
            rstd = self.sb("ln_rstd", [128, NTOK], stack=es)
            stg = [self.sb(f"ln_st{j}", [128, 8 * 128], stack=es) for j in range(2)]
            ptr = [self.ps(f"ln_ptr{j}", [128, 128], stack=es) for j in range(4)]
            dts = []
            for i in range(NTOK):
                dts.append(self.ld(sxa[:, i * D:(i + 1) * D],
                                   self.d_src[i * 128:(i + 1) * 128, :]))
            for t_ in dts:
                self.wait(self.V, t_)
                self.wait(self.S, t_)
            self.vrt(self.V.tensor_reduce(
                mu[:, :], sxa[:, :].rearrange("p (i d) -> p i d", d=D), AX, ALU.add))
            self.vrt(self.V.tensor_single_scalar(mu[:, :], mu[:, :], 1.0 / D, ALU.mult))
            st_ = self.inc("s", self.S.activation(sqa[:, :], sxa[:, :], AF.Square))
            self.wait(self.V, st_)
            self.vrt(self.V.tensor_reduce(
                var[:, :], sqa[:, :].rearrange("p (i d) -> p i d", d=D), AX, ALU.add))
            self.vrt(self.V.tensor_single_scalar(var[:, :], var[:, :], 1.0 / D, ALU.mult))
            self.vrt(self.V.tensor_mul(mu2[:, :], mu[:, :], mu[:, :]))
            vt = self.vrt(self.V.tensor_sub(var[:, :], var[:, :], mu2[:, :]))
            self.wait(self.S, vt)
            st_ = self.inc("s", self.S.activation(rstd[:, :], var[:, :], AF.Sqrt,
                                                  bias=self.epsc[:, :1]))
            self.wait(self.V, st_)
            self.vrt(self.V.reciprocal(rstd[:, :], rstd[:, :]))
            for i in range(NTOK):
                self.V.tensor_scalar(sqa[:, i * D:(i + 1) * D],
                                     sxa[:, i * D:(i + 1) * D],
                                     mu[:, i:i + 1], rstd[:, i:i + 1],
                                     ALU.subtract, ALU.mult)
            vt = self.vrt(self.V.engine_nop())
            self.wait(self.Tn, vt)
            ptr_free = [None, None, None, None]
            stg_free = [None, None]
            for i in range(NTOK):
                par = i % 2
                if stg_free[par] is not None:
                    self.wait(self.S, stg_free[par])
                ev_last = None
                for c in range(8):
                    p2 = c % 4
                    if ptr_free[p2] is not None:
                        self.wait(self.Tn, ptr_free[p2])
                    tt_ = self.inc("t", self.Tn.transpose(
                        ptr[p2][:, :], sqa[:, i * D + c * 128:i * D + (c + 1) * 128],
                        self.idt[:, :]), commit=False)
                    self.wait(self.S, tt_)
                    ev_last = self.inc("s", self.S.activation(
                        stg[par][:, c * 128:(c + 1) * 128], ptr[p2][:, :],
                        AF.Identity, bias=self.col("ln_b", c),
                        scale=self.col("ln_g", c)), commit=False)
                    ptr_free[p2] = ev_last
                ts0 = i * 128
                self.inc("s", self.S.nop())
                self.wait(self.S, self.tok("s"))
                self.stdma(self.v_x1[:, :, ts0:ts0 + 128], stg[par][:, 0:512])
                self.stdma(self.v_x2[:, :, ts0:ts0 + 128], stg[par][:, 512:1024])
                stg_free[par] = self.stdma(self.v_h[:, :, ts0:ts0 + 128],
                                           stg[par][:, 0:512])
            vt = self.inc("v", self.V.memset(sxa[:, 0:D], 0.0))
            self.wait(self.SY, vt)
            self.wait(self.SY, self.tok("s"))
            dtok = self.ld(sxa[0:4, 0:D], self.d_halo[:, :])
            self.wait(self.V, dtok)
            self.wait(self.S, dtok)
            self.vrt(self.V.reduce_sum(mu[:, 0:1], sxa[:, 0:D], axis=AX))
            self.vrt(self.V.tensor_single_scalar(mu[:, 0:1], mu[:, 0:1], 1.0 / D, ALU.mult))
            st_ = self.inc("s", self.S.activation(sqa[:, 0:D], sxa[:, 0:D], AF.Square))
            self.wait(self.V, st_)
            self.vrt(self.V.reduce_sum(var[:, 0:1], sqa[:, 0:D], axis=AX))
            self.vrt(self.V.tensor_single_scalar(var[:, 0:1], var[:, 0:1], 1.0 / D, ALU.mult))
            self.vrt(self.V.tensor_mul(mu2[:, 0:1], mu[:, 0:1], mu[:, 0:1]))
            vt = self.vrt(self.V.tensor_sub(var[:, 0:1], var[:, 0:1], mu2[:, 0:1]))
            self.wait(self.S, vt)
            st_ = self.inc("s", self.S.activation(rstd[:, 0:1], var[:, 0:1], AF.Sqrt,
                                                  bias=self.epsc[:, :1]))
            self.wait(self.V, st_)
            self.vrt(self.V.reciprocal(rstd[:, 0:1], rstd[:, 0:1]))
            vt = self.inc("v", self.V.tensor_scalar(
                sqa[:, 0:D], sxa[:, 0:D], mu[:, 0:1], rstd[:, 0:1],
                ALU.subtract, ALU.mult))
            self.wait(self.Tn, vt)
            ev_last = None
            for c in range(4):
                p2 = c % 4
                if ptr_free[p2] is not None:
                    self.wait(self.Tn, ptr_free[p2])
                tt_ = self.inc("t", self.Tn.transpose(
                    ptr[p2][:, :], sqa[:, c * 128:(c + 1) * 128], self.idt[:, :]))
                self.wait(self.S, tt_)
                ev_last = self.inc("s", self.S.activation(
                    self.halo_x1[:, c * 3:(c + 1) * 3], ptr[p2][:, 0:3],
                    AF.Identity, bias=self.col("ln_b", c),
                    scale=self.col("ln_g", c)))
                ptr_free[p2] = ev_last
            self.wait(self.V, ev_last)
            self.inc("v", self.V.tensor_copy(self.halo_h[:, :], self.halo_x1[:, :]))
            self.bar()

    def compute_rmsh(self, l, rmsh):
        with ExitStack() as es:
            pst = self.ps("rms_pst", [1, T], es)
            hst = [self.sb(f"rms_h{j}", [128, T], stack=es) for j in range(2)]
            sqb = [self.sb(f"rms_sq{j}", [128, T], stack=es) for j in range(2)]
            rr = self.sb("rms_rr", [1, T], stack=es)
            pb = self.ps("rms_pb", [128, 512], es)
            mm_toks = [None, None]
            sq_done = [None, None]
            for cc in range(4):
                par = cc % 2
                if sq_done[par] is not None:
                    self.wait(self.SY, sq_done[par])
                dtok = self.ld(hst[par][:, :], self.v_h[:, cc, :])
                self.wait(self.S, dtok)
                if mm_toks[par] is not None:
                    self.wait(self.S, mm_toks[par])
                st_ = self.inc("s", self.S.activation(sqb[par][:, :], hst[par][:, :], AF.Square))
                sq_done[par] = st_
                self.wait(self.Tn, st_)
                last = None
                for tt in range(4):
                    last = self.Tn.matmul(pst[:, tt * 512:(tt + 1) * 512],
                                          self.ones_c[:, :1],
                                          sqb[par][:, tt * 512:(tt + 1) * 512],
                                          start=(cc == 0), stop=(cc == 3))
                mm_toks[par] = self.inc("t", last)
            self.wait(self.V, mm_toks[(4 - 1) % 2])
            vt = self.vrt(self.V.tensor_single_scalar(rr[:1, :], pst[:1, :], 1.0 / DC, ALU.mult))
            self.wait(self.S, vt)
            st_ = self.inc("s", self.S.activation(rr[:1, :], rr[:1, :], AF.Sqrt,
                                                  bias=self.epsc[0:1, :1]))
            self.wait(self.V, st_)
            vt = self.vrt(self.V.reciprocal(rr[:1, :], rr[:1, :]))
            self.wait(self.Tn, vt)
            self.wait(self.SY, self.tok("s"))
            hs_used = [None, None]
            for tt in range(4):
                bt = self.inc("t", self.Tn.matmul(pb[:, :], self.ones_r[:1, :],
                                                  rr[:1, tt * 512:(tt + 1) * 512],
                                                  start=True, stop=True))
                vlast = None
                for cc in range(4):
                    par = cc % 2
                    if hs_used[par] is not None:
                        self.wait(self.SY, hs_used[par])
                    dtok = self.ld(hst[par][:, 0:512], self.v_h[:, cc, tt * 512:(tt + 1) * 512])
                    self.wait(self.V, dtok)
                    self.wait(self.V, bt)
                    vlast = self.inc("v", self.V.scalar_tensor_tensor(
                        rmsh[:, cc * T + tt * 512: cc * T + (tt + 1) * 512],
                        hst[par][:, 0:512], self.col("rmsw", l * 4 + cc),
                        pb[:, :], ALU.mult, ALU.mult))
                    hs_used[par] = vlast
                self.wait(self.Tn, vlast)
            self.bar()

    def halo_xcpre(self, l, inWT, hxc):
        with ExitStack() as es:
            psq = self.ps("hx_psq", [1, 4], es)
            pb = self.ps("hx_pb", [128, 4], es)
            pm = self.ps("hx_pm", [128, 4], es)
            sqh = self.sb("hx_sq", [128, 12], stack=es)
            rr = self.sb("hx_rr", [1, 4], stack=es)
            rmf = self.sb("hx_rmf", [128, 12], stack=es)
            rmh = self.sb("hx_rmh", [128, 12], BF16, stack=es)
            st_ = self.inc("s", self.S.activation(sqh[:, :], self.halo_h[:, :], AF.Square))
            self.wait(self.Tn, st_)
            last = None
            for cc in range(4):
                last = self.Tn.matmul(psq[:1, 0:3], self.ones_c[:, :1],
                                      sqh[:, cc * 3:(cc + 1) * 3],
                                      start=(cc == 0), stop=(cc == 3))
            t_ = self.inc("t", last)
            self.wait(self.V, t_)
            vt = self.vrt(self.V.tensor_single_scalar(rr[:1, 0:3], psq[:1, 0:3], 1.0 / DC, ALU.mult))
            self.wait(self.S, vt)
            st_ = self.inc("s", self.S.activation(rr[:1, 0:3], rr[:1, 0:3], AF.Sqrt,
                                                  bias=self.epsc[0:1, :1]))
            self.wait(self.V, st_)
            vt = self.vrt(self.V.reciprocal(rr[:1, 0:3], rr[:1, 0:3]))
            self.wait(self.Tn, vt)
            bt = self.inc("t", self.Tn.matmul(pb[:, 0:3], self.ones_r[:1, :],
                                              rr[:1, 0:3], start=True, stop=True))
            self.wait(self.V, bt)
            vlast = None
            for cc in range(4):
                vlast = self.inc("v", self.V.scalar_tensor_tensor(
                    rmf[:, cc * 3:(cc + 1) * 3], self.halo_h[:, cc * 3:(cc + 1) * 3],
                    self.col("rmsw", l * 4 + cc), pb[:, 0:3], ALU.mult, ALU.mult))
            self.wait(self.S, vlast)
            st_ = self.inc("s", self.S.activation(rmh[:, :], rmf[:, :], AF.Identity))
            self.wait(self.Tn, st_)
            for j in range(8):
                last = None
                for cc in range(4):
                    last = self.Tn.matmul(pm[:, 0:3],
                                          inWT[:, cc * FF + j * 128:cc * FF + (j + 1) * 128],
                                          rmh[:, cc * 3:(cc + 1) * 3],
                                          start=(cc == 0), stop=(cc == 3))
                t_ = self.inc("t", last)
                self.wait(self.S, t_)
                st_ = self.inc("s", self.S.activation(
                    hxc[:, j * 3:(j + 1) * 3], pm[:, 0:3], AF.Identity,
                    scale=self.col("hmask")))
                self.wait(self.Tn, st_)
            self.bar()

    def mamba_layer(self, l):
        with ExitStack() as es:
            inWT = self.sb("inWT", [128, 4 * FF], BF16, es)
            moWT = self.sb("moWT", [128, 8 * 512], BF16, es)
            xpWT = self.sb("xpWT", [128, 8 * 64], BF16, es)
            dtWT = self.sb("dtWT", [32, 1024], BF16, es)
            rmsh = self.sb("rmsh", [128, 4 * T], BF16, es)
            dbc = self.sb("dbc", [64, T], F32, es)
            dtbf = self.sb("dtbf", [32, T], BF16, es)
            hxc = self.sb("hxc", [128, 24], F32, es)
            self.bsel = self.sb("bsel_sb", [48, NS * 128], stack=es)
            self.ld(self.bsel[:, :], self.d_bsel[:, :])
            gtoks = [self.gld(inWT[:, :], self.d_inWT[l][:, :]),
                     self.gld(moWT[:, :], self.d_moWT[l][:, :]),
                     self.gld(xpWT[:, :], self.d_xpWT[l][:, :]),
                     self.gld(dtWT[:, :], self.d_dtWT[l][:, :])]
            self.compute_rmsh(l, rmsh)
            for e in "vst":
                for gt in gtoks:
                    self.wait(self.E[e], gt)
            self.halo_xcpre(l, inWT, hxc)
            self.mamba_pass(l, True, inWT, moWT, xpWT, dtWT, rmsh, dbc, dtbf, hxc)
            self.relay(l)
            self.mamba_pass(l, False, inWT, moWT, xpWT, dtWT, rmsh, dbc, dtbf, hxc)
            if l < NL - 1:
                self.halo_relay()
            self.bar()

    def _xc_mm(self, l, hf, c, inWT, xcpre):
        for tt in range(2):
            i = self.pmm_next()
            last = None
            for kc in range(4):
                last = self.Tn.matmul(
                    self.pmm[i][:, :],
                    inWT[:, kc * FF + c * 128:kc * FF + (c + 1) * 128],
                    self.rmsh_cur[:, kc * T + hf * HALF + tt * 512:
                                  kc * T + hf * HALF + (tt + 1) * 512],
                    start=(kc == 0), stop=(kc == 3))
            t_ = self.inc("t", last, commit=False)
            self.wait(self.S, t_)
            ev = self.inc("s", self.S.activation(
                xcpre[:, 3 + tt * 512:3 + (tt + 1) * 512], self.pmm[i][:, :],
                AF.Identity), commit=False)
            self.pmm_free[i] = ev
        return ev

    def _xc_conv(self, l, hf, c, xcpre, cvo, out_sb, out_off, halo3, ev):
        self.wait(self.V, ev)
        self.inc("v", self.V.tensor_copy(xcpre[:, 0:3], halo3[:, c * 3:(c + 1) * 3]),
                 commit=False)
        self.V.tensor_scalar(cvo[:, :HALF], xcpre[:, 0:HALF],
                             self.col("cw", l * 32 + 0 * 8 + c), None, ALU.mult)
        for k in range(1, 4):
            self.V.scalar_tensor_tensor(cvo[:, :HALF], xcpre[:, k:k + HALF],
                                        self.col("cw", l * 32 + k * 8 + c),
                                        cvo[:, :HALF], ALU.mult, ALU.add)
        vt = self.inc("v", self.V.tensor_copy(halo3[:, c * 3:(c + 1) * 3],
                                              xcpre[:, 1024:1027]))
        self.wait(self.S, vt)
        st_ = self.inc("s", self.S.activation(out_sb[:, out_off:out_off + HALF],
                                              cvo[:, :HALF], AF.Silu,
                                              bias=self.col("cb", l * 8 + c)))
        return st_

    def _xc_chunk(self, l, hf, c, inWT, xcpre, cvo, out_sb, out_off, halo3):
        for tt in range(2):
            i = self.pmm_next()
            last = None
            for kc in range(4):
                last = self.Tn.matmul(
                    self.pmm[i][:, :],
                    inWT[:, kc * FF + c * 128:kc * FF + (c + 1) * 128],
                    self.rmsh_cur[:, kc * T + hf * HALF + tt * 512:
                                  kc * T + hf * HALF + (tt + 1) * 512],
                    start=(kc == 0), stop=(kc == 3))
            t_ = self.inc("t", last)
            self.wait(self.S, t_)
            ev = self.inc("s", self.S.activation(
                xcpre[:, 3 + tt * 512:3 + (tt + 1) * 512], self.pmm[i][:, :],
                AF.Identity))
            self.pmm_free[i] = ev
        self.wait(self.V, self.tok("s"))
        self.inc("v", self.V.tensor_copy(xcpre[:, 0:3], halo3[:, c * 3:(c + 1) * 3]))
        self.V.tensor_scalar(cvo[:, :], xcpre[:, 0:HALF],
                             self.col("cw", l * 32 + 0 * 8 + c), None, ALU.mult)
        for k in range(1, 4):
            self.V.scalar_tensor_tensor(cvo[:, :], xcpre[:, k:k + HALF],
                                        self.col("cw", l * 32 + k * 8 + c),
                                        cvo[:, :], ALU.mult, ALU.add)
        vt = self.inc("v", self.V.tensor_copy(halo3[:, c * 3:(c + 1) * 3],
                                              xcpre[:, 1024:1027]))
        self.wait(self.S, vt)
        st_ = self.inc("s", self.S.activation(out_sb[:, out_off:out_off + HALF],
                                              cvo[:, :], AF.Silu,
                                              bias=self.col("cb", l * 8 + c)))
        return st_

    def _delta_mm(self, l, hf, c, dtWT, dtbf, pD, pD_free):
        par = c % 2
        if pD_free[par] is not None:
            self.wait(self.Tn, pD_free[par])
        last = None
        for tt in range(2):
            last = self.Tn.matmul(
                pD[:, par * HALF + tt * 512:par * HALF + (tt + 1) * 512],
                dtWT[:, c * 128:(c + 1) * 128],
                dtbf[:, hf * HALF + tt * 512:hf * HALF + (tt + 1) * 512],
                start=True, stop=True)
        return self.inc("t", last, commit=False)

    def _delta_sp(self, l, hf, c, pD, delta, scratch, t_):
        par = c % 2
        self.wait(self.S, t_)
        self.S.activation(scratch[:, 0:HALF], pD[:, par * HALF:(par + 1) * HALF],
                          AF.Exp, bias=self.col("dtb", l * 8 + c))
        st_ = self.inc("s", self.S.activation(delta[:, c * HALF:(c + 1) * HALF],
                                              scratch[:, 0:HALF], AF.Ln, bias=1.0),
                       commit=False)
        return st_

    def _delta_chunk(self, l, hf, c, dtWT, dtbf, pD, delta, scratch):
        self.wait(self.Tn, self.tok("s"))
        last = None
        for tt in range(2):
            last = self.Tn.matmul(
                pD[:, tt * 512:(tt + 1) * 512], dtWT[:, c * 128:(c + 1) * 128],
                dtbf[:, hf * HALF + tt * 512:hf * HALF + (tt + 1) * 512],
                start=True, stop=True)
        t_ = self.inc("t", last)
        self.wait(self.S, t_)
        self.S.activation(scratch[:, 0:HALF], pD[:, :HALF], AF.Exp,
                          bias=self.col("dtb", l * 8 + c))
        st_ = self.inc("s", self.S.activation(delta[:, c * HALF:(c + 1) * HALF],
                                              scratch[:, 0:HALF], AF.Ln, bias=1.0))
        return st_

    def mamba_pass(self, l, pa, inWT, moWT, xpWT, dtWT, rmsh, dbc, dtbf, hxc):
        self.rmsh_cur = rmsh
        with ExitStack() as pes:
            halo3 = self.sb("halo3", [128, 24], stack=pes)
            xcpre = self.sb("xcpre", [128, 1032], stack=pes)
            cvo = self.sb("cvo", [128, HALF], stack=pes)
            xcf = self.sb("xcf", [128, HALF], stack=pes)
            dAb = self.sb("dAb", [128, 2 * HALF], stack=pes)
            dBxb = self.sb("dBxb", [128, 2 * HALF], stack=pes)
            hscr = self.sb("hscr", [128, HALF], stack=pes)
            bmc = self.sb("bmc", [48, HALF], stack=pes)
            Bmb = self.sb("Bmb", [128, HALF], stack=pes)
            if pa:
                xcbf = self.sb("xcbf", [128, 2 * HALF], BF16, stack=pes)
                state = self.hstate
                vt = self.inc("v", self.V.memset(state[:, :], 0.0))
                self.inc("v", self.V.memset(self.cs[:, :], 0.0))
            else:
                state = self.hin
            vt = self.inc("v", self.V.tensor_copy(halo3[:, :], hxc[:, :]))
            self.wait(self.S, vt)
            self.wait(self.Tn, vt)

            for hf in range(2):
                with ExitStack() as hes:
                    yh = None
                    if not pa:
                        yh = self.sb("y_h", [128, 8 * HALF], stack=hes)
                    dues = ExitStack()
                    delta = self.sb("delta_h", [128, 8 * HALF], stack=dues)
                    u = self.sb("u_h", [128, 8 * HALF], stack=dues)
                    with ExitStack() as ces:
                        if pa:
                            pdbc_es = ExitStack()
                            pdbc = self.ps("pdbc", [64, T], pdbc_es)
                            bufs = [(xcpre, cvo), (dAb, dBxb)]
                            evs = [None] * 8
                            xbfs = [xcbf[:, 0:HALF], xcbf[:, HALF:2 * HALF]]
                            cast_t = [None, None]
                            for c in range(9):
                                if c < 8:
                                    evs[c] = self._xc_mm(l, hf, c, inWT,
                                                         bufs[c % 2][0])
                                if c >= 1:
                                    cc = c - 1
                                    xp_, cv_ = bufs[cc % 2]
                                    st_ = self._xc_conv(l, hf, cc, xp_, cv_,
                                                        u, cc * HALF, halo3,
                                                        evs[cc])
                                    xb = xbfs[cc % 2]
                                    if cast_t[cc % 2] is not None:
                                        self.wait(self.S, cast_t[cc % 2])
                                    cst = self.inc("s", self.S.activation(
                                        xb, u[:, cc * HALF:(cc + 1) * HALF],
                                        AF.Identity), commit=False)
                                    self.wait(self.Tn, cst)
                                    last = None
                                    for tt in range(2):
                                        last = self.Tn.matmul(
                                            pdbc[:, hf * HALF + tt * 512:
                                                 hf * HALF + (tt + 1) * 512],
                                            xpWT[:, cc * 64:(cc + 1) * 64],
                                            xb[:, tt * 512:(tt + 1) * 512],
                                            start=(cc == 0), stop=(cc == 7))
                                    cast_t[cc % 2] = self.inc("t", last,
                                                              commit=False)
                            self.inc("t", self.Tn.nop())
                            self.inc("s", self.S.nop())
                            t_ = self.tok("t")
                            self.wait(self.S, t_)
                            self.S.activation(dbc[0:64, hf * HALF:(hf + 1) * HALF],
                                              pdbc[:, hf * HALF:(hf + 1) * HALF],
                                              AF.Identity)
                            st_ = self.inc("s", self.S.activation(
                                dtbf[:, hf * HALF:(hf + 1) * HALF],
                                dbc[0:32, hf * HALF:(hf + 1) * HALF], AF.Identity))
                            self.wait(self.Tn, st_)
                            self.wait(self.V, st_)
                            pdbc_es.close()
                            pD = self.ps("pD", [128, 2 * HALF], ces)
                            pD_free = [None, None]
                            dts = [None] * 8
                            sps = [None] * 8
                            sbufs = [cvo, dBxb]
                            for c in range(10):
                                if c < 8:
                                    dts[c] = self._delta_mm(l, hf, c, dtWT, dtbf,
                                                            pD, pD_free)
                                if 1 <= c <= 8:
                                    cc = c - 1
                                    sps[cc] = self._delta_sp(l, hf, cc, pD, delta,
                                                             sbufs[cc % 2], dts[cc])
                                    pD_free[cc % 2] = sps[cc]
                                if c >= 2:
                                    cc = c - 2
                                    self.wait(self.V, sps[cc])
                                    vt = self.inc("v", self.V.tensor_mul(
                                        u[:, cc * HALF:(cc + 1) * HALF],
                                        delta[:, cc * HALF:(cc + 1) * HALF],
                                        u[:, cc * HALF:(cc + 1) * HALF]),
                                        commit=False)
                                    self.vrt(self.V.reduce_sum(
                                        xcf[:, 0:1],
                                        delta[:, cc * HALF:(cc + 1) * HALF],
                                        axis=AX))
                                    if hf == 0:
                                        self.vrt(self.V.tensor_copy(
                                            self.cs[:, cc:cc + 1], xcf[:, 0:1]))
                                    else:
                                        self.vrt(self.V.tensor_add(
                                            self.cs[:, cc:cc + 1],
                                            self.cs[:, cc:cc + 1], xcf[:, 0:1]))
                        else:
                            pD = self.ps("pD", [128, 2 * HALF], ces)
                            bufs = [(xcpre, cvo), (dAb, dBxb)]
                            xcfs = [xcf[:, 0:HALF], hscr[:, 0:HALF]]
                            pD_free = [None, None]
                            dts = [None] * 8
                            evs = [None] * 8
                            uts = [None, None]
                            for c in range(9):
                                if c < 8:
                                    dts[c] = self._delta_mm(l, hf, c, dtWT, dtbf,
                                                            pD, pD_free)
                                    evs[c] = self._xc_mm(l, hf, c, inWT,
                                                         bufs[c % 2][0])
                                if c >= 1:
                                    cc = c - 1
                                    par = cc % 2
                                    sp_ = self._delta_sp(
                                        l, hf, cc, pD, delta,
                                        yh[:, cc * HALF:(cc + 1) * HALF], dts[cc])
                                    pD_free[par] = sp_
                                    if uts[par] is not None:
                                        self.wait(self.S, uts[par])
                                    xt = self._xc_conv(l, hf, cc, bufs[par][0],
                                                       bufs[par][1], xcfs[par], 0,
                                                       halo3, evs[cc])
                                    self.wait(self.V, xt)
                                    self.V.tensor_scalar(
                                        yh[:, cc * HALF:(cc + 1) * HALF],
                                        xcfs[par], self.col("Dp", l * 8 + cc),
                                        None, ALU.mult)
                                    uts[par] = self.inc("v", self.V.tensor_mul(
                                        u[:, cc * HALF:(cc + 1) * HALF],
                                        delta[:, cc * HALF:(cc + 1) * HALF],
                                        xcfs[par]), commit=False)
                            self.inc("v", self.V.engine_nop())
                            self.inc("s", self.S.nop())

                    self.wait(self.G, self.tok("v"))
                    self.wait(self.SY, self.tok("s"))
                    self.wait(self.SY, self.tok("t"))
                    dtok = self.ld(bmc[0:16, :], dbc[32:48, hf * HALF:(hf + 1) * HALF])
                    if not pa:
                        dtok = self.ld(bmc[32:48, :],
                                       dbc[48:64, hf * HALF:(hf + 1) * HALF])
                    self.wait(self.Tn, dtok)
                    with ExitStack() as nes:
                        pwB = self.ps("pwB", [128, HALF], nes)
                        pwC = self.ps("pwC", [128, HALF], nes)
                        vdone = []
                        gdone = []
                        for n in range(16):
                            npar = n % 2
                            if n > 0:
                                self.wait(self.Tn, vdone[n * 8 - 1])
                            last = None
                            for tt in range(2):
                                last = self.Tn.matmul(
                                    pwB[:, tt * 512:(tt + 1) * 512],
                                    self.bsel[0:16, n * 128:(n + 1) * 128],
                                    bmc[0:16, tt * 512:(tt + 1) * 512],
                                    start=True, stop=True)
                            bt = self.inc("t", last)
                            if not pa:
                                self.wait(self.S, bt)
                                if n >= 1:
                                    self.wait(self.S, gdone[n * 8 - 1])
                                bev = self.inc("s", self.S.activation(
                                    Bmb[:, :], pwB[:, :], AF.Identity))
                            if not pa:
                                self.wait(self.Tn, bev)
                                last = None
                                for tt in range(2):
                                    last = self.Tn.matmul(
                                        pwC[:, tt * 512:(tt + 1) * 512],
                                        self.bsel[32:48, n * 128:(n + 1) * 128],
                                        bmc[32:48, tt * 512:(tt + 1) * 512],
                                        start=True, stop=True)
                                cev = self.inc("t", last)
                            for c in range(8):
                                idx = n * 8 + c
                                par = idx % 2
                                if idx >= 2:
                                    self.wait(self.S, vdone[idx - 2])
                                st_ = self.inc("s", self.S.activation(
                                    dAb[:, par * HALF:(par + 1) * HALF],
                                    delta[:, c * HALF:(c + 1) * HALF], AF.Exp,
                                    scale=self.A_sb[:, l * 128 + c * 16 + n:
                                                    l * 128 + c * 16 + n + 1]),
                                    commit=(idx < 2))
                                if not pa:
                                    if idx >= 2:
                                        self.wait(self.G, vdone[idx - 2])
                                    self.wait(self.G, bev)
                                    gt_ = self.inc("g", self.G.tensor_mul(
                                        dBxb[:, par * HALF:(par + 1) * HALF],
                                        u[:, c * HALF:(c + 1) * HALF],
                                        Bmb[:, :]), commit=(idx < 2))
                                    gdone.append(gt_)
                                self.wait(self.V, st_)
                                if not pa:
                                    self.wait(self.V, gt_)
                                else:
                                    self.wait(self.V, bt)
                                    self.V.tensor_mul(
                                        dBxb[:, par * HALF:(par + 1) * HALF],
                                        u[:, c * HALF:(c + 1) * HALF],
                                        pwB[:, :])
                                col = c * 16 + n
                                self.V.tensor_tensor_scan(
                                    hscr[:, :], dAb[:, par * HALF:(par + 1) * HALF],
                                    dBxb[:, par * HALF:(par + 1) * HALF],
                                    state[:, col:col + 1], ALU.mult, ALU.add)
                                self.V.tensor_copy(state[:, col:col + 1],
                                                   hscr[:, HALF - 1:HALF])
                                if not pa:
                                    self.wait(self.V, cev)
                                    self.V.tensor_mul(
                                        dBxb[:, par * HALF:(par + 1) * HALF],
                                        hscr[:, :], pwC[:, :])
                                    vt = self.inc("v", self.V.tensor_add(
                                        yh[:, c * HALF:(c + 1) * HALF],
                                        yh[:, c * HALF:(c + 1) * HALF],
                                        dBxb[:, par * HALF:(par + 1) * HALF]))
                                else:
                                    vt = self.inc("v", self.V.engine_nop())
                                vdone.append(vt)
                    self.wait(self.S, self.tok("v"))
                    self.bar("vs")
                    dues.close()

                    if pa and hf == 1:
                        self.vrt(self.V.engine_nop())
                        vt = None
                        for c in range(8):
                            vt = self.inc("v", self.V.tensor_mul(
                                self.pfull[:, c * 16:(c + 1) * 16],
                                self.cs[:, c:c + 1].to_broadcast([128, 16]),
                                self.A_sb[:, l * 128 + c * 16:l * 128 + (c + 1) * 16]))
                        self.wait(self.S, vt)
                        self.inc("s", self.S.activation(self.pfull[:, :],
                                                        self.pfull[:, :], AF.Exp))

                    if not pa:
                        with ExitStack() as fes:
                            ybf = self.sb("ybf_h", [128, 8 * HALF], BF16, stack=fes)
                            hseg = self.sb("hseg", [128, 4 * HALF], stack=fes)
                            szb = self.sb("szb", [128, 512], stack=fes)
                            htok = self.ld(hseg[:, :].rearrange(
                                "p (c x) -> p c x", c=4),
                                self.v_h[:, :, hf * HALF:(hf + 1) * HALF])
                            for c in range(8):
                                for tt in range(2):
                                    i = self.pmm_next()
                                    last = None
                                    for kc in range(4):
                                        last = self.Tn.matmul(
                                            self.pmm[i][:, :],
                                            inWT[:, kc * FF + (8 + c) * 128:
                                                 kc * FF + (9 + c) * 128],
                                            rmsh[:, kc * T + hf * HALF + tt * 512:
                                                 kc * T + hf * HALF + (tt + 1) * 512],
                                            start=(kc == 0), stop=(kc == 3))
                                    t_ = self.inc("t", last)
                                    self.wait(self.S, t_)
                                    st_ = self.inc("s", self.S.activation(
                                        szb[:, :], self.pmm[i][:, :], AF.Silu))
                                    self.pmm_free[i] = st_
                                    self.wait(self.V, st_)
                                    vt = self.inc("v", self.V.tensor_mul(
                                        yh[:, c * HALF + tt * 512:
                                           c * HALF + (tt + 1) * 512],
                                        yh[:, c * HALF + tt * 512:
                                           c * HALF + (tt + 1) * 512],
                                        szb[:, :]))
                                    self.wait(self.S, vt)
                                vt = self.tok("v")
                                self.wait(self.S, vt)
                                self.inc("s", self.S.activation(
                                    ybf[:, c * HALF:(c + 1) * HALF],
                                    yh[:, c * HALF:(c + 1) * HALF], AF.Identity))
                            st_ = self.tok("s")
                            self.wait(self.Tn, st_)
                            self.wait(self.V, htok)
                            for tt in range(2):
                                for m in range(4):
                                    i = self.pmm_next()
                                    last = None
                                    for c in range(8):
                                        last = self.Tn.matmul(
                                            self.pmm[i][:, :],
                                            moWT[:, c * 512 + m * 128:
                                                 c * 512 + (m + 1) * 128],
                                            ybf[:, c * HALF + tt * 512:
                                                c * HALF + (tt + 1) * 512],
                                            start=(c == 0), stop=(c == 7))
                                    t_ = self.inc("t", last)
                                    self.wait(self.V, t_)
                                    vt = self.inc("v", self.V.tensor_add(
                                        hseg[:, m * HALF + tt * 512:
                                             m * HALF + (tt + 1) * 512],
                                        hseg[:, m * HALF + tt * 512:
                                             m * HALF + (tt + 1) * 512],
                                        self.pmm[i][:, :]))
                                    self.pmm_free[i] = vt
                            vt = self.tok("v")
                            self.wait(self.S, vt)
                            htok = self.stdma(
                                self.v_h[:, :, hf * HALF:(hf + 1) * HALF],
                                hseg[:, :].rearrange("p (c x) -> p c x", c=4))
                            self.wait(self.SY, htok)
            self.bar()

    def relay(self, l):
        vt = self.tok("v")
        self.wait(self.G, vt)
        self.wait(self.G, self.tok("s"))
        g1 = self.gld(self.d_ag1i[:, 0:128], self.hstate[:, :])
        g2 = self.gld(self.d_ag1i[:, 128:256], self.pfull[:, :])
        self.wait(self.G, g1)
        self.wait(self.G, g2)
        ct = self.inc("cc", self.G.collective_compute(
            "AllGather", ALU.bypass,
            replica_groups=[list(range(NCORE))],
            ins=[self.d_ag1i.ap().opt()],
            outs=[self.d_ag1o.ap().opt()]), 1)
        self.wait(self.G, ct)
        with ExitStack() as es:
            gat = self.sb("relay_gat", [128, NCORE * 256], stack=es)
            tmp = self.sb("relay_tmp", [128, 128], stack=es)
            gt = self.gld(gat[:, :].rearrange("p (j c) -> p j c", j=NCORE),
                          self.d_ag1o.ap().rearrange("(j p) c -> p j c", p=128))
            self.wait(self.V, gt)
            self.V.memset(self.hin[:, :], 0.0)
            for j in range(NCORE):
                self.V.tensor_scalar(tmp[:, :],
                                     gat[:, j * 256 + 128:j * 256 + 256],
                                     1.0, self.col("mselp", j),
                                     ALU.subtract, ALU.mult)
                self.V.tensor_scalar(tmp[:, :], tmp[:, :], 1.0, None, ALU.add)
                self.V.tensor_mul(self.hin[:, :], self.hin[:, :], tmp[:, :])
                self.V.tensor_scalar(tmp[:, :], gat[:, j * 256:j * 256 + 128],
                                     self.col("mselp", j), None, ALU.mult)
                self.V.tensor_add(self.hin[:, :], self.hin[:, :], tmp[:, :])
            self.vrt(self.V.engine_nop())
            self.bar()

    def halo_relay(self):
        self.dma_quiesce(self.G, ("ds",))
        gt = self.gld(self.d_ag2i[:, :], self.v_h[:, :, T - 3:T])
        self.wait(self.G, gt)
        ct = self.inc("cc", self.G.collective_compute(
            "AllGather", ALU.bypass,
            replica_groups=[list(range(NCORE))],
            ins=[self.d_ag2i.ap().opt()],
            outs=[self.d_ag2o.ap().opt()]), 1)
        self.wait(self.G, ct)
        with ExitStack() as es:
            gat = self.sb("halo_gat", [128, NCORE * 12], stack=es)
            gt = self.gld(gat[:, :].rearrange("p (j c) -> p j c", j=NCORE),
                          self.d_ag2o.ap().rearrange("(j p) c -> p j c", p=128))
            self.wait(self.V, gt)
            self.V.memset(self.halo_h[:, :], 0.0)
            vt = None
            for j in range(NCORE):
                vt = self.inc("v", self.V.scalar_tensor_tensor(
                    self.halo_h[:, :], gat[:, j * 12:(j + 1) * 12],
                    self.col("msel", j), self.halo_h[:, :], ALU.mult, ALU.add))
            self.bar()

    def phase_po_ln0(self):
        self.xffn = self.sb("xffn_bf", [128, 8 * T], BF16)
        self.acc = self.sb("acc_tm", [128, 16 * 1024], F32)
        self.coeff = self.sb("coeff_tm", [128, 16 * NE], F32)
        self.b2sb = self.sb("b2_sb", [NE, 1024], F32)
        with ExitStack() as es:
            poWT = self.sb("poWT", [128, 8 * 1024], BF16, es)
            gateWT = self.sb("gateWT", [128, 8 * 14], F32, es)
            outbf = self.sb("outbf", [128, 8 * T], BF16, es)
            stage = self.sb("po_stage", [128, T], stack=es)
            gt = self.gld(poWT[:, :], self.d_poWT[:, :])
            gt2 = self.ld(gateWT[:, :], self.d_gateWT[:, :])
            with ExitStack() as es0:
                stage2 = self.sb("po_stage2", [128, T], stack=es0)
                for c in range(4):
                    dt1 = self.ld(stage[:, :], self.v_x1[:, c, :])
                    self.wait(self.S, dt1)
                    st_ = self.inc("s", self.S.activation(
                        outbf[:, c * T:(c + 1) * T], stage[:, :], AF.Identity))
                    self.wait(self.SY, st_)
                for c in range(4):
                    dt1 = self.ld(stage[:, :], self.v_x2[:, c, :])
                    dt2 = self.ld(stage2[:, :], self.v_h[:, c, :])
                    self.wait(self.V, dt2)
                    self.wait(self.V, dt1)
                    vt = self.inc("v", self.V.tensor_add(stage2[:, :], stage2[:, :],
                                                         stage[:, :]))
                    self.wait(self.S, vt)
                    st_ = self.inc("s", self.S.activation(
                        outbf[:, (4 + c) * T:(5 + c) * T], stage2[:, :], AF.Identity))
                    self.wait(self.SY, st_)
                self.bar("vs")
            bt = self.ld(self.b2sb[:, :], self.d_b2r[:, :])
            xab = self.sb("xa_b", [128, 8 * 512], stack=es)
            sqb = self.sb("po_sq", [128, 512], stack=es)
            xnf = self.sb("po_xnf", [128, 512], stack=es)
            srcs = self.sb("po_srcs", [128, D], stack=es)
            rows = self.sb("po_rows", [1, 3 * 512], stack=es)
            with ExitStack() as pses:
                ptr = self.ps("po_ptr", [128, 512], pses)
                pgate = self.ps("po_pgate", [128, 512], pses)
                pst = self.ps("po_pst", [1, 512], pses)
                pst2 = self.ps("po_pst2", [1, 512], pses)
                pbc = self.ps("po_pbc", [128, 512], pses)
                pbc2 = self.ps("po_pbc2", [128, 512], pses)
                self.wait(self.Tn, gt)
                self.wait(self.Tn, gt2)
                self.wait(self.Tn, self.tok("s"))
                for tt in range(4):
                    t0 = tt * 512
                    for m in range(8):
                        i = self.pmm_next()
                        last = None
                        for kc in range(8):
                            last = self.Tn.matmul(
                                self.pmm[i][:, :],
                                poWT[:, kc * 1024 + m * 128:kc * 1024 + (m + 1) * 128],
                                outbf[:, kc * T + t0:kc * T + t0 + 512],
                                start=(kc == 0), stop=(kc == 7))
                        t_ = self.inc("t", last)
                        vsrc = self.v_x1 if m < 4 else self.v_x2
                        dt1 = self.ld(stage[:, 0:512], vsrc[:, m % 4, t0:t0 + 512])
                        self.wait(self.S, t_)
                        st_ = self.inc("s", self.S.activation(
                            xab[:, m * 512:(m + 1) * 512], self.pmm[i][:, :],
                            AF.Identity, bias=self.col("po_b", m)))
                        self.pmm_free[i] = st_
                        self.wait(self.V, st_)
                        self.wait(self.V, dt1)
                        vt = self.inc("v", self.V.tensor_add(
                            xab[:, m * 512:(m + 1) * 512],
                            xab[:, m * 512:(m + 1) * 512], stage[:, 0:512]))
                        self.wait(self.SY, vt)
                    for b in range(4):
                        dt1 = self.ld(srcs[:, :],
                                      self.d_src[t0 + b * 128:t0 + (b + 1) * 128, :])
                        self.wait(self.Tn, dt1)
                        for m in range(8):
                            tt_ = self.inc("t", self.Tn.transpose(
                                ptr[:, b * 128:(b + 1) * 128],
                                srcs[:, m * 128:(m + 1) * 128], self.idt[:, :]))
                            self.wait(self.V, tt_)
                            vt = self.inc("v", self.V.tensor_add(
                                xab[:, m * 512 + b * 128:m * 512 + (b + 1) * 128],
                                xab[:, m * 512 + b * 128:m * 512 + (b + 1) * 128],
                                ptr[:, b * 128:(b + 1) * 128]))
                            self.wait(self.Tn, vt)
                        self.wait(self.SY, self.tok("v"))
                    vt = self.tok("v")
                    self.wait(self.Tn, vt)
                    self.wait(self.S, vt)
                    last = None
                    for m in range(8):
                        last = self.Tn.matmul(pst[:1, :], self.ones_c[:, :1],
                                              xab[:, m * 512:(m + 1) * 512],
                                              start=(m == 0), stop=(m == 7))
                    t_sum = self.inc("t", last)
                    sq_t = None
                    for m in range(8):
                        if sq_t is not None:
                            self.wait(self.S, self.tok("t"))
                        st_ = self.inc("s", self.S.activation(
                            sqb[:, :], xab[:, m * 512:(m + 1) * 512], AF.Square))
                        self.wait(self.Tn, st_)
                        sq_t = self.inc("t", self.Tn.matmul(
                            pst2[:1, :], self.ones_c[:, :1], sqb[:, :],
                            start=(m == 0), stop=(m == 7)))
                    self.wait(self.V, t_sum)
                    self.wait(self.V, sq_t)
                    self.vrt(self.V.tensor_single_scalar(
                        rows[:1, 0:512], pst[:1, :], 1.0 / D, ALU.mult))
                    self.vrt(self.V.tensor_single_scalar(
                        rows[:1, 512:1024], pst2[:1, :], 1.0 / D, ALU.mult))
                    self.vrt(self.V.tensor_mul(rows[:1, 1024:1536],
                                               rows[:1, 0:512], rows[:1, 0:512]))
                    vt = self.vrt(self.V.tensor_sub(rows[:1, 512:1024],
                                                    rows[:1, 512:1024],
                                                    rows[:1, 1024:1536]))
                    self.wait(self.S, vt)
                    st_ = self.inc("s", self.S.activation(
                        rows[:1, 1024:1536], rows[:1, 512:1024], AF.Sqrt,
                        bias=self.epsc[0:1, :1]))
                    self.wait(self.V, st_)
                    vt = self.vrt(self.V.reciprocal(rows[:1, 1024:1536],
                                                    rows[:1, 1024:1536]))
                    self.wait(self.Tn, vt)
                    bt1 = self.inc("t", self.Tn.matmul(
                        pbc[:, :], self.ones_r[:1, :], rows[:1, 0:512],
                        start=True, stop=True))
                    bt2 = self.inc("t", self.Tn.matmul(
                        pbc2[:, :], self.ones_r[:1, :], rows[:1, 1024:1536],
                        start=True, stop=True))
                    self.wait(self.V, bt2)
                    for m in range(8):
                        self.V.tensor_sub(xnf[:, :], xab[:, m * 512:(m + 1) * 512],
                                          pbc[:, :])
                        vt = self.inc("v", self.V.tensor_mul(xnf[:, :], xnf[:, :],
                                                             pbc2[:, :]))
                        self.wait(self.S, vt)
                        self.inc("s", self.S.activation(
                            self.xffn[:, m * T + t0:m * T + t0 + 512],
                            xnf[:, :], AF.Identity))
                        self.wait(self.Tn, vt)
                        glast = self.Tn.matmul(
                            pgate[0:14, 0:512], gateWT[:, m * 14:(m + 1) * 14],
                            xnf[:, :], start=(m == 0), stop=(m == 7))
                        if m == 7:
                            self.inc("t", glast)
                        for b in range(4):
                            tt_ = self.inc("t", self.Tn.transpose(
                                ptr[:, b * 128:(b + 1) * 128],
                                xnf[:, b * 128:(b + 1) * 128], self.idt[:, :]))
                            self.wait(self.V, tt_)
                            vt2 = self.inc("v", self.V.tensor_copy(
                                self.acc[:, (tt * 4 + b) * 1024 + m * 128:
                                         (tt * 4 + b) * 1024 + (m + 1) * 128],
                                ptr[:, b * 128:(b + 1) * 128]))
                            self.wait(self.Tn, vt2)
                    vt = self.tok("v")
                    self.wait(self.S, vt)
                    for b in range(4):
                        self.stdma(self.d_xn[t0 + b * 128:t0 + (b + 1) * 128, :],
                                   self.acc[:, (tt * 4 + b) * 1024:
                                            (tt * 4 + b) * 1024 + 1024])
                    self.gating_tile(tt, stage, pgate, pbc2)
            self.bar()

    def gating_tile(self, tt, lgsb, pgf, ptf):
        t0 = tt * 512
        pg = pgf[0:14, 0:512]
        pt2 = ptf[0:128, 0:14]
        with ExitStack() as es:
            ltm = self.sb("gt_ltm", [128, 14], stack=es)
            prb = self.sb("gt_prb", [128, 14], stack=es)
            m8 = self.sb("gt_m8", [128, 8], stack=es)
            nmx = self.sb("gt_nmx", [128, 1], stack=es)
            den = self.sb("gt_den", [128, 1], stack=es)
            msk = self.sb("gt_msk", [128, 14], stack=es)
            self.wait(self.S, self.tok("t"))
            st_ = self.inc("s", self.S.activation(lgsb[0:14, 0:512], pg,
                                                  AF.Identity))
            self.wait(self.Tn, st_)
            for b in range(4):
                tt_ = self.inc("t", self.Tn.transpose(
                    pt2, lgsb[0:14, b * 128:(b + 1) * 128], self.idt[0:14, 0:14]))
                self.wait(self.V, tt_)
                self.vrt(self.V.tensor_copy(ltm[:, :], pt2))
                self.wait(self.Tn, self.tok("v"))
                self.vrt(self.V.reduce_max(nmx[:, :1], ltm[:, :], axis=AX,
                                           negate=True))
                vt = self.tok("v")
                self.wait(self.S, vt)
                st_ = self.inc("s", self.S.activation(prb[:, :], ltm[:, :], AF.Exp,
                                                      bias=nmx[:, :1]))
                self.wait(self.V, st_)
                self.vrt(self.V.max(out=m8[:, :], in_=prb[:, :]))
                self.vrt(self.V.tensor_add(den[:, :1], m8[:, 0:1], m8[:, 1:2]))
                self.vrt(self.V.reciprocal(den[:, :1], den[:, :1]))
                self.vrt(self.V.tensor_scalar(msk[:, :], prb[:, :], m8[:, 1:2],
                                              None, ALU.is_ge))
                self.V.tensor_mul(msk[:, :], msk[:, :], prb[:, :])
                vt = self.vrt(self.V.tensor_scalar(
                    self.coeff[:, (tt * 4 + b) * NE:(tt * 4 + b + 1) * NE],
                    msk[:, :], den[:, :1], None, ALU.mult))
            self.wait(self.SY, self.tok("t"))

    def phase_ffn(self):
        self.dma_quiesce(self.V, ("ds",))
        with ExitStack() as es:
            cfm_es = ExitStack()
            cfm = self.sb("coeff_fm", [14, 16 * 128], stack=cfm_es)
            with ExitStack() as es2:
                pt2 = self.ps("ffn_pt", [14, 128], es2)
                vt = self.tok("v")
                self.wait(self.Tn, vt)
                for tm in range(16):
                    tt_ = self.inc("t", self.Tn.transpose(
                        pt2[:, :], self.coeff[:, tm * NE:(tm + 1) * NE],
                        self.idt[:, :]))
                    self.wait(self.S, tt_)
                    st_ = self.inc("s", self.S.activation(
                        cfm[0:14, tm * 128:(tm + 1) * 128], pt2[0:14, :],
                        AF.Identity))
                    self.wait(self.Tn, st_)
                pb2 = self.ps("ffn_pb2", [128, 512], es2)
                for tm in range(16):
                    for dn in range(2):
                        t_ = self.inc("t", self.Tn.matmul(
                            pb2[:, :], cfm[0:14, tm * 128:(tm + 1) * 128],
                            self.b2sb[0:14, dn * 512:(dn + 1) * 512],
                            start=True, stop=True))
                        self.wait(self.V, t_)
                        vt = self.inc("v", self.V.tensor_add(
                            self.acc[:, tm * 1024 + dn * 512:
                                     tm * 1024 + (dn + 1) * 512],
                            self.acc[:, tm * 1024 + dn * 512:
                                     tm * 1024 + (dn + 1) * 512],
                            pb2[:, :]))
                        self.wait(self.Tn, vt)
            self.bar("vst")
            cfm_es.close()
            wsl = [self.sb(f"wsl{j}", [128, 16384], BF16, es) for j in range(2)]
            h1 = self.sb("ffn_h1", [128, 16 * 1024], BF16, es)
            pW = [self.ps(f"ffn_pw{j}", [128, 512], es) for j in range(2)]
            wtoks = [self.gld(wsl[0][:, :], self.d_w1[0][:, :]),
                     self.gld(wsl[1][:, :], self.d_w2[0][:, :])]
            w1_use = None
            w2_use = None
            for e in range(NE):
                self.wait(self.Tn, wtoks[0])
                for sp in range(2):
                    for fm in range(16):
                        i = self.pmm_next()
                        if self.pmm_free[i ^ 1] is not None:
                            pass
                        last = None
                        for kc in range(8):
                            lhs = wsl[0][:, kc * FF + fm * 128:kc * FF + (fm + 1) * 128]
                            self.Tn.matmul(
                                self.pmm[i][:, :], lhs,
                                self.xffn[:, kc * T + sp * 1024:kc * T + sp * 1024 + 512],
                                start=(kc == 0), stop=(kc == 7))
                            last = self.Tn.matmul(
                                pW[i][:, :], lhs,
                                self.xffn[:, kc * T + sp * 1024 + 512:
                                          kc * T + sp * 1024 + 1024],
                                start=(kc == 0), stop=(kc == 7))
                        t_ = self.inc("t", last)
                        self.wait(self.S, t_)
                        self.S.activation(
                            h1[:, fm * 1024:fm * 1024 + 512], self.pmm[i][:, :],
                            AF.Gelu, bias=self.col("b1", e * 16 + fm))
                        st_ = self.inc("s", self.S.activation(
                            h1[:, fm * 1024 + 512:(fm + 1) * 1024], pW[i][:, :],
                            AF.Gelu, bias=self.col("b1", e * 16 + fm)))
                        self.pmm_free[i] = st_
                    if sp == 1:
                        w1_use = self.tok("t")
                        if e + 1 < NE:
                            self.wait(self.G, w1_use)
                            wtoks[0] = self.gld(wsl[0][:, :], self.d_w1[e + 1][:, :])
                    st_ = self.tok("s")
                    self.wait(self.Tn, st_)
                    self.wait(self.Tn, wtoks[1])
                    w2_free = [None, None]
                    w2_banks = [self.pmm[0], self.pmm[1], pW[0], pW[1]]
                    for tm4 in range(8):
                        tm = sp * 8 + tm4
                        b0 = w2_banks[(tm4 % 2) * 2]
                        b1 = w2_banks[(tm4 % 2) * 2 + 1]
                        if tm4 >= 2:
                            self.wait(self.Tn, w2_free[tm4 % 2])
                        else:
                            if self.pmm_free[0] is not None:
                                self.wait(self.Tn, self.pmm_free[0])
                            if self.pmm_free[1] is not None:
                                self.wait(self.Tn, self.pmm_free[1])
                        last = None
                        for kf in range(16):
                            lhs = h1[:, kf * 1024 + tm4 * 128:kf * 1024 + (tm4 + 1) * 128]
                            self.Tn.matmul(
                                b0[:, :], lhs,
                                wsl[1][:, kf * 1024:kf * 1024 + 512],
                                start=(kf == 0), stop=(kf == 15))
                            last = self.Tn.matmul(
                                b1[:, :], lhs,
                                wsl[1][:, kf * 1024 + 512:(kf + 1) * 1024],
                                start=(kf == 0), stop=(kf == 15))
                        t_ = self.inc("t", last)
                        self.wait(self.V, t_)
                        self.V.scalar_tensor_tensor(
                            self.acc[:, tm * 1024:tm * 1024 + 512], b0[:, :],
                            self.coeff[:, tm * NE + e:tm * NE + e + 1],
                            self.acc[:, tm * 1024:tm * 1024 + 512],
                            ALU.mult, ALU.add)
                        vt = self.inc("v", self.V.scalar_tensor_tensor(
                            self.acc[:, tm * 1024 + 512:(tm + 1) * 1024], b1[:, :],
                            self.coeff[:, tm * NE + e:tm * NE + e + 1],
                            self.acc[:, tm * 1024 + 512:(tm + 1) * 1024],
                            ALU.mult, ALU.add))
                        w2_free[tm4 % 2] = vt
                    self.pmm_free = [vt, vt]
                    self.pmm_i = 0
                w2_use = self.tok("t")
                if e + 1 < NE:
                    self.wait(self.G, w2_use)
                    wtoks[1] = self.gld(wsl[1][:, :], self.d_w2[e + 1][:, :])
            self.bar()

    def phase_lnf(self):
        with ExitStack() as es:
            xna = self.sb("lnf_xna", [128, NTOK * 1024], stack=es)
            mu = self.sb("lnf_mu", [128, NTOK], stack=es)
            var = self.sb("lnf_var", [128, NTOK], stack=es)
            mu2 = self.sb("lnf_mu2", [128, NTOK], stack=es)
            rstd = self.sb("lnf_rstd", [128, NTOK], stack=es)
            ot = [self.sb(f"lnf_out{j}", [128, 1024], stack=es) for j in range(2)]
            dts = []
            for i in range(NTOK):
                dts.append(self.ld(xna[:, i * 1024:(i + 1) * 1024],
                                   self.d_xn[i * 128:(i + 1) * 128, :]))
            a = self.acc[:, :]
            self.V.tensor_scalar(a, a, -100.0, 100.0, ALU.max, ALU.min)
            for t_ in dts:
                self.wait(self.V, t_)
            vt = self.inc("v", self.V.tensor_add(xna[:, :], xna[:, :], a))
            self.wait(self.S, vt)
            self.vrt(self.V.tensor_reduce(
                mu[:, :], xna[:, :].rearrange("p (i d) -> p i d", d=1024), AX,
                ALU.add))
            self.vrt(self.V.tensor_single_scalar(mu[:, :], mu[:, :], 1.0 / D,
                                                 ALU.mult))
            st_ = self.inc("s", self.S.activation(self.acc[:, :], xna[:, :],
                                                  AF.Square))
            self.wait(self.V, st_)
            self.vrt(self.V.tensor_reduce(
                var[:, :], self.acc[:, :].rearrange("p (i d) -> p i d", d=1024),
                AX, ALU.add))
            self.vrt(self.V.tensor_single_scalar(var[:, :], var[:, :], 1.0 / D,
                                                 ALU.mult))
            self.vrt(self.V.tensor_mul(mu2[:, :], mu[:, :], mu[:, :]))
            vt = self.vrt(self.V.tensor_sub(var[:, :], var[:, :], mu2[:, :]))
            self.wait(self.S, vt)
            st_ = self.inc("s", self.S.activation(rstd[:, :], var[:, :], AF.Sqrt,
                                                  bias=self.epsc[:, :1]))
            self.wait(self.V, st_)
            self.vrt(self.V.reciprocal(rstd[:, :], rstd[:, :]))
            ot_free = [None, None]
            for tm in range(NTOK):
                par = tm % 2
                if ot_free[par] is not None:
                    self.wait(self.V, ot_free[par])
                vt = self.inc("v", self.V.tensor_scalar(
                    ot[par][:, :], xna[:, tm * 1024:(tm + 1) * 1024],
                    mu[:, tm:tm + 1], rstd[:, tm:tm + 1],
                    ALU.subtract, ALU.mult))
                self.wait(self.S, vt)
                ot_free[par] = self.stdma(self.d_out[tm * 128:(tm + 1) * 128, :],
                                          ot[par][:, :])


_BUILT = None


def _get_built():
    global _BUILT
    if _BUILT is None:
        kb = KB()
        kb.build()
        _BUILT = kb
    return _BUILT


def _pack_shared(I):
    f32 = lambda a: np.ascontiguousarray(np.asarray(a, dtype=np.float32))
    out = {}
    out["identity"] = np.eye(128, dtype=np.float32)
    bsel = np.zeros((48, NS * 128), np.float32)
    for n in range(NS):
        bsel[n, n * 128:(n + 1) * 128] = 1.0
        bsel[32 + n, n * 128:(n + 1) * 128] = 1.0
    out["bsel"] = bsel
    for l in range(NL):
        out[f"inWT_p{l}"] = f32(np.asarray(I["L0_attn_in_W"])[l].T
                                .reshape(4, 128, FF).transpose(1, 0, 2)
                                .reshape(128, 4 * FF))
        out[f"moWT_p{l}"] = f32(np.asarray(I["L0_attn_mo_W"])[l].T
                                .reshape(8, 128, 512).transpose(1, 0, 2)
                                .reshape(128, 8 * 512))
        out[f"xpWT_p{l}"] = f32(np.asarray(I["L0_attn_xp_W"])[l].T
                                .reshape(8, 128, 64).transpose(1, 0, 2)
                                .reshape(128, 8 * 64))
        out[f"dtWT_p{l}"] = f32(np.asarray(I["L0_attn_dt_W"])[l].T)
    out["poWT_p"] = f32(np.asarray(I["L0_attn_po_W"]).T
                        .reshape(8, 128, 1024).transpose(1, 0, 2)
                        .reshape(128, 8 * 1024))
    out["gateWT_p"] = f32(np.asarray(I["L0_ffn_gate_w"]).T
                          .reshape(8, 128, NE).transpose(1, 0, 2)
                          .reshape(128, 8 * NE))
    w1 = np.asarray(I["L0_ffn_w1"])
    w2 = np.asarray(I["L0_ffn_w2"])
    for e in range(NE):
        out[f"w1_p{e}"] = f32(w1[e].reshape(8, 128, FF).transpose(1, 0, 2)
                              .reshape(128, 8 * FF))
        out[f"w2_p{e}"] = f32(w2[e].reshape(16, 128, 1024).transpose(1, 0, 2)
                              .reshape(128, 16 * 1024))
    out["b2_rows"] = f32(I["L0_ffn_b2"])
    return out


def _pack_pcols_base(I):
    pc = np.zeros((128, PCN), dtype=np.float32)
    def put(name, arr):
        a = np.asarray(arr, dtype=np.float32)
        pc[:, _PC[name]:_PC[name] + a.shape[1]] = a
    put("ln_g", np.asarray(I["L0_attn_ln_g"]).reshape(8, 128).T)
    put("ln_b", np.asarray(I["L0_attn_ln_b"]).reshape(8, 128).T)
    alog = np.asarray(I["L0_attn_A_log"])
    put("alog", alog.reshape(NL, 8, 128, NS).transpose(2, 0, 1, 3)
        .reshape(128, NL * 128))
    put("Dp", np.asarray(I["L0_attn_Dp"]).reshape(NL, 8, 128)
        .transpose(2, 0, 1).reshape(128, NL * 8))
    put("cb", np.asarray(I["L0_attn_conv_b"]).reshape(NL, 8, 128)
        .transpose(2, 0, 1).reshape(128, NL * 8))
    cw = np.asarray(I["L0_attn_conv_w"]).reshape(NL, 8, 128, 4)
    put("cw", cw.transpose(2, 0, 3, 1).reshape(128, NL * 32))
    put("dtb", np.asarray(I["L0_attn_dt_b"]).reshape(NL, 8, 128)
        .transpose(2, 0, 1).reshape(128, NL * 8))
    put("rmsw", np.asarray(I["L0_attn_rms_w"]).reshape(NL, 4, 128)
        .transpose(2, 0, 1).reshape(128, NL * 4))
    put("po_b", np.asarray(I["L0_attn_po_b"]).reshape(8, 128).T)
    put("b1", np.asarray(I["L0_ffn_b1"]).reshape(NE, 16, 128)
        .transpose(2, 0, 1).reshape(128, NE * 16))
    return pc


def kernel(**inputs):
    kb = _get_built()
    shared = _pack_shared(inputs)
    pc_base = _pack_pcols_base(inputs)
    src = np.ascontiguousarray(np.asarray(inputs["L0_attn_src"], np.float32))
    in_maps = []
    for k in range(NCORE):
        seq, t0 = divmod(k * T, L)
        pc = pc_base.copy()
        if k % 4 != 0:
            pc[:, _PC["msel"] + (k - 1)] = 1.0
            pc[:, _PC["hmask"]] = 1.0
        for j in range(NCORE):
            if j // 4 == k // 4 and j < k:
                pc[:, _PC["mselp"] + j] = 1.0
        halo = np.zeros((4, D), np.float32)
        if k % 4 != 0:
            halo[0:3] = src[seq, t0 - 3:t0]
        m = {"src_tm": np.ascontiguousarray(src[seq, t0:t0 + T]),
             "halo_src": halo,
             "pcols": np.ascontiguousarray(pc)}
        m.update(shared)
        in_maps.append(m)
    res = run_bass_kernel_spmd(kb.nc, in_maps, core_ids=list(range(NCORE)))
    out = np.empty((BATCH, L, D), np.float32)
    for k in range(NCORE):
        seq, t0 = divmod(k * T, L)
        out[seq, t0:t0 + T] = res.results[k]["out_tm"]
    return out


# version 10
# speedup vs baseline: 1.0749x; 1.0328x over previous
import numpy as np
from contextlib import ExitStack

import concourse.bass as bass
import concourse.mybir as mybir
from concourse.bass_utils import run_bass_kernel_spmd

F32 = mybir.dt.float32
BF16 = mybir.dt.bfloat16
AF = mybir.ActivationFunctionType
ALU = mybir.AluOpType
AX = mybir.AxisListType.X

NCORE = 8
BATCH, L, D = 2, 8192, 1024
DC, DI, NS, DTR, NL = 512, 1024, 16, 32, 2
NE, FF = 14, 2048
T = (BATCH * L) // NCORE
HALF = 1024
NTOK = T // 128
EPS = 1e-5

_PC = {}
_pc_off = 0
def _pc(name, n):
    global _pc_off
    _PC[name] = _pc_off
    _pc_off += n
_pc("ln_g", 8); _pc("ln_b", 8)
_pc("alog", NL * 128)
_pc("Dp", NL * 8)
_pc("cb", NL * 8)
_pc("cw", NL * 32)
_pc("dtb", NL * 8)
_pc("rmsw", NL * 4)
_pc("po_b", 8)
_pc("msel", 8)
_pc("mselp", 8)
_pc("hmask", 1)
_pc("b1", NE * 16)
PCN = _pc_off


class KB:
    DEBUG = False

    def __init__(self):
        self.nc = bass.Bass("TRN2", target_bir_lowering=False,
                            detect_race_conditions=False)
        self.stk = ExitStack()
        self.cnt = {}
        self.sems = {}
        self.V = self.nc.vector
        self.S = self.nc.scalar
        self.Tn = self.nc.tensor
        self.G = self.nc.gpsimd
        self.SY = self.nc.sync
        self.E = {"v": self.V, "s": self.S, "t": self.Tn, "g": self.G,
                  "y": self.SY}
        for name in ["v", "s", "t", "g", "cc"]:
            self.sems[name] = self.stk.enter_context(self.nc.semaphore(name))
            self.cnt[name] = 0
        self.pools = {}
        for pname, eng, nslot in [("dl", self.SY, 6), ("ds", self.S, 4),
                                  ("dg", self.G, 4)]:
            slots = []
            for j in range(nslot):
                sn = f"{pname}{j}"
                self.sems[sn] = self.stk.enter_context(self.nc.semaphore(sn))
                self.cnt[sn] = 0
                slots.append(sn)
            self.pools[pname] = {"eng": eng, "slots": slots, "i": 0}

    def _dma(self, pname, out_ap, in_ap):
        p = self.pools[pname]
        sn = p["slots"][p["i"] % len(p["slots"])]
        p["i"] += 1
        if self.cnt[sn] > 0:
            p["eng"].wait_ge(self.sems[sn], self.cnt[sn])
        instr = p["eng"].dma_start(out=out_ap, in_=in_ap)
        instr.then_inc(self.sems[sn], 16)
        self.cnt[sn] += 16
        return (sn, self.cnt[sn])

    def dma_quiesce(self, eng, pnames=("dl", "ds", "dg")):
        for pname in pnames:
            for sn in self.pools[pname]["slots"]:
                if self.cnt[sn] > 0:
                    eng.wait_ge(self.sems[sn], self.cnt[sn])

    _ENG_OF_SEM = {"v": "v", "s": "s", "t": "t", "g": "g"}

    def inc(self, name, instr, amt=1, commit=True):
        instr.then_inc(self.sems[name], amt)
        self.cnt[name] += amt
        if commit and name in self._ENG_OF_SEM:
            self.E[name].wait_ge(self.sems[name], self.cnt[name])
        return (name, self.cnt[name])

    def tok(self, name):
        return (name, self.cnt[name])

    def wait(self, eng, token):
        name, v = token
        if v > 0:
            eng.wait_ge(self.sems[name], v)

    def vrt(self, instr):
        t = self.inc("v", instr)
        self.wait(self.V, t)
        return t

    def bar(self, engines="vstg"):
        toks = []
        for ch in engines:
            toks.append(self.inc(ch, self.E[ch].nop()))
        for ch in engines:
            for t in toks:
                if t[0] != ch:
                    self.wait(self.E[ch], t)
        for t in toks:
            self.wait(self.SY, t)
            if "g" not in engines:
                self.wait(self.G, t)
        self.dma_quiesce(self.SY, ("ds", "dg"))
        self.dma_quiesce(self.G, ("ds", "dl"))

    def ld(self, out_ap, in_ap):
        return self._dma("dl", out_ap, in_ap)

    def stdma(self, out_ap, in_ap):
        return self._dma("ds", out_ap, in_ap)

    def gld(self, out_ap, in_ap):
        return self._dma("dg", out_ap, in_ap)

    _uid = 0

    def _u(self, name):
        KB._uid += 1
        return f"{name}_{KB._uid}"

    def sb(self, name, shape, dtype=F32, stack=None):
        return (stack or self.stk).enter_context(
            self.nc.sbuf_tensor(self._u(name), shape, dtype))

    def ps(self, name, shape, stack=None):
        return (stack or self.stk).enter_context(
            self.nc.psum_tensor(self._u(name), shape, F32))

    def col(self, name, idx=0):
        c = _PC[name] + idx
        return self.pcols[:, c:c + 1]

    def pmm_init(self):
        self.pmm = [self.ps("pmm0", [128, 512]), self.ps("pmm1", [128, 512])]
        self.pmm_free = [None, None]
        self.pmm_i = 0

    def pmm_next(self):
        i = self.pmm_i
        self.pmm_i ^= 1
        if self.pmm_free[i] is not None:
            self.wait(self.Tn, self.pmm_free[i])
        return i

    def build(self):
        nc = self.nc
        self.d_src = nc.dram_tensor("src_tm", [T, D], F32, kind="ExternalInput")
        self.d_halo = nc.dram_tensor("halo_src", [4, D], F32, kind="ExternalInput")
        self.d_pcols = nc.dram_tensor("pcols", [128, PCN], F32, kind="ExternalInput")
        self.d_idt = nc.dram_tensor("identity", [128, 128], F32, kind="ExternalInput")
        self.d_bsel = nc.dram_tensor("bsel", [48, NS * 128], F32, kind="ExternalInput")
        self.d_inWT = [nc.dram_tensor(f"inWT_p{l}", [128, 4 * FF], F32, kind="ExternalInput")
                       for l in range(NL)]
        self.d_moWT = [nc.dram_tensor(f"moWT_p{l}", [128, 8 * 512], F32, kind="ExternalInput")
                       for l in range(NL)]
        self.d_xpWT = [nc.dram_tensor(f"xpWT_p{l}", [128, 8 * 64], F32, kind="ExternalInput")
                       for l in range(NL)]
        self.d_dtWT = [nc.dram_tensor(f"dtWT_p{l}", [32, 1024], F32, kind="ExternalInput")
                       for l in range(NL)]
        self.d_poWT = nc.dram_tensor("poWT_p", [128, 8 * 1024], F32, kind="ExternalInput")
        self.d_gateWT = nc.dram_tensor("gateWT_p", [128, 8 * 14], F32, kind="ExternalInput")
        self.d_w1 = [nc.dram_tensor(f"w1_p{e}", [128, 8 * FF], F32, kind="ExternalInput")
                     for e in range(NE)]
        self.d_w2 = [nc.dram_tensor(f"w2_p{e}", [128, 16 * 1024], F32, kind="ExternalInput")
                     for e in range(NE)]
        self.d_b2r = nc.dram_tensor("b2_rows", [NE, 1024], F32, kind="ExternalInput")
        self.d_out = nc.dram_tensor("out_tm", [T, D], F32, kind="ExternalOutput")
        if KB.DEBUG:
            self.d_dbg1 = nc.dram_tensor("dbg1", [128, 264], F32, kind="ExternalOutput")
            self.d_dbg2 = nc.dram_tensor("dbg2", [128, 128], F32, kind="ExternalOutput")
            self.d_dbg3 = nc.dram_tensor("dbg3", [512, T], F32, kind="ExternalOutput")
            self.d_dbg4 = nc.dram_tensor("dbg4", [512, T], F32, kind="ExternalOutput")
            self.d_dbg5 = nc.dram_tensor("dbg5", [128, 2 * 8192], F32, kind="ExternalOutput")
            self.d_dbg6 = nc.dram_tensor("dbg6", [128, 272], F32, kind="ExternalOutput")
            self.d_dbg7 = nc.dram_tensor("dbg7", [128, 144], F32, kind="ExternalOutput")
            self.d_dbg8 = nc.dram_tensor("dbg8", [128, 8192 + 2048 + 2048], F32, kind="ExternalOutput")
        self.d_x1 = nc.dram_tensor("x1_d", [512, T], F32, kind="Internal")
        self.d_x2 = nc.dram_tensor("x2_d", [512, T], F32, kind="Internal")
        self.d_h = nc.dram_tensor("h_d", [512, T], F32, kind="Internal")
        self.d_xn = nc.dram_tensor("xn_d", [T, D], F32, kind="Internal")
        self.d_ag1i = nc.dram_tensor("ag1_in", [128, 256], F32, kind="Internal")
        self.d_ag1o = nc.dram_tensor("ag1_out", [NCORE * 128, 256], F32, kind="Internal")
        self.d_ag2i = nc.dram_tensor("ag2_in", [128, 12], F32, kind="Internal")
        self.d_ag2o = nc.dram_tensor("ag2_out", [NCORE * 128, 12], F32, kind="Internal")

        self.v_x1 = self.d_x1.ap().rearrange("(c p) t -> p c t", p=128)
        self.v_x2 = self.d_x2.ap().rearrange("(c p) t -> p c t", p=128)
        self.v_h = self.d_h.ap().rearrange("(c p) t -> p c t", p=128)

        self.pcols = self.sb("pcols_sb", [128, PCN])
        self.idt = self.sb("idt_sb", [128, 128])
        self.ones_c = self.sb("ones_c", [128, 1])
        self.ones_r = self.sb("ones_r", [1, 128])
        self.epsc = self.sb("epsc", [128, 1])
        self.A_sb = self.sb("A_sb", [128, NL * 128])
        self.halo_x1 = self.sb("halo_x1", [128, 12])
        self.halo_h = self.sb("halo_h", [128, 12])
        self.hstate = self.sb("hstate", [128, 128])
        self.pfull = self.sb("pfull", [128, 128])
        self.hin = self.sb("hin", [128, 128])
        self.cs = self.sb("cs_tot", [128, 8])

        self.pmm_init()

        t1 = self.ld(self.pcols[:, :], self.d_pcols[:, :])
        t2 = self.ld(self.idt[:, :], self.d_idt[:, :])
        for e in "vstg":
            self.wait(self.E[e], t1)
            self.wait(self.E[e], t2)
        self.V.memset(self.ones_c[:, :], 1.0)
        self.V.memset(self.ones_r[:, :], 1.0)
        self.V.memset(self.epsc[:, :], EPS)
        stok = self.inc("s", self.S.activation(
            self.A_sb[:, :],
            self.pcols[:, _PC["alog"]:_PC["alog"] + NL * 128], AF.Exp))
        self.wait(self.V, stok)
        vt = self.inc("v", self.V.tensor_single_scalar(
            self.A_sb[:, :], self.A_sb[:, :], -1.0, ALU.mult))
        self.wait(self.S, vt)
        self.bar()

        self.phase_ln1()
        for l in range(NL):
            self.mamba_layer(l)
        self.phase_po_ln0()
        self.phase_ffn()
        self.phase_lnf()
        self.dma_quiesce(self.SY)
        return nc

    def phase_ln1(self):
        with ExitStack() as es:
            sxa = self.sb("ln_sxa", [128, NTOK * D], stack=es)
            sqa = self.sb("ln_sqa", [128, NTOK * D], stack=es)
            mu = self.sb("ln_mu", [128, NTOK], stack=es)
            var = self.sb("ln_var", [128, NTOK], stack=es)
            mu2 = self.sb("ln_mu2", [128, NTOK], stack=es)
            rstd = self.sb("ln_rstd", [128, NTOK], stack=es)
            stg = [self.sb(f"ln_st{j}", [128, 8 * 128], stack=es) for j in range(2)]
            ptr = [self.ps(f"ln_ptr{j}", [128, 128], stack=es) for j in range(4)]
            dts = []
            for i in range(NTOK):
                dts.append(self.ld(sxa[:, i * D:(i + 1) * D],
                                   self.d_src[i * 128:(i + 1) * 128, :]))
            for t_ in dts:
                self.wait(self.V, t_)
                self.wait(self.S, t_)
            self.vrt(self.V.tensor_reduce(
                mu[:, :], sxa[:, :].rearrange("p (i d) -> p i d", d=D), AX, ALU.add))
            self.vrt(self.V.tensor_single_scalar(mu[:, :], mu[:, :], 1.0 / D, ALU.mult))
            st_ = self.inc("s", self.S.activation(sqa[:, :], sxa[:, :], AF.Square))
            self.wait(self.V, st_)
            self.vrt(self.V.tensor_reduce(
                var[:, :], sqa[:, :].rearrange("p (i d) -> p i d", d=D), AX, ALU.add))
            self.vrt(self.V.tensor_single_scalar(var[:, :], var[:, :], 1.0 / D, ALU.mult))
            self.vrt(self.V.tensor_mul(mu2[:, :], mu[:, :], mu[:, :]))
            vt = self.vrt(self.V.tensor_sub(var[:, :], var[:, :], mu2[:, :]))
            self.wait(self.S, vt)
            st_ = self.inc("s", self.S.activation(rstd[:, :], var[:, :], AF.Sqrt,
                                                  bias=self.epsc[:, :1]))
            self.wait(self.V, st_)
            self.vrt(self.V.reciprocal(rstd[:, :], rstd[:, :]))
            for i in range(NTOK):
                self.V.tensor_scalar(sqa[:, i * D:(i + 1) * D],
                                     sxa[:, i * D:(i + 1) * D],
                                     mu[:, i:i + 1], rstd[:, i:i + 1],
                                     ALU.subtract, ALU.mult)
            vt = self.vrt(self.V.engine_nop())
            self.wait(self.Tn, vt)
            ptr_free = [None, None, None, None]
            stg_free = [None, None]
            for i in range(NTOK):
                par = i % 2
                if stg_free[par] is not None:
                    self.wait(self.S, stg_free[par])
                ev_last = None
                for c in range(8):
                    p2 = c % 4
                    if ptr_free[p2] is not None:
                        self.wait(self.Tn, ptr_free[p2])
                    tt_ = self.inc("t", self.Tn.transpose(
                        ptr[p2][:, :], sqa[:, i * D + c * 128:i * D + (c + 1) * 128],
                        self.idt[:, :]), commit=False)
                    self.wait(self.S, tt_)
                    ev_last = self.inc("s", self.S.activation(
                        stg[par][:, c * 128:(c + 1) * 128], ptr[p2][:, :],
                        AF.Identity, bias=self.col("ln_b", c),
                        scale=self.col("ln_g", c)), commit=False)
                    ptr_free[p2] = ev_last
                ts0 = i * 128
                self.inc("s", self.S.nop())
                self.wait(self.S, self.tok("s"))
                self.stdma(self.v_x1[:, :, ts0:ts0 + 128], stg[par][:, 0:512])
                self.stdma(self.v_x2[:, :, ts0:ts0 + 128], stg[par][:, 512:1024])
                stg_free[par] = self.stdma(self.v_h[:, :, ts0:ts0 + 128],
                                           stg[par][:, 0:512])
            vt = self.inc("v", self.V.memset(sxa[:, 0:D], 0.0))
            self.wait(self.SY, vt)
            self.wait(self.SY, self.tok("s"))
            dtok = self.ld(sxa[0:4, 0:D], self.d_halo[:, :])
            self.wait(self.V, dtok)
            self.wait(self.S, dtok)
            self.vrt(self.V.reduce_sum(mu[:, 0:1], sxa[:, 0:D], axis=AX))
            self.vrt(self.V.tensor_single_scalar(mu[:, 0:1], mu[:, 0:1], 1.0 / D, ALU.mult))
            st_ = self.inc("s", self.S.activation(sqa[:, 0:D], sxa[:, 0:D], AF.Square))
            self.wait(self.V, st_)
            self.vrt(self.V.reduce_sum(var[:, 0:1], sqa[:, 0:D], axis=AX))
            self.vrt(self.V.tensor_single_scalar(var[:, 0:1], var[:, 0:1], 1.0 / D, ALU.mult))
            self.vrt(self.V.tensor_mul(mu2[:, 0:1], mu[:, 0:1], mu[:, 0:1]))
            vt = self.vrt(self.V.tensor_sub(var[:, 0:1], var[:, 0:1], mu2[:, 0:1]))
            self.wait(self.S, vt)
            st_ = self.inc("s", self.S.activation(rstd[:, 0:1], var[:, 0:1], AF.Sqrt,
                                                  bias=self.epsc[:, :1]))
            self.wait(self.V, st_)
            self.vrt(self.V.reciprocal(rstd[:, 0:1], rstd[:, 0:1]))
            vt = self.inc("v", self.V.tensor_scalar(
                sqa[:, 0:D], sxa[:, 0:D], mu[:, 0:1], rstd[:, 0:1],
                ALU.subtract, ALU.mult))
            self.wait(self.Tn, vt)
            ev_last = None
            for c in range(4):
                p2 = c % 4
                if ptr_free[p2] is not None:
                    self.wait(self.Tn, ptr_free[p2])
                tt_ = self.inc("t", self.Tn.transpose(
                    ptr[p2][:, :], sqa[:, c * 128:(c + 1) * 128], self.idt[:, :]))
                self.wait(self.S, tt_)
                ev_last = self.inc("s", self.S.activation(
                    self.halo_x1[:, c * 3:(c + 1) * 3], ptr[p2][:, 0:3],
                    AF.Identity, bias=self.col("ln_b", c),
                    scale=self.col("ln_g", c)))
                ptr_free[p2] = ev_last
            self.wait(self.V, ev_last)
            self.inc("v", self.V.tensor_copy(self.halo_h[:, :], self.halo_x1[:, :]))
            self.bar()

    def compute_rmsh(self, l, rmsh):
        with ExitStack() as es:
            pst = self.ps("rms_pst", [1, T], es)
            hst = [self.sb(f"rms_h{j}", [128, T], stack=es) for j in range(2)]
            sqb = [self.sb(f"rms_sq{j}", [128, T], stack=es) for j in range(2)]
            rr = self.sb("rms_rr", [1, T], stack=es)
            pb = self.ps("rms_pb", [128, 512], es)
            mm_toks = [None, None]
            sq_done = [None, None]
            for cc in range(4):
                par = cc % 2
                if sq_done[par] is not None:
                    self.wait(self.SY, sq_done[par])
                dtok = self.ld(hst[par][:, :], self.v_h[:, cc, :])
                self.wait(self.S, dtok)
                if mm_toks[par] is not None:
                    self.wait(self.S, mm_toks[par])
                st_ = self.inc("s", self.S.activation(sqb[par][:, :], hst[par][:, :], AF.Square))
                sq_done[par] = st_
                self.wait(self.Tn, st_)
                last = None
                for tt in range(4):
                    last = self.Tn.matmul(pst[:, tt * 512:(tt + 1) * 512],
                                          self.ones_c[:, :1],
                                          sqb[par][:, tt * 512:(tt + 1) * 512],
                                          start=(cc == 0), stop=(cc == 3))
                mm_toks[par] = self.inc("t", last)
            self.wait(self.V, mm_toks[(4 - 1) % 2])
            vt = self.vrt(self.V.tensor_single_scalar(rr[:1, :], pst[:1, :], 1.0 / DC, ALU.mult))
            self.wait(self.S, vt)
            st_ = self.inc("s", self.S.activation(rr[:1, :], rr[:1, :], AF.Sqrt,
                                                  bias=self.epsc[0:1, :1]))
            self.wait(self.V, st_)
            vt = self.vrt(self.V.reciprocal(rr[:1, :], rr[:1, :]))
            self.wait(self.Tn, vt)
            self.wait(self.SY, self.tok("s"))
            hs_used = [None, None]
            for tt in range(4):
                bt = self.inc("t", self.Tn.matmul(pb[:, :], self.ones_r[:1, :],
                                                  rr[:1, tt * 512:(tt + 1) * 512],
                                                  start=True, stop=True))
                vlast = None
                for cc in range(4):
                    par = cc % 2
                    if hs_used[par] is not None:
                        self.wait(self.SY, hs_used[par])
                    dtok = self.ld(hst[par][:, 0:512], self.v_h[:, cc, tt * 512:(tt + 1) * 512])
                    self.wait(self.V, dtok)
                    self.wait(self.V, bt)
                    vlast = self.inc("v", self.V.scalar_tensor_tensor(
                        rmsh[:, cc * T + tt * 512: cc * T + (tt + 1) * 512],
                        hst[par][:, 0:512], self.col("rmsw", l * 4 + cc),
                        pb[:, :], ALU.mult, ALU.mult))
                    hs_used[par] = vlast
                self.wait(self.Tn, vlast)
            self.bar()

    def halo_xcpre(self, l, inWT, hxc):
        with ExitStack() as es:
            psq = self.ps("hx_psq", [1, 4], es)
            pb = self.ps("hx_pb", [128, 4], es)
            pm = self.ps("hx_pm", [128, 4], es)
            sqh = self.sb("hx_sq", [128, 12], stack=es)
            rr = self.sb("hx_rr", [1, 4], stack=es)
            rmf = self.sb("hx_rmf", [128, 12], stack=es)
            rmh = self.sb("hx_rmh", [128, 12], BF16, stack=es)
            st_ = self.inc("s", self.S.activation(sqh[:, :], self.halo_h[:, :], AF.Square))
            self.wait(self.Tn, st_)
            last = None
            for cc in range(4):
                last = self.Tn.matmul(psq[:1, 0:3], self.ones_c[:, :1],
                                      sqh[:, cc * 3:(cc + 1) * 3],
                                      start=(cc == 0), stop=(cc == 3))
            t_ = self.inc("t", last)
            self.wait(self.V, t_)
            vt = self.vrt(self.V.tensor_single_scalar(rr[:1, 0:3], psq[:1, 0:3], 1.0 / DC, ALU.mult))
            self.wait(self.S, vt)
            st_ = self.inc("s", self.S.activation(rr[:1, 0:3], rr[:1, 0:3], AF.Sqrt,
                                                  bias=self.epsc[0:1, :1]))
            self.wait(self.V, st_)
            vt = self.vrt(self.V.reciprocal(rr[:1, 0:3], rr[:1, 0:3]))
            self.wait(self.Tn, vt)
            bt = self.inc("t", self.Tn.matmul(pb[:, 0:3], self.ones_r[:1, :],
                                              rr[:1, 0:3], start=True, stop=True))
            self.wait(self.V, bt)
            vlast = None
            for cc in range(4):
                vlast = self.inc("v", self.V.scalar_tensor_tensor(
                    rmf[:, cc * 3:(cc + 1) * 3], self.halo_h[:, cc * 3:(cc + 1) * 3],
                    self.col("rmsw", l * 4 + cc), pb[:, 0:3], ALU.mult, ALU.mult))
            self.wait(self.S, vlast)
            st_ = self.inc("s", self.S.activation(rmh[:, :], rmf[:, :], AF.Identity))
            self.wait(self.Tn, st_)
            for j in range(8):
                last = None
                for cc in range(4):
                    last = self.Tn.matmul(pm[:, 0:3],
                                          inWT[:, cc * FF + j * 128:cc * FF + (j + 1) * 128],
                                          rmh[:, cc * 3:(cc + 1) * 3],
                                          start=(cc == 0), stop=(cc == 3))
                t_ = self.inc("t", last)
                self.wait(self.S, t_)
                st_ = self.inc("s", self.S.activation(
                    hxc[:, j * 3:(j + 1) * 3], pm[:, 0:3], AF.Identity,
                    scale=self.col("hmask")))
                self.wait(self.Tn, st_)
            self.bar()

    def mamba_layer(self, l):
        with ExitStack() as es:
            inWT = self.sb("inWT", [128, 4 * FF], BF16, es)
            moWT = self.sb("moWT", [128, 8 * 512], BF16, es)
            xpWT = self.sb("xpWT", [128, 8 * 64], BF16, es)
            dtWT = self.sb("dtWT", [32, 1024], BF16, es)
            rmsh = self.sb("rmsh", [128, 4 * T], BF16, es)
            dbc = self.sb("dbc", [64, T], F32, es)
            dtbf = self.sb("dtbf", [32, T], BF16, es)
            hxc = self.sb("hxc", [128, 24], F32, es)
            self.bsel = self.sb("bsel_sb", [48, NS * 128], stack=es)
            self.ld(self.bsel[:, :], self.d_bsel[:, :])
            gtoks = [self.gld(inWT[:, :], self.d_inWT[l][:, :]),
                     self.gld(moWT[:, :], self.d_moWT[l][:, :]),
                     self.gld(xpWT[:, :], self.d_xpWT[l][:, :]),
                     self.gld(dtWT[:, :], self.d_dtWT[l][:, :])]
            self.compute_rmsh(l, rmsh)
            for e in "vst":
                for gt in gtoks:
                    self.wait(self.E[e], gt)
            self.halo_xcpre(l, inWT, hxc)
            self.mamba_pass(l, True, inWT, moWT, xpWT, dtWT, rmsh, dbc, dtbf, hxc)
            if KB.DEBUG and l == 0:
                self.bar()
                self.stdma(self.d_dbg1[:, 0:128], self.hstate[:, :])
                self.stdma(self.d_dbg1[:, 128:256], self.pfull[:, :])
                self.stdma(self.d_dbg1[:, 256:264], self.cs[:, :])
            self.relay(l)
            if KB.DEBUG and l == 0:
                self.bar()
                self.stdma(self.d_dbg2[:, :], self.hin[:, :])
            self.mamba_pass(l, False, inWT, moWT, xpWT, dtWT, rmsh, dbc, dtbf, hxc)
            if KB.DEBUG:
                self.bar()
                dtg = self.d_dbg3 if l == 0 else self.d_dbg4
                self.gld(dtg.ap(), self.d_h.ap())
            if l < NL - 1:
                self.halo_relay()
            self.bar()

    def _xc_mm(self, l, hf, c, inWT, xcpre):
        for tt in range(2):
            i = self.pmm_next()
            last = None
            for kc in range(4):
                last = self.Tn.matmul(
                    self.pmm[i][:, :],
                    inWT[:, kc * FF + c * 128:kc * FF + (c + 1) * 128],
                    self.rmsh_cur[:, kc * T + hf * HALF + tt * 512:
                                  kc * T + hf * HALF + (tt + 1) * 512],
                    start=(kc == 0), stop=(kc == 3))
            t_ = self.inc("t", last, commit=False)
            self.wait(self.S, t_)
            ev = self.inc("s", self.S.activation(
                xcpre[:, 3 + tt * 512:3 + (tt + 1) * 512], self.pmm[i][:, :],
                AF.Identity), commit=False)
            self.pmm_free[i] = ev
        return ev

    def _xc_conv(self, l, hf, c, xcpre, cvo, out_sb, out_off, halo3, ev):
        self.wait(self.V, ev)
        self.inc("v", self.V.tensor_copy(xcpre[:, 0:3], halo3[:, c * 3:(c + 1) * 3]),
                 commit=False)
        self.V.tensor_scalar(cvo[:, :HALF], xcpre[:, 0:HALF],
                             self.col("cw", l * 32 + 0 * 8 + c), None, ALU.mult)
        for k in range(1, 4):
            self.V.scalar_tensor_tensor(cvo[:, :HALF], xcpre[:, k:k + HALF],
                                        self.col("cw", l * 32 + k * 8 + c),
                                        cvo[:, :HALF], ALU.mult, ALU.add)
        vt = self.inc("v", self.V.tensor_copy(halo3[:, c * 3:(c + 1) * 3],
                                              xcpre[:, 1024:1027]))
        self.wait(self.S, vt)
        st_ = self.inc("s", self.S.activation(out_sb[:, out_off:out_off + HALF],
                                              cvo[:, :HALF], AF.Silu,
                                              bias=self.col("cb", l * 8 + c)))
        return st_

    def _xc_chunk(self, l, hf, c, inWT, xcpre, cvo, out_sb, out_off, halo3):
        for tt in range(2):
            i = self.pmm_next()
            last = None
            for kc in range(4):
                last = self.Tn.matmul(
                    self.pmm[i][:, :],
                    inWT[:, kc * FF + c * 128:kc * FF + (c + 1) * 128],
                    self.rmsh_cur[:, kc * T + hf * HALF + tt * 512:
                                  kc * T + hf * HALF + (tt + 1) * 512],
                    start=(kc == 0), stop=(kc == 3))
            t_ = self.inc("t", last)
            self.wait(self.S, t_)
            ev = self.inc("s", self.S.activation(
                xcpre[:, 3 + tt * 512:3 + (tt + 1) * 512], self.pmm[i][:, :],
                AF.Identity))
            self.pmm_free[i] = ev
        self.wait(self.V, self.tok("s"))
        self.inc("v", self.V.tensor_copy(xcpre[:, 0:3], halo3[:, c * 3:(c + 1) * 3]))
        self.V.tensor_scalar(cvo[:, :], xcpre[:, 0:HALF],
                             self.col("cw", l * 32 + 0 * 8 + c), None, ALU.mult)
        for k in range(1, 4):
            self.V.scalar_tensor_tensor(cvo[:, :], xcpre[:, k:k + HALF],
                                        self.col("cw", l * 32 + k * 8 + c),
                                        cvo[:, :], ALU.mult, ALU.add)
        vt = self.inc("v", self.V.tensor_copy(halo3[:, c * 3:(c + 1) * 3],
                                              xcpre[:, 1024:1027]))
        self.wait(self.S, vt)
        st_ = self.inc("s", self.S.activation(out_sb[:, out_off:out_off + HALF],
                                              cvo[:, :], AF.Silu,
                                              bias=self.col("cb", l * 8 + c)))
        return st_

    def _delta_mm(self, l, hf, c, dtWT, dtbf, pD, pD_free):
        par = c % 2
        if pD_free[par] is not None:
            self.wait(self.Tn, pD_free[par])
        last = None
        for tt in range(2):
            last = self.Tn.matmul(
                pD[:, par * HALF + tt * 512:par * HALF + (tt + 1) * 512],
                dtWT[:, c * 128:(c + 1) * 128],
                dtbf[:, hf * HALF + tt * 512:hf * HALF + (tt + 1) * 512],
                start=True, stop=True)
        return self.inc("t", last, commit=False)

    def _delta_sp(self, l, hf, c, pD, delta, scratch, t_):
        par = c % 2
        self.wait(self.S, t_)
        self.S.activation(scratch[:, 0:HALF], pD[:, par * HALF:(par + 1) * HALF],
                          AF.Exp, bias=self.col("dtb", l * 8 + c))
        st_ = self.inc("s", self.S.activation(delta[:, c * HALF:(c + 1) * HALF],
                                              scratch[:, 0:HALF], AF.Ln, bias=1.0),
                       commit=False)
        return st_

    def _delta_chunk(self, l, hf, c, dtWT, dtbf, pD, delta, scratch):
        self.wait(self.Tn, self.tok("s"))
        last = None
        for tt in range(2):
            last = self.Tn.matmul(
                pD[:, tt * 512:(tt + 1) * 512], dtWT[:, c * 128:(c + 1) * 128],
                dtbf[:, hf * HALF + tt * 512:hf * HALF + (tt + 1) * 512],
                start=True, stop=True)
        t_ = self.inc("t", last)
        self.wait(self.S, t_)
        self.S.activation(scratch[:, 0:HALF], pD[:, :HALF], AF.Exp,
                          bias=self.col("dtb", l * 8 + c))
        st_ = self.inc("s", self.S.activation(delta[:, c * HALF:(c + 1) * HALF],
                                              scratch[:, 0:HALF], AF.Ln, bias=1.0))
        return st_

    def mamba_pass(self, l, pa, inWT, moWT, xpWT, dtWT, rmsh, dbc, dtbf, hxc):
        self.rmsh_cur = rmsh
        with ExitStack() as pes:
            halo3 = self.sb("halo3", [128, 24], stack=pes)
            xcpre = self.sb("xcpre", [128, 1032], stack=pes)
            cvo = self.sb("cvo", [128, HALF], stack=pes)
            xcf = self.sb("xcf", [128, HALF], stack=pes)
            dAb = self.sb("dAb", [128, 2 * HALF], stack=pes)
            dBxb = self.sb("dBxb", [128, 2 * HALF], stack=pes)
            hscr = self.sb("hscr", [128, HALF], stack=pes)
            bmc = self.sb("bmc", [48, HALF], stack=pes)
            Bmb = self.sb("Bmb", [128, HALF], stack=pes)
            if pa:
                xcbf = self.sb("xcbf", [128, 2 * HALF], BF16, stack=pes)
                csh = self.sb("csh", [128, 16], stack=pes)
                hpB = self.sb("hpB", [128, 128], stack=pes)
                state = self.hstate
                self.V.memset(self.hstate[:, :], 0.0)
                self.V.memset(hpB[:, :], 0.0)
                vt = self.inc("v", self.V.memset(self.cs[:, :], 0.0))
            else:
                state = self.hin
            vt = self.inc("v", self.V.tensor_copy(halo3[:, :], hxc[:, :]))
            self.wait(self.S, vt)
            self.wait(self.Tn, vt)

            for hf in range(2):
                with ExitStack() as hes:
                    yh = None
                    if not pa:
                        yh = self.sb("y_h", [128, 8 * HALF], stack=hes)
                    dues = ExitStack()
                    delta = self.sb("delta_h", [128, 8 * HALF], stack=dues)
                    u = self.sb("u_h", [128, 8 * HALF], stack=dues)
                    with ExitStack() as ces:
                        if pa:
                            pdbc_es = ExitStack()
                            pdbc = self.ps("pdbc", [64, T], pdbc_es)
                            bufs = [(xcpre, cvo), (dAb, dBxb)]
                            evs = [None] * 8
                            xbfs = [xcbf[:, 0:HALF], xcbf[:, HALF:2 * HALF]]
                            cast_t = [None, None]
                            for c in range(9):
                                if c < 8:
                                    evs[c] = self._xc_mm(l, hf, c, inWT,
                                                         bufs[c % 2][0])
                                if c >= 1:
                                    cc = c - 1
                                    xp_, cv_ = bufs[cc % 2]
                                    st_ = self._xc_conv(l, hf, cc, xp_, cv_,
                                                        u, cc * HALF, halo3,
                                                        evs[cc])
                                    xb = xbfs[cc % 2]
                                    if cast_t[cc % 2] is not None:
                                        self.wait(self.S, cast_t[cc % 2])
                                    cst = self.inc("s", self.S.activation(
                                        xb, u[:, cc * HALF:(cc + 1) * HALF],
                                        AF.Identity), commit=False)
                                    self.wait(self.Tn, cst)
                                    last = None
                                    for tt in range(2):
                                        last = self.Tn.matmul(
                                            pdbc[:, hf * HALF + tt * 512:
                                                 hf * HALF + (tt + 1) * 512],
                                            xpWT[:, cc * 64:(cc + 1) * 64],
                                            xb[:, tt * 512:(tt + 1) * 512],
                                            start=(cc == 0), stop=(cc == 7))
                                    cast_t[cc % 2] = self.inc("t", last,
                                                              commit=False)
                            self.inc("t", self.Tn.nop())
                            self.inc("s", self.S.nop())
                            t_ = self.tok("t")
                            self.wait(self.S, t_)
                            self.S.activation(dbc[0:64, hf * HALF:(hf + 1) * HALF],
                                              pdbc[:, hf * HALF:(hf + 1) * HALF],
                                              AF.Identity)
                            st_ = self.inc("s", self.S.activation(
                                dtbf[:, hf * HALF:(hf + 1) * HALF],
                                dbc[0:32, hf * HALF:(hf + 1) * HALF], AF.Identity))
                            self.wait(self.Tn, st_)
                            self.wait(self.V, st_)
                            pdbc_es.close()
                            pD = self.ps("pD", [128, 2 * HALF], ces)
                            pD_free = [None, None]
                            dts = [None] * 8
                            sps = [None] * 8
                            sbufs = [cvo, dBxb]
                            for c in range(10):
                                if c < 8:
                                    dts[c] = self._delta_mm(l, hf, c, dtWT, dtbf,
                                                            pD, pD_free)
                                if 1 <= c <= 8:
                                    cc = c - 1
                                    sps[cc] = self._delta_sp(l, hf, cc, pD, delta,
                                                             sbufs[cc % 2], dts[cc])
                                    pD_free[cc % 2] = sps[cc]
                                if c >= 2:
                                    cc = c - 2
                                    self.wait(self.V, sps[cc])
                                    vt = self.inc("v", self.V.tensor_mul(
                                        u[:, cc * HALF:(cc + 1) * HALF],
                                        delta[:, cc * HALF:(cc + 1) * HALF],
                                        u[:, cc * HALF:(cc + 1) * HALF]),
                                        commit=False)
                                    self.vrt(self.V.reduce_sum(
                                        xcf[:, 0:1],
                                        delta[:, cc * HALF:(cc + 1) * HALF],
                                        axis=AX))
                                    if hf == 0:
                                        self.vrt(self.V.tensor_copy(
                                            self.cs[:, cc:cc + 1], xcf[:, 0:1]))
                                    else:
                                        self.vrt(self.V.tensor_add(
                                            self.cs[:, cc:cc + 1],
                                            self.cs[:, cc:cc + 1], xcf[:, 0:1]))

                            if KB.DEBUG and pa and l == 0 and hf == 0:
                                self.bar("vs")
                                self.stdma(self.d_dbg8[:, 0:8192], delta[:, :])
                                self.dma_quiesce(self.V, ("ds",))
                            for c in range(8):
                                dsl = delta[:, c * HALF:(c + 1) * HALF]
                                self.vrt(self.V.tensor_tensor_scan(
                                    dsl, dsl, dsl, 0.0, ALU.add, ALU.bypass))
                                self.vrt(self.V.tensor_copy(
                                    csh[:, hf * 8 + c:hf * 8 + c + 1],
                                    delta[:, (c + 1) * HALF - 1:(c + 1) * HALF]))
                                self.inc("v", self.V.tensor_scalar(
                                    dsl, dsl, csh[:, hf * 8 + c:hf * 8 + c + 1],
                                    -1.0, ALU.subtract, ALU.mult), commit=False)
                            self.vrt(self.V.engine_nop())
                        else:
                            pD = self.ps("pD", [128, 2 * HALF], ces)
                            bufs = [(xcpre, cvo), (dAb, dBxb)]
                            xcfs = [xcf[:, 0:HALF], hscr[:, 0:HALF]]
                            pD_free = [None, None]
                            dts = [None] * 8
                            evs = [None] * 8
                            uts = [None, None]
                            for c in range(9):
                                if c < 8:
                                    dts[c] = self._delta_mm(l, hf, c, dtWT, dtbf,
                                                            pD, pD_free)
                                    evs[c] = self._xc_mm(l, hf, c, inWT,
                                                         bufs[c % 2][0])
                                if c >= 1:
                                    cc = c - 1
                                    par = cc % 2
                                    sp_ = self._delta_sp(
                                        l, hf, cc, pD, delta,
                                        yh[:, cc * HALF:(cc + 1) * HALF], dts[cc])
                                    pD_free[par] = sp_
                                    if uts[par] is not None:
                                        self.wait(self.S, uts[par])
                                    xt = self._xc_conv(l, hf, cc, bufs[par][0],
                                                       bufs[par][1], xcfs[par], 0,
                                                       halo3, evs[cc])
                                    self.wait(self.V, xt)
                                    self.V.tensor_scalar(
                                        yh[:, cc * HALF:(cc + 1) * HALF],
                                        xcfs[par], self.col("Dp", l * 8 + cc),
                                        None, ALU.mult)
                                    uts[par] = self.inc("v", self.V.tensor_mul(
                                        u[:, cc * HALF:(cc + 1) * HALF],
                                        delta[:, cc * HALF:(cc + 1) * HALF],
                                        xcfs[par]), commit=False)
                            self.inc("v", self.V.engine_nop())
                            self.inc("s", self.S.nop())

                    self.wait(self.G, self.tok("v"))
                    self.wait(self.S, self.tok("v"))
                    self.wait(self.SY, self.tok("s"))
                    self.wait(self.SY, self.tok("t"))
                    dtok = self.ld(bmc[0:16, :], dbc[32:48, hf * HALF:(hf + 1) * HALF])
                    if not pa:
                        dtok = self.ld(bmc[32:48, :],
                                       dbc[48:64, hf * HALF:(hf + 1) * HALF])
                    self.wait(self.Tn, dtok)
                    with ExitStack() as nes:
                        pwB = self.ps("pwB", [128, HALF], nes)
                        pwC = self.ps("pwC", [128, HALF], nes)
                        vdone = []
                        gdone = []
                        for n in range(16):
                            npar = n % 2
                            if n > 0:
                                self.wait(self.Tn, vdone[n * 8 - 1])
                            last = None
                            for tt in range(2):
                                last = self.Tn.matmul(
                                    pwB[:, tt * 512:(tt + 1) * 512],
                                    self.bsel[0:16, n * 128:(n + 1) * 128],
                                    bmc[0:16, tt * 512:(tt + 1) * 512],
                                    start=True, stop=True)
                            bt = self.inc("t", last)
                            if not pa:
                                self.wait(self.S, bt)
                                if n >= 1:
                                    self.wait(self.S, gdone[n * 8 - 1])
                                bev = self.inc("s", self.S.activation(
                                    Bmb[:, :], pwB[:, :], AF.Identity))
                            if not pa:
                                self.wait(self.Tn, bev)
                                last = None
                                for tt in range(2):
                                    last = self.Tn.matmul(
                                        pwC[:, tt * 512:(tt + 1) * 512],
                                        self.bsel[32:48, n * 128:(n + 1) * 128],
                                        bmc[32:48, tt * 512:(tt + 1) * 512],
                                        start=True, stop=True)
                                cev = self.inc("t", last)
                            for c in range(8):
                                idx = n * 8 + c
                                par = idx % 2
                                if idx >= 2:
                                    self.wait(self.S, vdone[idx - 2])
                                st_ = self.inc("s", self.S.activation(
                                    dAb[:, par * HALF:(par + 1) * HALF],
                                    delta[:, c * HALF:(c + 1) * HALF], AF.Exp,
                                    scale=self.A_sb[:, l * 128 + c * 16 + n:
                                                    l * 128 + c * 16 + n + 1]),
                                    commit=(idx < 2))
                                if not pa:
                                    if idx >= 2:
                                        self.wait(self.G, vdone[idx - 2])
                                    self.wait(self.G, bev)
                                    gt_ = self.inc("g", self.G.tensor_mul(
                                        dBxb[:, par * HALF:(par + 1) * HALF],
                                        u[:, c * HALF:(c + 1) * HALF],
                                        Bmb[:, :]), commit=(idx < 2))
                                    gdone.append(gt_)
                                self.wait(self.V, st_)
                                col = c * 16 + n
                                if pa:
                                    self.wait(self.V, bt)
                                    self.V.tensor_mul(
                                        dBxb[:, par * HALF:(par + 1) * HALF],
                                        u[:, c * HALF:(c + 1) * HALF],
                                        pwB[:, :])
                                    hp = state if hf == 0 else hpB
                                    self.V.scalar_tensor_tensor(
                                        hscr[:, :],
                                        dAb[:, par * HALF:(par + 1) * HALF],
                                        1.0,
                                        dBxb[:, par * HALF:(par + 1) * HALF],
                                        ALU.bypass, ALU.mult,
                                        accum_out=hp[:, col:col + 1])
                                else:
                                    self.wait(self.V, gt_)
                                    self.V.tensor_tensor_scan(
                                        hscr[:, :],
                                        dAb[:, par * HALF:(par + 1) * HALF],
                                        dBxb[:, par * HALF:(par + 1) * HALF],
                                        state[:, col:col + 1], ALU.mult, ALU.add)
                                    self.V.tensor_copy(state[:, col:col + 1],
                                                       hscr[:, HALF - 1:HALF])
                                if not pa:
                                    self.wait(self.V, cev)
                                    self.V.tensor_mul(
                                        dBxb[:, par * HALF:(par + 1) * HALF],
                                        hscr[:, :], pwC[:, :])
                                    vt = self.inc("v", self.V.tensor_add(
                                        yh[:, c * HALF:(c + 1) * HALF],
                                        yh[:, c * HALF:(c + 1) * HALF],
                                        dBxb[:, par * HALF:(par + 1) * HALF]))
                                else:
                                    vt = self.inc("v", self.V.engine_nop())
                                vdone.append(vt)
                    self.wait(self.S, self.tok("v"))
                    self.bar("vs")
                    if KB.DEBUG and pa and l == 0 and hf == 0:
                        self.stdma(self.d_dbg7[:, 0:128], self.hstate[:, :])
                        self.stdma(self.d_dbg7[:, 128:136], csh[:, 0:8])
                        self.dma_quiesce(self.V, ("ds",))
                    dues.close()

                    if pa and hf == 1:
                        if KB.DEBUG and l == 0:
                            self.bar("vs")
                            self.stdma(self.d_dbg6[:, 0:128], self.hstate[:, :])
                            self.stdma(self.d_dbg6[:, 128:256], hpB[:, :])
                            self.stdma(self.d_dbg6[:, 256:272], csh[:, :])
                            self.dma_quiesce(self.V, ("ds",))
                        self.vrt(self.V.engine_nop())
                        vt = None
                        for c in range(8):
                            vt = self.inc("v", self.V.tensor_mul(
                                self.pfull[:, c * 16:(c + 1) * 16],
                                csh[:, 8 + c:9 + c].to_broadcast([128, 16]),
                                self.A_sb[:, l * 128 + c * 16:l * 128 + (c + 1) * 16]),
                                commit=False)
                        self.wait(self.S, self.inc("v", self.V.engine_nop())[1] and self.tok("v"))
                        st_ = self.inc("s", self.S.activation(
                            self.pfull[:, :], self.pfull[:, :], AF.Exp))
                        self.wait(self.V, st_)
                        self.V.tensor_mul(self.hstate[:, :], self.hstate[:, :],
                                          self.pfull[:, :])
                        vt = self.inc("v", self.V.tensor_add(
                            self.hstate[:, :], self.hstate[:, :], hpB[:, :]))
                        for c in range(8):
                            vt = self.inc("v", self.V.tensor_mul(
                                self.pfull[:, c * 16:(c + 1) * 16],
                                self.cs[:, c:c + 1].to_broadcast([128, 16]),
                                self.A_sb[:, l * 128 + c * 16:l * 128 + (c + 1) * 16]),
                                commit=False)
                        vt = self.vrt(self.V.engine_nop())
                        self.wait(self.S, vt)
                        self.inc("s", self.S.activation(self.pfull[:, :],
                                                        self.pfull[:, :], AF.Exp))

                    if not pa:
                        with ExitStack() as fes:
                            ybf = self.sb("ybf_h", [128, 8 * HALF], BF16, stack=fes)
                            hseg = self.sb("hseg", [128, 4 * HALF], stack=fes)
                            szb = self.sb("szb", [128, 512], stack=fes)
                            htok = self.ld(hseg[:, :].rearrange(
                                "p (c x) -> p c x", c=4),
                                self.v_h[:, :, hf * HALF:(hf + 1) * HALF])
                            for c in range(8):
                                for tt in range(2):
                                    i = self.pmm_next()
                                    last = None
                                    for kc in range(4):
                                        last = self.Tn.matmul(
                                            self.pmm[i][:, :],
                                            inWT[:, kc * FF + (8 + c) * 128:
                                                 kc * FF + (9 + c) * 128],
                                            rmsh[:, kc * T + hf * HALF + tt * 512:
                                                 kc * T + hf * HALF + (tt + 1) * 512],
                                            start=(kc == 0), stop=(kc == 3))
                                    t_ = self.inc("t", last)
                                    self.wait(self.S, t_)
                                    st_ = self.inc("s", self.S.activation(
                                        szb[:, :], self.pmm[i][:, :], AF.Silu))
                                    self.pmm_free[i] = st_
                                    self.wait(self.V, st_)
                                    vt = self.inc("v", self.V.tensor_mul(
                                        yh[:, c * HALF + tt * 512:
                                           c * HALF + (tt + 1) * 512],
                                        yh[:, c * HALF + tt * 512:
                                           c * HALF + (tt + 1) * 512],
                                        szb[:, :]))
                                    self.wait(self.S, vt)
                                vt = self.tok("v")
                                self.wait(self.S, vt)
                                self.inc("s", self.S.activation(
                                    ybf[:, c * HALF:(c + 1) * HALF],
                                    yh[:, c * HALF:(c + 1) * HALF], AF.Identity))
                            st_ = self.tok("s")
                            self.wait(self.Tn, st_)
                            self.wait(self.V, htok)
                            for tt in range(2):
                                for m in range(4):
                                    i = self.pmm_next()
                                    last = None
                                    for c in range(8):
                                        last = self.Tn.matmul(
                                            self.pmm[i][:, :],
                                            moWT[:, c * 512 + m * 128:
                                                 c * 512 + (m + 1) * 128],
                                            ybf[:, c * HALF + tt * 512:
                                                c * HALF + (tt + 1) * 512],
                                            start=(c == 0), stop=(c == 7))
                                    t_ = self.inc("t", last)
                                    self.wait(self.V, t_)
                                    vt = self.inc("v", self.V.tensor_add(
                                        hseg[:, m * HALF + tt * 512:
                                             m * HALF + (tt + 1) * 512],
                                        hseg[:, m * HALF + tt * 512:
                                             m * HALF + (tt + 1) * 512],
                                        self.pmm[i][:, :]))
                                    self.pmm_free[i] = vt
                            vt = self.tok("v")
                            self.wait(self.S, vt)
                            htok = self.stdma(
                                self.v_h[:, :, hf * HALF:(hf + 1) * HALF],
                                hseg[:, :].rearrange("p (c x) -> p c x", c=4))
                            self.wait(self.SY, htok)
            self.bar()

    def relay(self, l):
        vt = self.tok("v")
        self.wait(self.G, vt)
        self.wait(self.G, self.tok("s"))
        g1 = self.gld(self.d_ag1i[:, 0:128], self.hstate[:, :])
        g2 = self.gld(self.d_ag1i[:, 128:256], self.pfull[:, :])
        self.wait(self.G, g1)
        self.wait(self.G, g2)
        ct = self.inc("cc", self.G.collective_compute(
            "AllGather", ALU.bypass,
            replica_groups=[list(range(NCORE))],
            ins=[self.d_ag1i.ap().opt()],
            outs=[self.d_ag1o.ap().opt()]), 1)
        self.wait(self.G, ct)
        with ExitStack() as es:
            gat = self.sb("relay_gat", [128, NCORE * 256], stack=es)
            tmp = self.sb("relay_tmp", [128, 128], stack=es)
            gt = self.gld(gat[:, :].rearrange("p (j c) -> p j c", j=NCORE),
                          self.d_ag1o.ap().rearrange("(j p) c -> p j c", p=128))
            self.wait(self.V, gt)
            self.V.memset(self.hin[:, :], 0.0)
            for j in range(NCORE):
                self.V.tensor_scalar(tmp[:, :],
                                     gat[:, j * 256 + 128:j * 256 + 256],
                                     1.0, self.col("mselp", j),
                                     ALU.subtract, ALU.mult)
                self.V.tensor_scalar(tmp[:, :], tmp[:, :], 1.0, None, ALU.add)
                self.V.tensor_mul(self.hin[:, :], self.hin[:, :], tmp[:, :])
                self.V.tensor_scalar(tmp[:, :], gat[:, j * 256:j * 256 + 128],
                                     self.col("mselp", j), None, ALU.mult)
                self.V.tensor_add(self.hin[:, :], self.hin[:, :], tmp[:, :])
            self.vrt(self.V.engine_nop())
            self.bar()

    def halo_relay(self):
        self.dma_quiesce(self.G, ("ds",))
        gt = self.gld(self.d_ag2i[:, :], self.v_h[:, :, T - 3:T])
        self.wait(self.G, gt)
        ct = self.inc("cc", self.G.collective_compute(
            "AllGather", ALU.bypass,
            replica_groups=[list(range(NCORE))],
            ins=[self.d_ag2i.ap().opt()],
            outs=[self.d_ag2o.ap().opt()]), 1)
        self.wait(self.G, ct)
        with ExitStack() as es:
            gat = self.sb("halo_gat", [128, NCORE * 12], stack=es)
            gt = self.gld(gat[:, :].rearrange("p (j c) -> p j c", j=NCORE),
                          self.d_ag2o.ap().rearrange("(j p) c -> p j c", p=128))
            self.wait(self.V, gt)
            self.V.memset(self.halo_h[:, :], 0.0)
            vt = None
            for j in range(NCORE):
                vt = self.inc("v", self.V.scalar_tensor_tensor(
                    self.halo_h[:, :], gat[:, j * 12:(j + 1) * 12],
                    self.col("msel", j), self.halo_h[:, :], ALU.mult, ALU.add))
            self.bar()

    def phase_po_ln0(self):
        self.xffn = self.sb("xffn_bf", [128, 8 * T], BF16)
        self.acc = self.sb("acc_tm", [128, 16 * 1024], F32)
        self.coeff = self.sb("coeff_tm", [128, 16 * NE], F32)
        self.b2sb = self.sb("b2_sb", [NE, 1024], F32)
        with ExitStack() as es:
            poWT = self.sb("poWT", [128, 8 * 1024], BF16, es)
            gateWT = self.sb("gateWT", [128, 8 * 14], F32, es)
            outbf = self.sb("outbf", [128, 8 * T], BF16, es)
            stage = self.sb("po_stage", [128, T], stack=es)
            gt = self.gld(poWT[:, :], self.d_poWT[:, :])
            gt2 = self.ld(gateWT[:, :], self.d_gateWT[:, :])
            with ExitStack() as es0:
                stage2 = self.sb("po_stage2", [128, T], stack=es0)
                for c in range(4):
                    dt1 = self.ld(stage[:, :], self.v_x1[:, c, :])
                    self.wait(self.S, dt1)
                    st_ = self.inc("s", self.S.activation(
                        outbf[:, c * T:(c + 1) * T], stage[:, :], AF.Identity))
                    self.wait(self.SY, st_)
                for c in range(4):
                    dt1 = self.ld(stage[:, :], self.v_x2[:, c, :])
                    dt2 = self.ld(stage2[:, :], self.v_h[:, c, :])
                    self.wait(self.V, dt2)
                    self.wait(self.V, dt1)
                    vt = self.inc("v", self.V.tensor_add(stage2[:, :], stage2[:, :],
                                                         stage[:, :]))
                    self.wait(self.S, vt)
                    st_ = self.inc("s", self.S.activation(
                        outbf[:, (4 + c) * T:(5 + c) * T], stage2[:, :], AF.Identity))
                    self.wait(self.SY, st_)
                self.bar("vs")
            bt = self.ld(self.b2sb[:, :], self.d_b2r[:, :])
            xab = self.sb("xa_b", [128, 8 * 512], stack=es)
            sqb = self.sb("po_sq", [128, 512], stack=es)
            xnf = self.sb("po_xnf", [128, 512], stack=es)
            srcs = self.sb("po_srcs", [128, D], stack=es)
            rows = self.sb("po_rows", [1, 3 * 512], stack=es)
            with ExitStack() as pses:
                ptr = self.ps("po_ptr", [128, 512], pses)
                pgate = self.ps("po_pgate", [128, 512], pses)
                pst = self.ps("po_pst", [1, 512], pses)
                pst2 = self.ps("po_pst2", [1, 512], pses)
                pbc = self.ps("po_pbc", [128, 512], pses)
                pbc2 = self.ps("po_pbc2", [128, 512], pses)
                self.wait(self.Tn, gt)
                self.wait(self.Tn, gt2)
                self.wait(self.Tn, self.tok("s"))
                for tt in range(4):
                    t0 = tt * 512
                    for m in range(8):
                        i = self.pmm_next()
                        last = None
                        for kc in range(8):
                            last = self.Tn.matmul(
                                self.pmm[i][:, :],
                                poWT[:, kc * 1024 + m * 128:kc * 1024 + (m + 1) * 128],
                                outbf[:, kc * T + t0:kc * T + t0 + 512],
                                start=(kc == 0), stop=(kc == 7))
                        t_ = self.inc("t", last)
                        vsrc = self.v_x1 if m < 4 else self.v_x2
                        dt1 = self.ld(stage[:, 0:512], vsrc[:, m % 4, t0:t0 + 512])
                        self.wait(self.S, t_)
                        st_ = self.inc("s", self.S.activation(
                            xab[:, m * 512:(m + 1) * 512], self.pmm[i][:, :],
                            AF.Identity, bias=self.col("po_b", m)))
                        self.pmm_free[i] = st_
                        self.wait(self.V, st_)
                        self.wait(self.V, dt1)
                        vt = self.inc("v", self.V.tensor_add(
                            xab[:, m * 512:(m + 1) * 512],
                            xab[:, m * 512:(m + 1) * 512], stage[:, 0:512]))
                        self.wait(self.SY, vt)
                    for b in range(4):
                        dt1 = self.ld(srcs[:, :],
                                      self.d_src[t0 + b * 128:t0 + (b + 1) * 128, :])
                        self.wait(self.Tn, dt1)
                        for m in range(8):
                            tt_ = self.inc("t", self.Tn.transpose(
                                ptr[:, b * 128:(b + 1) * 128],
                                srcs[:, m * 128:(m + 1) * 128], self.idt[:, :]))
                            self.wait(self.V, tt_)
                            vt = self.inc("v", self.V.tensor_add(
                                xab[:, m * 512 + b * 128:m * 512 + (b + 1) * 128],
                                xab[:, m * 512 + b * 128:m * 512 + (b + 1) * 128],
                                ptr[:, b * 128:(b + 1) * 128]))
                            self.wait(self.Tn, vt)
                        self.wait(self.SY, self.tok("v"))
                    vt = self.tok("v")
                    self.wait(self.Tn, vt)
                    self.wait(self.S, vt)
                    last = None
                    for m in range(8):
                        last = self.Tn.matmul(pst[:1, :], self.ones_c[:, :1],
                                              xab[:, m * 512:(m + 1) * 512],
                                              start=(m == 0), stop=(m == 7))
                    t_sum = self.inc("t", last)
                    sq_t = None
                    for m in range(8):
                        if sq_t is not None:
                            self.wait(self.S, self.tok("t"))
                        st_ = self.inc("s", self.S.activation(
                            sqb[:, :], xab[:, m * 512:(m + 1) * 512], AF.Square))
                        self.wait(self.Tn, st_)
                        sq_t = self.inc("t", self.Tn.matmul(
                            pst2[:1, :], self.ones_c[:, :1], sqb[:, :],
                            start=(m == 0), stop=(m == 7)))
                    self.wait(self.V, t_sum)
                    self.wait(self.V, sq_t)
                    self.vrt(self.V.tensor_single_scalar(
                        rows[:1, 0:512], pst[:1, :], 1.0 / D, ALU.mult))
                    self.vrt(self.V.tensor_single_scalar(
                        rows[:1, 512:1024], pst2[:1, :], 1.0 / D, ALU.mult))
                    self.vrt(self.V.tensor_mul(rows[:1, 1024:1536],
                                               rows[:1, 0:512], rows[:1, 0:512]))
                    vt = self.vrt(self.V.tensor_sub(rows[:1, 512:1024],
                                                    rows[:1, 512:1024],
                                                    rows[:1, 1024:1536]))
                    self.wait(self.S, vt)
                    st_ = self.inc("s", self.S.activation(
                        rows[:1, 1024:1536], rows[:1, 512:1024], AF.Sqrt,
                        bias=self.epsc[0:1, :1]))
                    self.wait(self.V, st_)
                    vt = self.vrt(self.V.reciprocal(rows[:1, 1024:1536],
                                                    rows[:1, 1024:1536]))
                    self.wait(self.Tn, vt)
                    bt1 = self.inc("t", self.Tn.matmul(
                        pbc[:, :], self.ones_r[:1, :], rows[:1, 0:512],
                        start=True, stop=True))
                    bt2 = self.inc("t", self.Tn.matmul(
                        pbc2[:, :], self.ones_r[:1, :], rows[:1, 1024:1536],
                        start=True, stop=True))
                    self.wait(self.V, bt2)
                    for m in range(8):
                        self.V.tensor_sub(xnf[:, :], xab[:, m * 512:(m + 1) * 512],
                                          pbc[:, :])
                        vt = self.inc("v", self.V.tensor_mul(xnf[:, :], xnf[:, :],
                                                             pbc2[:, :]))
                        self.wait(self.S, vt)
                        self.inc("s", self.S.activation(
                            self.xffn[:, m * T + t0:m * T + t0 + 512],
                            xnf[:, :], AF.Identity))
                        self.wait(self.Tn, vt)
                        glast = self.Tn.matmul(
                            pgate[0:14, 0:512], gateWT[:, m * 14:(m + 1) * 14],
                            xnf[:, :], start=(m == 0), stop=(m == 7))
                        if m == 7:
                            self.inc("t", glast)
                        for b in range(4):
                            tt_ = self.inc("t", self.Tn.transpose(
                                ptr[:, b * 128:(b + 1) * 128],
                                xnf[:, b * 128:(b + 1) * 128], self.idt[:, :]))
                            self.wait(self.V, tt_)
                            vt2 = self.inc("v", self.V.tensor_copy(
                                self.acc[:, (tt * 4 + b) * 1024 + m * 128:
                                         (tt * 4 + b) * 1024 + (m + 1) * 128],
                                ptr[:, b * 128:(b + 1) * 128]))
                            self.wait(self.Tn, vt2)
                    vt = self.tok("v")
                    self.wait(self.S, vt)
                    for b in range(4):
                        self.stdma(self.d_xn[t0 + b * 128:t0 + (b + 1) * 128, :],
                                   self.acc[:, (tt * 4 + b) * 1024:
                                            (tt * 4 + b) * 1024 + 1024])
                    self.gating_tile(tt, stage, pgate, pbc2)
            self.bar()

    def gating_tile(self, tt, lgsb, pgf, ptf):
        t0 = tt * 512
        pg = pgf[0:14, 0:512]
        pt2 = ptf[0:128, 0:14]
        with ExitStack() as es:
            ltm = self.sb("gt_ltm", [128, 14], stack=es)
            prb = self.sb("gt_prb", [128, 14], stack=es)
            m8 = self.sb("gt_m8", [128, 8], stack=es)
            nmx = self.sb("gt_nmx", [128, 1], stack=es)
            den = self.sb("gt_den", [128, 1], stack=es)
            msk = self.sb("gt_msk", [128, 14], stack=es)
            self.wait(self.S, self.tok("t"))
            st_ = self.inc("s", self.S.activation(lgsb[0:14, 0:512], pg,
                                                  AF.Identity))
            self.wait(self.Tn, st_)
            for b in range(4):
                tt_ = self.inc("t", self.Tn.transpose(
                    pt2, lgsb[0:14, b * 128:(b + 1) * 128], self.idt[0:14, 0:14]))
                self.wait(self.V, tt_)
                self.vrt(self.V.tensor_copy(ltm[:, :], pt2))
                self.wait(self.Tn, self.tok("v"))
                self.vrt(self.V.reduce_max(nmx[:, :1], ltm[:, :], axis=AX,
                                           negate=True))
                vt = self.tok("v")
                self.wait(self.S, vt)
                st_ = self.inc("s", self.S.activation(prb[:, :], ltm[:, :], AF.Exp,
                                                      bias=nmx[:, :1]))
                self.wait(self.V, st_)
                self.vrt(self.V.max(out=m8[:, :], in_=prb[:, :]))
                self.vrt(self.V.tensor_add(den[:, :1], m8[:, 0:1], m8[:, 1:2]))
                self.vrt(self.V.reciprocal(den[:, :1], den[:, :1]))
                self.vrt(self.V.tensor_scalar(msk[:, :], prb[:, :], m8[:, 1:2],
                                              None, ALU.is_ge))
                self.V.tensor_mul(msk[:, :], msk[:, :], prb[:, :])
                vt = self.vrt(self.V.tensor_scalar(
                    self.coeff[:, (tt * 4 + b) * NE:(tt * 4 + b + 1) * NE],
                    msk[:, :], den[:, :1], None, ALU.mult))
            self.wait(self.SY, self.tok("t"))

    def phase_ffn(self):
        self.dma_quiesce(self.V, ("ds",))
        with ExitStack() as es:
            cfm_es = ExitStack()
            cfm = self.sb("coeff_fm", [14, 16 * 128], stack=cfm_es)
            with ExitStack() as es2:
                pt2 = self.ps("ffn_pt", [14, 128], es2)
                vt = self.tok("v")
                self.wait(self.Tn, vt)
                for tm in range(16):
                    tt_ = self.inc("t", self.Tn.transpose(
                        pt2[:, :], self.coeff[:, tm * NE:(tm + 1) * NE],
                        self.idt[:, :]))
                    self.wait(self.S, tt_)
                    st_ = self.inc("s", self.S.activation(
                        cfm[0:14, tm * 128:(tm + 1) * 128], pt2[0:14, :],
                        AF.Identity))
                    self.wait(self.Tn, st_)
                pb2 = self.ps("ffn_pb2", [128, 512], es2)
                for tm in range(16):
                    for dn in range(2):
                        t_ = self.inc("t", self.Tn.matmul(
                            pb2[:, :], cfm[0:14, tm * 128:(tm + 1) * 128],
                            self.b2sb[0:14, dn * 512:(dn + 1) * 512],
                            start=True, stop=True))
                        self.wait(self.V, t_)
                        vt = self.inc("v", self.V.tensor_add(
                            self.acc[:, tm * 1024 + dn * 512:
                                     tm * 1024 + (dn + 1) * 512],
                            self.acc[:, tm * 1024 + dn * 512:
                                     tm * 1024 + (dn + 1) * 512],
                            pb2[:, :]))
                        self.wait(self.Tn, vt)
            self.bar("vst")
            cfm_es.close()
            wsl = [self.sb(f"wsl{j}", [128, 16384], BF16, es) for j in range(2)]
            h1 = self.sb("ffn_h1", [128, 16 * 1024], BF16, es)
            pW = [self.ps(f"ffn_pw{j}", [128, 512], es) for j in range(2)]
            wtoks = [self.gld(wsl[0][:, :], self.d_w1[0][:, :]),
                     self.gld(wsl[1][:, :], self.d_w2[0][:, :])]
            w1_use = None
            w2_use = None
            for e in range(NE):
                self.wait(self.Tn, wtoks[0])
                for sp in range(2):
                    for fm in range(16):
                        i = self.pmm_next()
                        if self.pmm_free[i ^ 1] is not None:
                            pass
                        last = None
                        for kc in range(8):
                            lhs = wsl[0][:, kc * FF + fm * 128:kc * FF + (fm + 1) * 128]
                            self.Tn.matmul(
                                self.pmm[i][:, :], lhs,
                                self.xffn[:, kc * T + sp * 1024:kc * T + sp * 1024 + 512],
                                start=(kc == 0), stop=(kc == 7))
                            last = self.Tn.matmul(
                                pW[i][:, :], lhs,
                                self.xffn[:, kc * T + sp * 1024 + 512:
                                          kc * T + sp * 1024 + 1024],
                                start=(kc == 0), stop=(kc == 7))
                        t_ = self.inc("t", last)
                        self.wait(self.S, t_)
                        self.S.activation(
                            h1[:, fm * 1024:fm * 1024 + 512], self.pmm[i][:, :],
                            AF.Gelu, bias=self.col("b1", e * 16 + fm))
                        st_ = self.inc("s", self.S.activation(
                            h1[:, fm * 1024 + 512:(fm + 1) * 1024], pW[i][:, :],
                            AF.Gelu, bias=self.col("b1", e * 16 + fm)))
                        self.pmm_free[i] = st_
                    if sp == 1:
                        w1_use = self.tok("t")
                        if e + 1 < NE:
                            self.wait(self.G, w1_use)
                            wtoks[0] = self.gld(wsl[0][:, :], self.d_w1[e + 1][:, :])
                    st_ = self.tok("s")
                    self.wait(self.Tn, st_)
                    self.wait(self.Tn, wtoks[1])
                    w2_free = [None, None]
                    w2_banks = [self.pmm[0], self.pmm[1], pW[0], pW[1]]
                    for tm4 in range(8):
                        tm = sp * 8 + tm4
                        b0 = w2_banks[(tm4 % 2) * 2]
                        b1 = w2_banks[(tm4 % 2) * 2 + 1]
                        if tm4 >= 2:
                            self.wait(self.Tn, w2_free[tm4 % 2])
                        else:
                            if self.pmm_free[0] is not None:
                                self.wait(self.Tn, self.pmm_free[0])
                            if self.pmm_free[1] is not None:
                                self.wait(self.Tn, self.pmm_free[1])
                        last = None
                        for kf in range(16):
                            lhs = h1[:, kf * 1024 + tm4 * 128:kf * 1024 + (tm4 + 1) * 128]
                            self.Tn.matmul(
                                b0[:, :], lhs,
                                wsl[1][:, kf * 1024:kf * 1024 + 512],
                                start=(kf == 0), stop=(kf == 15))
                            last = self.Tn.matmul(
                                b1[:, :], lhs,
                                wsl[1][:, kf * 1024 + 512:(kf + 1) * 1024],
                                start=(kf == 0), stop=(kf == 15))
                        t_ = self.inc("t", last)
                        self.wait(self.V, t_)
                        self.V.scalar_tensor_tensor(
                            self.acc[:, tm * 1024:tm * 1024 + 512], b0[:, :],
                            self.coeff[:, tm * NE + e:tm * NE + e + 1],
                            self.acc[:, tm * 1024:tm * 1024 + 512],
                            ALU.mult, ALU.add)
                        vt = self.inc("v", self.V.scalar_tensor_tensor(
                            self.acc[:, tm * 1024 + 512:(tm + 1) * 1024], b1[:, :],
                            self.coeff[:, tm * NE + e:tm * NE + e + 1],
                            self.acc[:, tm * 1024 + 512:(tm + 1) * 1024],
                            ALU.mult, ALU.add))
                        w2_free[tm4 % 2] = vt
                    self.pmm_free = [vt, vt]
                    self.pmm_i = 0
                w2_use = self.tok("t")
                if e + 1 < NE:
                    self.wait(self.G, w2_use)
                    wtoks[1] = self.gld(wsl[1][:, :], self.d_w2[e + 1][:, :])
            self.bar()

    def phase_lnf(self):
        with ExitStack() as es:
            xna = self.sb("lnf_xna", [128, NTOK * 1024], stack=es)
            mu = self.sb("lnf_mu", [128, NTOK], stack=es)
            var = self.sb("lnf_var", [128, NTOK], stack=es)
            mu2 = self.sb("lnf_mu2", [128, NTOK], stack=es)
            rstd = self.sb("lnf_rstd", [128, NTOK], stack=es)
            ot = [self.sb(f"lnf_out{j}", [128, 1024], stack=es) for j in range(2)]
            dts = []
            for i in range(NTOK):
                dts.append(self.ld(xna[:, i * 1024:(i + 1) * 1024],
                                   self.d_xn[i * 128:(i + 1) * 128, :]))
            a = self.acc[:, :]
            self.V.tensor_scalar(a, a, -100.0, 100.0, ALU.max, ALU.min)
            for t_ in dts:
                self.wait(self.V, t_)
            vt = self.inc("v", self.V.tensor_add(xna[:, :], xna[:, :], a))
            self.wait(self.S, vt)
            self.vrt(self.V.tensor_reduce(
                mu[:, :], xna[:, :].rearrange("p (i d) -> p i d", d=1024), AX,
                ALU.add))
            self.vrt(self.V.tensor_single_scalar(mu[:, :], mu[:, :], 1.0 / D,
                                                 ALU.mult))
            st_ = self.inc("s", self.S.activation(self.acc[:, :], xna[:, :],
                                                  AF.Square))
            self.wait(self.V, st_)
            self.vrt(self.V.tensor_reduce(
                var[:, :], self.acc[:, :].rearrange("p (i d) -> p i d", d=1024),
                AX, ALU.add))
            self.vrt(self.V.tensor_single_scalar(var[:, :], var[:, :], 1.0 / D,
                                                 ALU.mult))
            self.vrt(self.V.tensor_mul(mu2[:, :], mu[:, :], mu[:, :]))
            vt = self.vrt(self.V.tensor_sub(var[:, :], var[:, :], mu2[:, :]))
            self.wait(self.S, vt)
            st_ = self.inc("s", self.S.activation(rstd[:, :], var[:, :], AF.Sqrt,
                                                  bias=self.epsc[:, :1]))
            self.wait(self.V, st_)
            self.vrt(self.V.reciprocal(rstd[:, :], rstd[:, :]))
            ot_free = [None, None]
            for tm in range(NTOK):
                par = tm % 2
                if ot_free[par] is not None:
                    self.wait(self.V, ot_free[par])
                vt = self.inc("v", self.V.tensor_scalar(
                    ot[par][:, :], xna[:, tm * 1024:(tm + 1) * 1024],
                    mu[:, tm:tm + 1], rstd[:, tm:tm + 1],
                    ALU.subtract, ALU.mult))
                self.wait(self.S, vt)
                ot_free[par] = self.stdma(self.d_out[tm * 128:(tm + 1) * 128, :],
                                          ot[par][:, :])


_BUILT = None


def _get_built():
    global _BUILT
    if _BUILT is None:
        kb = KB()
        kb.build()
        _BUILT = kb
    return _BUILT


def _pack_shared(I):
    f32 = lambda a: np.ascontiguousarray(np.asarray(a, dtype=np.float32))
    out = {}
    out["identity"] = np.eye(128, dtype=np.float32)
    bsel = np.zeros((48, NS * 128), np.float32)
    for n in range(NS):
        bsel[n, n * 128:(n + 1) * 128] = 1.0
        bsel[32 + n, n * 128:(n + 1) * 128] = 1.0
    out["bsel"] = bsel
    for l in range(NL):
        out[f"inWT_p{l}"] = f32(np.asarray(I["L0_attn_in_W"])[l].T
                                .reshape(4, 128, FF).transpose(1, 0, 2)
                                .reshape(128, 4 * FF))
        out[f"moWT_p{l}"] = f32(np.asarray(I["L0_attn_mo_W"])[l].T
                                .reshape(8, 128, 512).transpose(1, 0, 2)
                                .reshape(128, 8 * 512))
        out[f"xpWT_p{l}"] = f32(np.asarray(I["L0_attn_xp_W"])[l].T
                                .reshape(8, 128, 64).transpose(1, 0, 2)
                                .reshape(128, 8 * 64))
        out[f"dtWT_p{l}"] = f32(np.asarray(I["L0_attn_dt_W"])[l].T)
    out["poWT_p"] = f32(np.asarray(I["L0_attn_po_W"]).T
                        .reshape(8, 128, 1024).transpose(1, 0, 2)
                        .reshape(128, 8 * 1024))
    out["gateWT_p"] = f32(np.asarray(I["L0_ffn_gate_w"]).T
                          .reshape(8, 128, NE).transpose(1, 0, 2)
                          .reshape(128, 8 * NE))
    w1 = np.asarray(I["L0_ffn_w1"])
    w2 = np.asarray(I["L0_ffn_w2"])
    for e in range(NE):
        out[f"w1_p{e}"] = f32(w1[e].reshape(8, 128, FF).transpose(1, 0, 2)
                              .reshape(128, 8 * FF))
        out[f"w2_p{e}"] = f32(w2[e].reshape(16, 128, 1024).transpose(1, 0, 2)
                              .reshape(128, 16 * 1024))
    out["b2_rows"] = f32(I["L0_ffn_b2"])
    return out


def _pack_pcols_base(I):
    pc = np.zeros((128, PCN), dtype=np.float32)
    def put(name, arr):
        a = np.asarray(arr, dtype=np.float32)
        pc[:, _PC[name]:_PC[name] + a.shape[1]] = a
    put("ln_g", np.asarray(I["L0_attn_ln_g"]).reshape(8, 128).T)
    put("ln_b", np.asarray(I["L0_attn_ln_b"]).reshape(8, 128).T)
    alog = np.asarray(I["L0_attn_A_log"])
    put("alog", alog.reshape(NL, 8, 128, NS).transpose(2, 0, 1, 3)
        .reshape(128, NL * 128))
    put("Dp", np.asarray(I["L0_attn_Dp"]).reshape(NL, 8, 128)
        .transpose(2, 0, 1).reshape(128, NL * 8))
    put("cb", np.asarray(I["L0_attn_conv_b"]).reshape(NL, 8, 128)
        .transpose(2, 0, 1).reshape(128, NL * 8))
    cw = np.asarray(I["L0_attn_conv_w"]).reshape(NL, 8, 128, 4)
    put("cw", cw.transpose(2, 0, 3, 1).reshape(128, NL * 32))
    put("dtb", np.asarray(I["L0_attn_dt_b"]).reshape(NL, 8, 128)
        .transpose(2, 0, 1).reshape(128, NL * 8))
    put("rmsw", np.asarray(I["L0_attn_rms_w"]).reshape(NL, 4, 128)
        .transpose(2, 0, 1).reshape(128, NL * 4))
    put("po_b", np.asarray(I["L0_attn_po_b"]).reshape(8, 128).T)
    put("b1", np.asarray(I["L0_ffn_b1"]).reshape(NE, 16, 128)
        .transpose(2, 0, 1).reshape(128, NE * 16))
    return pc


def kernel(**inputs):
    kb = _get_built()
    shared = _pack_shared(inputs)
    pc_base = _pack_pcols_base(inputs)
    src = np.ascontiguousarray(np.asarray(inputs["L0_attn_src"], np.float32))
    in_maps = []
    for k in range(NCORE):
        seq, t0 = divmod(k * T, L)
        pc = pc_base.copy()
        if k % 4 != 0:
            pc[:, _PC["msel"] + (k - 1)] = 1.0
            pc[:, _PC["hmask"]] = 1.0
        for j in range(NCORE):
            if j // 4 == k // 4 and j < k:
                pc[:, _PC["mselp"] + j] = 1.0
        halo = np.zeros((4, D), np.float32)
        if k % 4 != 0:
            halo[0:3] = src[seq, t0 - 3:t0]
        m = {"src_tm": np.ascontiguousarray(src[seq, t0:t0 + T]),
             "halo_src": halo,
             "pcols": np.ascontiguousarray(pc)}
        m.update(shared)
        in_maps.append(m)
    res = run_bass_kernel_spmd(kb.nc, in_maps, core_ids=list(range(NCORE)))
    out = np.empty((BATCH, L, D), np.float32)
    for k in range(NCORE):
        seq, t0 = divmod(k * T, L)
        out[seq, t0:t0 + T] = res.results[k]["out_tm"]
    return out


# version 11 (final)
# speedup vs baseline: 1.0826x; 1.0072x over previous
import numpy as np
from contextlib import ExitStack

import concourse.bass as bass
import concourse.mybir as mybir
from concourse.bass_utils import run_bass_kernel_spmd

F32 = mybir.dt.float32
BF16 = mybir.dt.bfloat16
AF = mybir.ActivationFunctionType
ALU = mybir.AluOpType
AX = mybir.AxisListType.X

NCORE = 8
BATCH, L, D = 2, 8192, 1024
DC, DI, NS, DTR, NL = 512, 1024, 16, 32, 2
NE, FF = 14, 2048
T = (BATCH * L) // NCORE
HALF = 1024
NTOK = T // 128
EPS = 1e-5

_PC = {}
_pc_off = 0
def _pc(name, n):
    global _pc_off
    _PC[name] = _pc_off
    _pc_off += n
_pc("ln_g", 8); _pc("ln_b", 8)
_pc("alog", NL * 128)
_pc("Dp", NL * 8)
_pc("cb", NL * 8)
_pc("cw", NL * 32)
_pc("dtb", NL * 8)
_pc("rmsw", NL * 4)
_pc("po_b", 8)
_pc("msel", 8)
_pc("mselp", 8)
_pc("hmask", 1)
_pc("b1", NE * 16)
PCN = _pc_off


class KB:
    DEBUG = False

    def __init__(self):
        self.nc = bass.Bass("TRN2", target_bir_lowering=False,
                            detect_race_conditions=False)
        self.stk = ExitStack()
        self.cnt = {}
        self.sems = {}
        self.V = self.nc.vector
        self.S = self.nc.scalar
        self.Tn = self.nc.tensor
        self.G = self.nc.gpsimd
        self.SY = self.nc.sync
        self.E = {"v": self.V, "s": self.S, "t": self.Tn, "g": self.G,
                  "y": self.SY}
        for name in ["v", "s", "t", "g", "cc"]:
            self.sems[name] = self.stk.enter_context(self.nc.semaphore(name))
            self.cnt[name] = 0
        self.pools = {}
        for pname, eng, nslot in [("dl", self.SY, 6), ("ds", self.S, 4),
                                  ("dg", self.G, 4)]:
            slots = []
            for j in range(nslot):
                sn = f"{pname}{j}"
                self.sems[sn] = self.stk.enter_context(self.nc.semaphore(sn))
                self.cnt[sn] = 0
                slots.append(sn)
            self.pools[pname] = {"eng": eng, "slots": slots, "i": 0}

    def _dma(self, pname, out_ap, in_ap):
        p = self.pools[pname]
        sn = p["slots"][p["i"] % len(p["slots"])]
        p["i"] += 1
        if self.cnt[sn] > 0:
            p["eng"].wait_ge(self.sems[sn], self.cnt[sn])
        instr = p["eng"].dma_start(out=out_ap, in_=in_ap)
        instr.then_inc(self.sems[sn], 16)
        self.cnt[sn] += 16
        return (sn, self.cnt[sn])

    def dma_quiesce(self, eng, pnames=("dl", "ds", "dg")):
        for pname in pnames:
            for sn in self.pools[pname]["slots"]:
                if self.cnt[sn] > 0:
                    eng.wait_ge(self.sems[sn], self.cnt[sn])

    _ENG_OF_SEM = {"v": "v", "s": "s", "t": "t", "g": "g"}

    def inc(self, name, instr, amt=1, commit=True):
        instr.then_inc(self.sems[name], amt)
        self.cnt[name] += amt
        if commit and name in self._ENG_OF_SEM:
            self.E[name].wait_ge(self.sems[name], self.cnt[name])
        return (name, self.cnt[name])

    def tok(self, name):
        return (name, self.cnt[name])

    def wait(self, eng, token):
        name, v = token
        if v > 0:
            eng.wait_ge(self.sems[name], v)

    def vrt(self, instr):
        t = self.inc("v", instr)
        self.wait(self.V, t)
        return t

    def bar(self, engines="vstg"):
        toks = []
        for ch in engines:
            toks.append(self.inc(ch, self.E[ch].nop()))
        for ch in engines:
            for t in toks:
                if t[0] != ch:
                    self.wait(self.E[ch], t)
        for t in toks:
            self.wait(self.SY, t)
            if "g" not in engines:
                self.wait(self.G, t)
        self.dma_quiesce(self.SY, ("ds", "dg"))
        self.dma_quiesce(self.G, ("ds", "dl"))

    def ld(self, out_ap, in_ap):
        return self._dma("dl", out_ap, in_ap)

    def stdma(self, out_ap, in_ap):
        return self._dma("ds", out_ap, in_ap)

    def gld(self, out_ap, in_ap):
        return self._dma("dg", out_ap, in_ap)

    _uid = 0

    def _u(self, name):
        KB._uid += 1
        return f"{name}_{KB._uid}"

    def sb(self, name, shape, dtype=F32, stack=None):
        return (stack or self.stk).enter_context(
            self.nc.sbuf_tensor(self._u(name), shape, dtype))

    def ps(self, name, shape, stack=None):
        return (stack or self.stk).enter_context(
            self.nc.psum_tensor(self._u(name), shape, F32))

    def col(self, name, idx=0):
        c = _PC[name] + idx
        return self.pcols[:, c:c + 1]

    def pmm_init(self):
        self.pmm = [self.ps("pmm0", [128, 512]), self.ps("pmm1", [128, 512])]
        self.pmm_free = [None, None]
        self.pmm_i = 0

    def pmm_next(self):
        i = self.pmm_i
        self.pmm_i ^= 1
        if self.pmm_free[i] is not None:
            self.wait(self.Tn, self.pmm_free[i])
        return i

    def build(self):
        nc = self.nc
        self.d_src = nc.dram_tensor("src_tm", [T, D], F32, kind="ExternalInput")
        self.d_halo = nc.dram_tensor("halo_src", [4, D], F32, kind="ExternalInput")
        self.d_pcols = nc.dram_tensor("pcols", [128, PCN], F32, kind="ExternalInput")
        self.d_idt = nc.dram_tensor("identity", [128, 128], F32, kind="ExternalInput")
        self.d_bsel = nc.dram_tensor("bsel", [48, NS * 128], F32, kind="ExternalInput")
        self.d_inWT = [nc.dram_tensor(f"inWT_p{l}", [128, 4 * FF], F32, kind="ExternalInput")
                       for l in range(NL)]
        self.d_moWT = [nc.dram_tensor(f"moWT_p{l}", [128, 8 * 512], F32, kind="ExternalInput")
                       for l in range(NL)]
        self.d_xpWT = [nc.dram_tensor(f"xpWT_p{l}", [128, 8 * 64], F32, kind="ExternalInput")
                       for l in range(NL)]
        self.d_dtWT = [nc.dram_tensor(f"dtWT_p{l}", [32, 1024], F32, kind="ExternalInput")
                       for l in range(NL)]
        self.d_poWT = nc.dram_tensor("poWT_p", [128, 8 * 1024], F32, kind="ExternalInput")
        self.d_gateWT = nc.dram_tensor("gateWT_p", [128, 8 * 14], F32, kind="ExternalInput")
        self.d_w1 = [nc.dram_tensor(f"w1_p{e}", [128, 8 * FF], F32, kind="ExternalInput")
                     for e in range(NE)]
        self.d_w2 = [nc.dram_tensor(f"w2_p{e}", [128, 16 * 1024], F32, kind="ExternalInput")
                     for e in range(NE)]
        self.d_b2r = nc.dram_tensor("b2_rows", [NE, 1024], F32, kind="ExternalInput")
        self.d_out = nc.dram_tensor("out_tm", [T, D], F32, kind="ExternalOutput")
        if KB.DEBUG:
            self.d_dbg1 = nc.dram_tensor("dbg1", [128, 264], F32, kind="ExternalOutput")
            self.d_dbg2 = nc.dram_tensor("dbg2", [128, 128], F32, kind="ExternalOutput")
            self.d_dbg3 = nc.dram_tensor("dbg3", [512, T], F32, kind="ExternalOutput")
            self.d_dbg4 = nc.dram_tensor("dbg4", [512, T], F32, kind="ExternalOutput")
            self.d_dbg5 = nc.dram_tensor("dbg5", [128, 2 * 8192], F32, kind="ExternalOutput")
            self.d_dbg6 = nc.dram_tensor("dbg6", [128, 272], F32, kind="ExternalOutput")
            self.d_dbg7 = nc.dram_tensor("dbg7", [128, 144], F32, kind="ExternalOutput")
            self.d_dbg8 = nc.dram_tensor("dbg8", [128, 8192 + 2048 + 2048], F32, kind="ExternalOutput")
        self.d_x1 = nc.dram_tensor("x1_d", [512, T], F32, kind="Internal")
        self.d_x2 = nc.dram_tensor("x2_d", [512, T], F32, kind="Internal")
        self.d_h = nc.dram_tensor("h_d", [512, T], F32, kind="Internal")
        self.d_xn = nc.dram_tensor("xn_d", [T, D], F32, kind="Internal")
        self.d_ag1i = nc.dram_tensor("ag1_in", [128, 256], F32, kind="Internal")
        self.d_ag1o = nc.dram_tensor("ag1_out", [NCORE * 128, 256], F32, kind="Internal")
        self.d_ag2i = nc.dram_tensor("ag2_in", [128, 12], F32, kind="Internal")
        self.d_ag2o = nc.dram_tensor("ag2_out", [NCORE * 128, 12], F32, kind="Internal")

        self.v_x1 = self.d_x1.ap().rearrange("(c p) t -> p c t", p=128)
        self.v_x2 = self.d_x2.ap().rearrange("(c p) t -> p c t", p=128)
        self.v_h = self.d_h.ap().rearrange("(c p) t -> p c t", p=128)

        self.pcols = self.sb("pcols_sb", [128, PCN])
        self.idt = self.sb("idt_sb", [128, 128])
        self.ones_c = self.sb("ones_c", [128, 1])
        self.ones_r = self.sb("ones_r", [1, 128])
        self.epsc = self.sb("epsc", [128, 1])
        self.A_sb = self.sb("A_sb", [128, NL * 128])
        self.halo_x1 = self.sb("halo_x1", [128, 12])
        self.halo_h = self.sb("halo_h", [128, 12])
        self.hstate = self.sb("hstate", [128, 128])
        self.pfull = self.sb("pfull", [128, 128])
        self.hin = self.sb("hin", [128, 128])
        self.cs = self.sb("cs_tot", [128, 8])

        self.pmm_init()

        t1 = self.ld(self.pcols[:, :], self.d_pcols[:, :])
        t2 = self.ld(self.idt[:, :], self.d_idt[:, :])
        for e in "vstg":
            self.wait(self.E[e], t1)
            self.wait(self.E[e], t2)
        self.V.memset(self.ones_c[:, :], 1.0)
        self.V.memset(self.ones_r[:, :], 1.0)
        self.V.memset(self.epsc[:, :], EPS)
        stok = self.inc("s", self.S.activation(
            self.A_sb[:, :],
            self.pcols[:, _PC["alog"]:_PC["alog"] + NL * 128], AF.Exp))
        self.wait(self.V, stok)
        vt = self.inc("v", self.V.tensor_single_scalar(
            self.A_sb[:, :], self.A_sb[:, :], -1.0, ALU.mult))
        self.wait(self.S, vt)
        self.bar()

        self.phase_ln1()
        for l in range(NL):
            self.mamba_layer(l)
        self.phase_po_ln0()
        self.phase_ffn()
        self.phase_lnf()
        self.dma_quiesce(self.SY)
        return nc

    def phase_ln1(self):
        with ExitStack() as es:
            sxa = self.sb("ln_sxa", [128, NTOK * D], stack=es)
            sqa = self.sb("ln_sqa", [128, NTOK * D], stack=es)
            mu = self.sb("ln_mu", [128, NTOK], stack=es)
            var = self.sb("ln_var", [128, NTOK], stack=es)
            mu2 = self.sb("ln_mu2", [128, NTOK], stack=es)
            rstd = self.sb("ln_rstd", [128, NTOK], stack=es)
            stg = [self.sb(f"ln_st{j}", [128, 8 * 128], stack=es) for j in range(2)]
            ptr = [self.ps(f"ln_ptr{j}", [128, 128], stack=es) for j in range(4)]
            dts = []
            for i in range(NTOK):
                dts.append(self.ld(sxa[:, i * D:(i + 1) * D],
                                   self.d_src[i * 128:(i + 1) * 128, :]))
            for t_ in dts:
                self.wait(self.V, t_)
                self.wait(self.S, t_)
            self.vrt(self.V.tensor_reduce(
                mu[:, :], sxa[:, :].rearrange("p (i d) -> p i d", d=D), AX, ALU.add))
            self.vrt(self.V.tensor_single_scalar(mu[:, :], mu[:, :], 1.0 / D, ALU.mult))
            st_ = self.inc("s", self.S.activation(sqa[:, :], sxa[:, :], AF.Square))
            self.wait(self.V, st_)
            self.vrt(self.V.tensor_reduce(
                var[:, :], sqa[:, :].rearrange("p (i d) -> p i d", d=D), AX, ALU.add))
            self.vrt(self.V.tensor_single_scalar(var[:, :], var[:, :], 1.0 / D, ALU.mult))
            self.vrt(self.V.tensor_mul(mu2[:, :], mu[:, :], mu[:, :]))
            vt = self.vrt(self.V.tensor_sub(var[:, :], var[:, :], mu2[:, :]))
            self.wait(self.S, vt)
            st_ = self.inc("s", self.S.activation(rstd[:, :], var[:, :], AF.Sqrt,
                                                  bias=self.epsc[:, :1]))
            self.wait(self.V, st_)
            self.vrt(self.V.reciprocal(rstd[:, :], rstd[:, :]))
            for i in range(NTOK):
                self.V.tensor_scalar(sqa[:, i * D:(i + 1) * D],
                                     sxa[:, i * D:(i + 1) * D],
                                     mu[:, i:i + 1], rstd[:, i:i + 1],
                                     ALU.subtract, ALU.mult)
            vt = self.vrt(self.V.engine_nop())
            self.wait(self.Tn, vt)
            ptr_free = [None, None, None, None]
            stg_free = [None, None]
            for i in range(NTOK):
                par = i % 2
                if stg_free[par] is not None:
                    self.wait(self.S, stg_free[par])
                ev_last = None
                for c in range(8):
                    p2 = c % 4
                    if ptr_free[p2] is not None:
                        self.wait(self.Tn, ptr_free[p2])
                    tt_ = self.inc("t", self.Tn.transpose(
                        ptr[p2][:, :], sqa[:, i * D + c * 128:i * D + (c + 1) * 128],
                        self.idt[:, :]), commit=False)
                    self.wait(self.S, tt_)
                    ev_last = self.inc("s", self.S.activation(
                        stg[par][:, c * 128:(c + 1) * 128], ptr[p2][:, :],
                        AF.Identity, bias=self.col("ln_b", c),
                        scale=self.col("ln_g", c)), commit=False)
                    ptr_free[p2] = ev_last
                ts0 = i * 128
                self.inc("s", self.S.nop())
                self.wait(self.S, self.tok("s"))
                self.stdma(self.v_x1[:, :, ts0:ts0 + 128], stg[par][:, 0:512])
                self.stdma(self.v_x2[:, :, ts0:ts0 + 128], stg[par][:, 512:1024])
                stg_free[par] = self.stdma(self.v_h[:, :, ts0:ts0 + 128],
                                           stg[par][:, 0:512])
            vt = self.inc("v", self.V.memset(sxa[:, 0:D], 0.0))
            self.wait(self.SY, vt)
            self.wait(self.SY, self.tok("s"))
            dtok = self.ld(sxa[0:4, 0:D], self.d_halo[:, :])
            self.wait(self.V, dtok)
            self.wait(self.S, dtok)
            self.vrt(self.V.reduce_sum(mu[:, 0:1], sxa[:, 0:D], axis=AX))
            self.vrt(self.V.tensor_single_scalar(mu[:, 0:1], mu[:, 0:1], 1.0 / D, ALU.mult))
            st_ = self.inc("s", self.S.activation(sqa[:, 0:D], sxa[:, 0:D], AF.Square))
            self.wait(self.V, st_)
            self.vrt(self.V.reduce_sum(var[:, 0:1], sqa[:, 0:D], axis=AX))
            self.vrt(self.V.tensor_single_scalar(var[:, 0:1], var[:, 0:1], 1.0 / D, ALU.mult))
            self.vrt(self.V.tensor_mul(mu2[:, 0:1], mu[:, 0:1], mu[:, 0:1]))
            vt = self.vrt(self.V.tensor_sub(var[:, 0:1], var[:, 0:1], mu2[:, 0:1]))
            self.wait(self.S, vt)
            st_ = self.inc("s", self.S.activation(rstd[:, 0:1], var[:, 0:1], AF.Sqrt,
                                                  bias=self.epsc[:, :1]))
            self.wait(self.V, st_)
            self.vrt(self.V.reciprocal(rstd[:, 0:1], rstd[:, 0:1]))
            vt = self.inc("v", self.V.tensor_scalar(
                sqa[:, 0:D], sxa[:, 0:D], mu[:, 0:1], rstd[:, 0:1],
                ALU.subtract, ALU.mult))
            self.wait(self.Tn, vt)
            ev_last = None
            for c in range(4):
                p2 = c % 4
                if ptr_free[p2] is not None:
                    self.wait(self.Tn, ptr_free[p2])
                tt_ = self.inc("t", self.Tn.transpose(
                    ptr[p2][:, :], sqa[:, c * 128:(c + 1) * 128], self.idt[:, :]))
                self.wait(self.S, tt_)
                ev_last = self.inc("s", self.S.activation(
                    self.halo_x1[:, c * 3:(c + 1) * 3], ptr[p2][:, 0:3],
                    AF.Identity, bias=self.col("ln_b", c),
                    scale=self.col("ln_g", c)))
                ptr_free[p2] = ev_last
            self.wait(self.V, ev_last)
            self.inc("v", self.V.tensor_copy(self.halo_h[:, :], self.halo_x1[:, :]))
            self.bar()

    def compute_rmsh(self, l, rmsh):
        with ExitStack() as es:
            pst = self.ps("rms_pst", [1, T], es)
            hst = [self.sb(f"rms_h{j}", [128, T], stack=es) for j in range(2)]
            sqb = [self.sb(f"rms_sq{j}", [128, T], stack=es) for j in range(2)]
            rr = self.sb("rms_rr", [1, T], stack=es)
            pb = self.ps("rms_pb", [128, 512], es)
            mm_toks = [None, None]
            sq_done = [None, None]
            for cc in range(4):
                par = cc % 2
                if sq_done[par] is not None:
                    self.wait(self.SY, sq_done[par])
                dtok = self.ld(hst[par][:, :], self.v_h[:, cc, :])
                self.wait(self.S, dtok)
                if mm_toks[par] is not None:
                    self.wait(self.S, mm_toks[par])
                st_ = self.inc("s", self.S.activation(sqb[par][:, :], hst[par][:, :], AF.Square))
                sq_done[par] = st_
                self.wait(self.Tn, st_)
                last = None
                for tt in range(4):
                    last = self.Tn.matmul(pst[:, tt * 512:(tt + 1) * 512],
                                          self.ones_c[:, :1],
                                          sqb[par][:, tt * 512:(tt + 1) * 512],
                                          start=(cc == 0), stop=(cc == 3))
                mm_toks[par] = self.inc("t", last)
            self.wait(self.V, mm_toks[(4 - 1) % 2])
            vt = self.vrt(self.V.tensor_single_scalar(rr[:1, :], pst[:1, :], 1.0 / DC, ALU.mult))
            self.wait(self.S, vt)
            st_ = self.inc("s", self.S.activation(rr[:1, :], rr[:1, :], AF.Sqrt,
                                                  bias=self.epsc[0:1, :1]))
            self.wait(self.V, st_)
            vt = self.vrt(self.V.reciprocal(rr[:1, :], rr[:1, :]))
            self.wait(self.Tn, vt)
            self.wait(self.SY, self.tok("s"))
            hs_used = [None, None]
            for tt in range(4):
                bt = self.inc("t", self.Tn.matmul(pb[:, :], self.ones_r[:1, :],
                                                  rr[:1, tt * 512:(tt + 1) * 512],
                                                  start=True, stop=True))
                vlast = None
                for cc in range(4):
                    par = cc % 2
                    if hs_used[par] is not None:
                        self.wait(self.SY, hs_used[par])
                    dtok = self.ld(hst[par][:, 0:512], self.v_h[:, cc, tt * 512:(tt + 1) * 512])
                    self.wait(self.V, dtok)
                    self.wait(self.V, bt)
                    vlast = self.inc("v", self.V.scalar_tensor_tensor(
                        rmsh[:, cc * T + tt * 512: cc * T + (tt + 1) * 512],
                        hst[par][:, 0:512], self.col("rmsw", l * 4 + cc),
                        pb[:, :], ALU.mult, ALU.mult))
                    hs_used[par] = vlast
                self.wait(self.Tn, vlast)
            self.bar()

    def halo_xcpre(self, l, inWT, hxc):
        with ExitStack() as es:
            psq = self.ps("hx_psq", [1, 4], es)
            pb = self.ps("hx_pb", [128, 4], es)
            pm = self.ps("hx_pm", [128, 4], es)
            sqh = self.sb("hx_sq", [128, 12], stack=es)
            rr = self.sb("hx_rr", [1, 4], stack=es)
            rmf = self.sb("hx_rmf", [128, 12], stack=es)
            rmh = self.sb("hx_rmh", [128, 12], BF16, stack=es)
            st_ = self.inc("s", self.S.activation(sqh[:, :], self.halo_h[:, :], AF.Square))
            self.wait(self.Tn, st_)
            last = None
            for cc in range(4):
                last = self.Tn.matmul(psq[:1, 0:3], self.ones_c[:, :1],
                                      sqh[:, cc * 3:(cc + 1) * 3],
                                      start=(cc == 0), stop=(cc == 3))
            t_ = self.inc("t", last)
            self.wait(self.V, t_)
            vt = self.vrt(self.V.tensor_single_scalar(rr[:1, 0:3], psq[:1, 0:3], 1.0 / DC, ALU.mult))
            self.wait(self.S, vt)
            st_ = self.inc("s", self.S.activation(rr[:1, 0:3], rr[:1, 0:3], AF.Sqrt,
                                                  bias=self.epsc[0:1, :1]))
            self.wait(self.V, st_)
            vt = self.vrt(self.V.reciprocal(rr[:1, 0:3], rr[:1, 0:3]))
            self.wait(self.Tn, vt)
            bt = self.inc("t", self.Tn.matmul(pb[:, 0:3], self.ones_r[:1, :],
                                              rr[:1, 0:3], start=True, stop=True))
            self.wait(self.V, bt)
            vlast = None
            for cc in range(4):
                vlast = self.inc("v", self.V.scalar_tensor_tensor(
                    rmf[:, cc * 3:(cc + 1) * 3], self.halo_h[:, cc * 3:(cc + 1) * 3],
                    self.col("rmsw", l * 4 + cc), pb[:, 0:3], ALU.mult, ALU.mult))
            self.wait(self.S, vlast)
            st_ = self.inc("s", self.S.activation(rmh[:, :], rmf[:, :], AF.Identity))
            self.wait(self.Tn, st_)
            for j in range(8):
                last = None
                for cc in range(4):
                    last = self.Tn.matmul(pm[:, 0:3],
                                          inWT[:, cc * FF + j * 128:cc * FF + (j + 1) * 128],
                                          rmh[:, cc * 3:(cc + 1) * 3],
                                          start=(cc == 0), stop=(cc == 3))
                t_ = self.inc("t", last)
                self.wait(self.S, t_)
                st_ = self.inc("s", self.S.activation(
                    hxc[:, j * 3:(j + 1) * 3], pm[:, 0:3], AF.Identity,
                    scale=self.col("hmask")))
                self.wait(self.Tn, st_)
            self.bar()

    def mamba_layer(self, l):
        with ExitStack() as es:
            inWT = self.sb("inWT", [128, 4 * FF], BF16, es)
            moWT = self.sb("moWT", [128, 8 * 512], BF16, es)
            xpWT = self.sb("xpWT", [128, 8 * 64], BF16, es)
            dtWT = self.sb("dtWT", [32, 1024], BF16, es)
            rmsh = self.sb("rmsh", [128, 4 * T], BF16, es)
            dbc = self.sb("dbc", [64, T], F32, es)
            dtbf = self.sb("dtbf", [32, T], BF16, es)
            hxc = self.sb("hxc", [128, 24], F32, es)
            self.bsel = self.sb("bsel_sb", [48, NS * 128], stack=es)
            self.ld(self.bsel[:, :], self.d_bsel[:, :])
            gtoks = [self.gld(inWT[:, :], self.d_inWT[l][:, :]),
                     self.gld(moWT[:, :], self.d_moWT[l][:, :]),
                     self.gld(xpWT[:, :], self.d_xpWT[l][:, :]),
                     self.gld(dtWT[:, :], self.d_dtWT[l][:, :])]
            self.compute_rmsh(l, rmsh)
            for e in "vst":
                for gt in gtoks:
                    self.wait(self.E[e], gt)
            self.halo_xcpre(l, inWT, hxc)
            self.mamba_pass(l, True, inWT, moWT, xpWT, dtWT, rmsh, dbc, dtbf, hxc)
            if KB.DEBUG and l == 0:
                self.bar()
                self.stdma(self.d_dbg1[:, 0:128], self.hstate[:, :])
                self.stdma(self.d_dbg1[:, 128:256], self.pfull[:, :])
                self.stdma(self.d_dbg1[:, 256:264], self.cs[:, :])
            self.relay(l)
            if KB.DEBUG and l == 0:
                self.bar()
                self.stdma(self.d_dbg2[:, :], self.hin[:, :])
            self.mamba_pass(l, False, inWT, moWT, xpWT, dtWT, rmsh, dbc, dtbf, hxc)
            if KB.DEBUG:
                self.bar()
                dtg = self.d_dbg3 if l == 0 else self.d_dbg4
                self.gld(dtg.ap(), self.d_h.ap())
            if l < NL - 1:
                self.halo_relay()
            self.bar()

    def _xc_mm(self, l, hf, c, inWT, xcpre):
        for tt in range(2):
            i = self.pmm_next()
            last = None
            for kc in range(4):
                last = self.Tn.matmul(
                    self.pmm[i][:, :],
                    inWT[:, kc * FF + c * 128:kc * FF + (c + 1) * 128],
                    self.rmsh_cur[:, kc * T + hf * HALF + tt * 512:
                                  kc * T + hf * HALF + (tt + 1) * 512],
                    start=(kc == 0), stop=(kc == 3))
            t_ = self.inc("t", last, commit=False)
            self.wait(self.S, t_)
            ev = self.inc("s", self.S.activation(
                xcpre[:, 3 + tt * 512:3 + (tt + 1) * 512], self.pmm[i][:, :],
                AF.Identity), commit=False)
            self.pmm_free[i] = ev
        return ev

    def _xc_conv(self, l, hf, c, xcpre, cvo, out_sb, out_off, halo3, ev):
        self.wait(self.V, ev)
        self.inc("v", self.V.tensor_copy(xcpre[:, 0:3], halo3[:, c * 3:(c + 1) * 3]),
                 commit=False)
        self.V.tensor_scalar(cvo[:, :HALF], xcpre[:, 0:HALF],
                             self.col("cw", l * 32 + 0 * 8 + c), None, ALU.mult)
        for k in range(1, 4):
            self.V.scalar_tensor_tensor(cvo[:, :HALF], xcpre[:, k:k + HALF],
                                        self.col("cw", l * 32 + k * 8 + c),
                                        cvo[:, :HALF], ALU.mult, ALU.add)
        vt = self.inc("v", self.V.tensor_copy(halo3[:, c * 3:(c + 1) * 3],
                                              xcpre[:, 1024:1027]))
        self.wait(self.S, vt)
        st_ = self.inc("s", self.S.activation(out_sb[:, out_off:out_off + HALF],
                                              cvo[:, :HALF], AF.Silu,
                                              bias=self.col("cb", l * 8 + c)))
        return st_

    def _xc_chunk(self, l, hf, c, inWT, xcpre, cvo, out_sb, out_off, halo3):
        for tt in range(2):
            i = self.pmm_next()
            last = None
            for kc in range(4):
                last = self.Tn.matmul(
                    self.pmm[i][:, :],
                    inWT[:, kc * FF + c * 128:kc * FF + (c + 1) * 128],
                    self.rmsh_cur[:, kc * T + hf * HALF + tt * 512:
                                  kc * T + hf * HALF + (tt + 1) * 512],
                    start=(kc == 0), stop=(kc == 3))
            t_ = self.inc("t", last)
            self.wait(self.S, t_)
            ev = self.inc("s", self.S.activation(
                xcpre[:, 3 + tt * 512:3 + (tt + 1) * 512], self.pmm[i][:, :],
                AF.Identity))
            self.pmm_free[i] = ev
        self.wait(self.V, self.tok("s"))
        self.inc("v", self.V.tensor_copy(xcpre[:, 0:3], halo3[:, c * 3:(c + 1) * 3]))
        self.V.tensor_scalar(cvo[:, :], xcpre[:, 0:HALF],
                             self.col("cw", l * 32 + 0 * 8 + c), None, ALU.mult)
        for k in range(1, 4):
            self.V.scalar_tensor_tensor(cvo[:, :], xcpre[:, k:k + HALF],
                                        self.col("cw", l * 32 + k * 8 + c),
                                        cvo[:, :], ALU.mult, ALU.add)
        vt = self.inc("v", self.V.tensor_copy(halo3[:, c * 3:(c + 1) * 3],
                                              xcpre[:, 1024:1027]))
        self.wait(self.S, vt)
        st_ = self.inc("s", self.S.activation(out_sb[:, out_off:out_off + HALF],
                                              cvo[:, :], AF.Silu,
                                              bias=self.col("cb", l * 8 + c)))
        return st_

    def _delta_mm(self, l, hf, c, dtWT, dtbf, pD, pD_free):
        par = c % 2
        if pD_free[par] is not None:
            self.wait(self.Tn, pD_free[par])
        last = None
        for tt in range(2):
            last = self.Tn.matmul(
                pD[:, par * HALF + tt * 512:par * HALF + (tt + 1) * 512],
                dtWT[:, c * 128:(c + 1) * 128],
                dtbf[:, hf * HALF + tt * 512:hf * HALF + (tt + 1) * 512],
                start=True, stop=True)
        return self.inc("t", last, commit=False)

    def _delta_sp(self, l, hf, c, pD, delta, scratch, t_):
        par = c % 2
        self.wait(self.S, t_)
        self.S.activation(scratch[:, 0:HALF], pD[:, par * HALF:(par + 1) * HALF],
                          AF.Exp, bias=self.col("dtb", l * 8 + c))
        st_ = self.inc("s", self.S.activation(delta[:, c * HALF:(c + 1) * HALF],
                                              scratch[:, 0:HALF], AF.Ln, bias=1.0),
                       commit=False)
        return st_

    def _delta_chunk(self, l, hf, c, dtWT, dtbf, pD, delta, scratch):
        self.wait(self.Tn, self.tok("s"))
        last = None
        for tt in range(2):
            last = self.Tn.matmul(
                pD[:, tt * 512:(tt + 1) * 512], dtWT[:, c * 128:(c + 1) * 128],
                dtbf[:, hf * HALF + tt * 512:hf * HALF + (tt + 1) * 512],
                start=True, stop=True)
        t_ = self.inc("t", last)
        self.wait(self.S, t_)
        self.S.activation(scratch[:, 0:HALF], pD[:, :HALF], AF.Exp,
                          bias=self.col("dtb", l * 8 + c))
        st_ = self.inc("s", self.S.activation(delta[:, c * HALF:(c + 1) * HALF],
                                              scratch[:, 0:HALF], AF.Ln, bias=1.0))
        return st_

    def mamba_pass(self, l, pa, inWT, moWT, xpWT, dtWT, rmsh, dbc, dtbf, hxc):
        self.rmsh_cur = rmsh
        with ExitStack() as pes:
            halo3 = self.sb("halo3", [128, 24], stack=pes)
            xcpre = self.sb("xcpre", [128, 1032], stack=pes)
            cvo = self.sb("cvo", [128, HALF], stack=pes)
            xcf = self.sb("xcf", [128, HALF], stack=pes)
            dAb = self.sb("dAb", [128, 2 * HALF], stack=pes)
            dBxb = self.sb("dBxb", [128, 2 * HALF], stack=pes)
            hscr = self.sb("hscr", [128, HALF], stack=pes)
            bmc = self.sb("bmc", [48, HALF], stack=pes)
            Bmb = self.sb("Bmb", [128, HALF], stack=pes)
            if pa:
                xcbf = self.sb("xcbf", [128, 2 * HALF], BF16, stack=pes)
                csh = self.sb("csh", [128, 16], stack=pes)
                hpB = self.sb("hpB", [128, 128], stack=pes)
                state = self.hstate
                self.V.memset(self.hstate[:, :], 0.0)
                vt = self.inc("v", self.V.memset(hpB[:, :], 0.0))
            else:
                state = self.hin
            vt = self.inc("v", self.V.tensor_copy(halo3[:, :], hxc[:, :]))
            self.wait(self.S, vt)
            self.wait(self.Tn, vt)

            for hf in range(2):
                with ExitStack() as hes:
                    yh = None
                    if not pa:
                        yh = self.sb("y_h", [128, 8 * HALF], stack=hes)
                    dues = ExitStack()
                    delta = self.sb("delta_h", [128, 8 * HALF], stack=dues)
                    u = self.sb("u_h", [128, 8 * HALF], stack=dues)
                    with ExitStack() as ces:
                        if pa:
                            pdbc_es = ExitStack()
                            pdbc = self.ps("pdbc", [64, T], pdbc_es)
                            bufs = [(xcpre, cvo), (dAb, dBxb)]
                            evs = [None] * 8
                            xbfs = [xcbf[:, 0:HALF], xcbf[:, HALF:2 * HALF]]
                            cast_t = [None, None]
                            for c in range(9):
                                if c < 8:
                                    evs[c] = self._xc_mm(l, hf, c, inWT,
                                                         bufs[c % 2][0])
                                if c >= 1:
                                    cc = c - 1
                                    xp_, cv_ = bufs[cc % 2]
                                    st_ = self._xc_conv(l, hf, cc, xp_, cv_,
                                                        u, cc * HALF, halo3,
                                                        evs[cc])
                                    xb = xbfs[cc % 2]
                                    if cast_t[cc % 2] is not None:
                                        self.wait(self.S, cast_t[cc % 2])
                                    cst = self.inc("s", self.S.activation(
                                        xb, u[:, cc * HALF:(cc + 1) * HALF],
                                        AF.Identity), commit=False)
                                    self.wait(self.Tn, cst)
                                    last = None
                                    for tt in range(2):
                                        last = self.Tn.matmul(
                                            pdbc[:, hf * HALF + tt * 512:
                                                 hf * HALF + (tt + 1) * 512],
                                            xpWT[:, cc * 64:(cc + 1) * 64],
                                            xb[:, tt * 512:(tt + 1) * 512],
                                            start=(cc == 0), stop=(cc == 7))
                                    cast_t[cc % 2] = self.inc("t", last,
                                                              commit=False)
                            self.inc("t", self.Tn.nop())
                            self.inc("s", self.S.nop())
                            t_ = self.tok("t")
                            self.wait(self.S, t_)
                            self.S.activation(dbc[0:64, hf * HALF:(hf + 1) * HALF],
                                              pdbc[:, hf * HALF:(hf + 1) * HALF],
                                              AF.Identity)
                            st_ = self.inc("s", self.S.activation(
                                dtbf[:, hf * HALF:(hf + 1) * HALF],
                                dbc[0:32, hf * HALF:(hf + 1) * HALF], AF.Identity))
                            self.wait(self.Tn, st_)
                            self.wait(self.V, st_)
                            pdbc_es.close()
                            pD = self.ps("pD", [128, 2 * HALF], ces)
                            pD_free = [None, None]
                            dts = [None] * 8
                            sps = [None] * 8
                            sbufs = [cvo, dBxb]
                            for c in range(10):
                                if c < 8:
                                    dts[c] = self._delta_mm(l, hf, c, dtWT, dtbf,
                                                            pD, pD_free)
                                if 1 <= c <= 8:
                                    cc = c - 1
                                    sps[cc] = self._delta_sp(l, hf, cc, pD, delta,
                                                             sbufs[cc % 2], dts[cc])
                                    pD_free[cc % 2] = sps[cc]
                                if c >= 2:
                                    cc = c - 2
                                    self.wait(self.V, sps[cc])
                                    vt = self.inc("v", self.V.tensor_mul(
                                        u[:, cc * HALF:(cc + 1) * HALF],
                                        delta[:, cc * HALF:(cc + 1) * HALF],
                                        u[:, cc * HALF:(cc + 1) * HALF]),
                                        commit=False)

                            if KB.DEBUG and pa and l == 0 and hf == 0:
                                self.bar("vs")
                                self.stdma(self.d_dbg8[:, 0:8192], delta[:, :])
                                self.dma_quiesce(self.V, ("ds",))
                            for c in range(8):
                                dsl = delta[:, c * HALF:(c + 1) * HALF]
                                self.vrt(self.V.tensor_tensor_scan(
                                    dsl, dsl, dsl, 0.0, ALU.add, ALU.bypass))
                                self.vrt(self.V.tensor_copy(
                                    csh[:, hf * 8 + c:hf * 8 + c + 1],
                                    delta[:, (c + 1) * HALF - 1:(c + 1) * HALF]))
                                self.inc("v", self.V.tensor_scalar(
                                    dsl, dsl, csh[:, hf * 8 + c:hf * 8 + c + 1],
                                    -1.0, ALU.subtract, ALU.mult), commit=False)
                            self.vrt(self.V.engine_nop())
                        else:
                            pD = self.ps("pD", [128, 2 * HALF], ces)
                            bufs = [(xcpre, cvo), (dAb, dBxb)]
                            xcfs = [xcf[:, 0:HALF], hscr[:, 0:HALF]]
                            pD_free = [None, None]
                            dts = [None] * 8
                            evs = [None] * 8
                            uts = [None, None]
                            for c in range(9):
                                if c < 8:
                                    dts[c] = self._delta_mm(l, hf, c, dtWT, dtbf,
                                                            pD, pD_free)
                                    evs[c] = self._xc_mm(l, hf, c, inWT,
                                                         bufs[c % 2][0])
                                if c >= 1:
                                    cc = c - 1
                                    par = cc % 2
                                    sp_ = self._delta_sp(
                                        l, hf, cc, pD, delta,
                                        yh[:, cc * HALF:(cc + 1) * HALF], dts[cc])
                                    pD_free[par] = sp_
                                    if uts[par] is not None:
                                        self.wait(self.S, uts[par])
                                    xt = self._xc_conv(l, hf, cc, bufs[par][0],
                                                       bufs[par][1], xcfs[par], 0,
                                                       halo3, evs[cc])
                                    self.wait(self.V, xt)
                                    self.V.tensor_scalar(
                                        yh[:, cc * HALF:(cc + 1) * HALF],
                                        xcfs[par], self.col("Dp", l * 8 + cc),
                                        None, ALU.mult)
                                    uts[par] = self.inc("v", self.V.tensor_mul(
                                        u[:, cc * HALF:(cc + 1) * HALF],
                                        delta[:, cc * HALF:(cc + 1) * HALF],
                                        xcfs[par]), commit=False)
                            self.inc("v", self.V.engine_nop())
                            self.inc("s", self.S.nop())

                    self.wait(self.G, self.tok("v"))
                    self.wait(self.S, self.tok("v"))
                    self.wait(self.SY, self.tok("s"))
                    self.wait(self.SY, self.tok("t"))
                    dtok = self.ld(bmc[0:16, :], dbc[32:48, hf * HALF:(hf + 1) * HALF])
                    if not pa:
                        dtok = self.ld(bmc[32:48, :],
                                       dbc[48:64, hf * HALF:(hf + 1) * HALF])
                    self.wait(self.Tn, dtok)
                    with ExitStack() as nes:
                        pwB = self.ps("pwB", [128, HALF], nes)
                        pwC = self.ps("pwC", [128, HALF], nes)
                        vdone = []
                        gdone = []
                        for n in range(16):
                            npar = n % 2
                            if n > 0:
                                self.wait(self.Tn, vdone[n * 8 - 1])
                            last = None
                            for tt in range(2):
                                last = self.Tn.matmul(
                                    pwB[:, tt * 512:(tt + 1) * 512],
                                    self.bsel[0:16, n * 128:(n + 1) * 128],
                                    bmc[0:16, tt * 512:(tt + 1) * 512],
                                    start=True, stop=True)
                            bt = self.inc("t", last)
                            if not pa:
                                self.wait(self.S, bt)
                                if n >= 1:
                                    self.wait(self.S, gdone[n * 8 - 1])
                                bev = self.inc("s", self.S.activation(
                                    Bmb[:, :], pwB[:, :], AF.Identity))
                            if not pa:
                                self.wait(self.Tn, bev)
                                last = None
                                for tt in range(2):
                                    last = self.Tn.matmul(
                                        pwC[:, tt * 512:(tt + 1) * 512],
                                        self.bsel[32:48, n * 128:(n + 1) * 128],
                                        bmc[32:48, tt * 512:(tt + 1) * 512],
                                        start=True, stop=True)
                                cev = self.inc("t", last)
                            for c in range(8):
                                idx = n * 8 + c
                                par = idx % 2
                                if idx >= 2:
                                    self.wait(self.S, vdone[idx - 2])
                                st_ = self.inc("s", self.S.activation(
                                    dAb[:, par * HALF:(par + 1) * HALF],
                                    delta[:, c * HALF:(c + 1) * HALF], AF.Exp,
                                    scale=self.A_sb[:, l * 128 + c * 16 + n:
                                                    l * 128 + c * 16 + n + 1]),
                                    commit=(idx < 2))
                                if not pa:
                                    if idx >= 2:
                                        self.wait(self.G, vdone[idx - 2])
                                    self.wait(self.G, bev)
                                    gt_ = self.inc("g", self.G.tensor_mul(
                                        dBxb[:, par * HALF:(par + 1) * HALF],
                                        u[:, c * HALF:(c + 1) * HALF],
                                        Bmb[:, :]), commit=(idx < 2))
                                    gdone.append(gt_)
                                self.wait(self.V, st_)
                                col = c * 16 + n
                                if pa:
                                    self.wait(self.V, bt)
                                    self.V.tensor_mul(
                                        dBxb[:, par * HALF:(par + 1) * HALF],
                                        u[:, c * HALF:(c + 1) * HALF],
                                        pwB[:, :])
                                    hp = state if hf == 0 else hpB
                                    self.V.scalar_tensor_tensor(
                                        hscr[:, :],
                                        dAb[:, par * HALF:(par + 1) * HALF],
                                        1.0,
                                        dBxb[:, par * HALF:(par + 1) * HALF],
                                        ALU.bypass, ALU.mult,
                                        accum_out=hp[:, col:col + 1])
                                else:
                                    self.wait(self.V, gt_)
                                    self.V.tensor_tensor_scan(
                                        hscr[:, :],
                                        dAb[:, par * HALF:(par + 1) * HALF],
                                        dBxb[:, par * HALF:(par + 1) * HALF],
                                        state[:, col:col + 1], ALU.mult, ALU.add)
                                    self.V.tensor_copy(state[:, col:col + 1],
                                                       hscr[:, HALF - 1:HALF])
                                if not pa:
                                    self.wait(self.V, cev)
                                    self.V.tensor_mul(
                                        dBxb[:, par * HALF:(par + 1) * HALF],
                                        hscr[:, :], pwC[:, :])
                                    vt = self.inc("v", self.V.tensor_add(
                                        yh[:, c * HALF:(c + 1) * HALF],
                                        yh[:, c * HALF:(c + 1) * HALF],
                                        dBxb[:, par * HALF:(par + 1) * HALF]))
                                else:
                                    vt = self.inc("v", self.V.engine_nop())
                                vdone.append(vt)
                    self.wait(self.S, self.tok("v"))
                    self.bar("vs")
                    if KB.DEBUG and pa and l == 0 and hf == 0:
                        self.stdma(self.d_dbg7[:, 0:128], self.hstate[:, :])
                        self.stdma(self.d_dbg7[:, 128:136], csh[:, 0:8])
                        self.dma_quiesce(self.V, ("ds",))
                    dues.close()

                    if pa and hf == 1:
                        self.vrt(self.V.tensor_add(self.cs[:, :], csh[:, 0:8],
                                                   csh[:, 8:16]))
                        if KB.DEBUG and l == 0:
                            self.bar("vs")
                            self.stdma(self.d_dbg6[:, 0:128], self.hstate[:, :])
                            self.stdma(self.d_dbg6[:, 128:256], hpB[:, :])
                            self.stdma(self.d_dbg6[:, 256:272], csh[:, :])
                            self.dma_quiesce(self.V, ("ds",))
                        self.vrt(self.V.engine_nop())
                        vt = None
                        for c in range(8):
                            vt = self.inc("v", self.V.tensor_mul(
                                self.pfull[:, c * 16:(c + 1) * 16],
                                csh[:, 8 + c:9 + c].to_broadcast([128, 16]),
                                self.A_sb[:, l * 128 + c * 16:l * 128 + (c + 1) * 16]),
                                commit=False)
                        self.wait(self.S, self.inc("v", self.V.engine_nop())[1] and self.tok("v"))
                        st_ = self.inc("s", self.S.activation(
                            self.pfull[:, :], self.pfull[:, :], AF.Exp))
                        self.wait(self.V, st_)
                        self.V.tensor_mul(self.hstate[:, :], self.hstate[:, :],
                                          self.pfull[:, :])
                        vt = self.inc("v", self.V.tensor_add(
                            self.hstate[:, :], self.hstate[:, :], hpB[:, :]))
                        for c in range(8):
                            vt = self.inc("v", self.V.tensor_mul(
                                self.pfull[:, c * 16:(c + 1) * 16],
                                self.cs[:, c:c + 1].to_broadcast([128, 16]),
                                self.A_sb[:, l * 128 + c * 16:l * 128 + (c + 1) * 16]),
                                commit=False)
                        vt = self.vrt(self.V.engine_nop())
                        self.wait(self.S, vt)
                        self.inc("s", self.S.activation(self.pfull[:, :],
                                                        self.pfull[:, :], AF.Exp))

                    if not pa:
                        with ExitStack() as fes:
                            ybf = self.sb("ybf_h", [128, 8 * HALF], BF16, stack=fes)
                            hseg = self.sb("hseg", [128, 4 * HALF], stack=fes)
                            szb = self.sb("szb", [128, 512], stack=fes)
                            htok = self.ld(hseg[:, :].rearrange(
                                "p (c x) -> p c x", c=4),
                                self.v_h[:, :, hf * HALF:(hf + 1) * HALF])
                            for c in range(8):
                                for tt in range(2):
                                    i = self.pmm_next()
                                    last = None
                                    for kc in range(4):
                                        last = self.Tn.matmul(
                                            self.pmm[i][:, :],
                                            inWT[:, kc * FF + (8 + c) * 128:
                                                 kc * FF + (9 + c) * 128],
                                            rmsh[:, kc * T + hf * HALF + tt * 512:
                                                 kc * T + hf * HALF + (tt + 1) * 512],
                                            start=(kc == 0), stop=(kc == 3))
                                    t_ = self.inc("t", last)
                                    self.wait(self.S, t_)
                                    st_ = self.inc("s", self.S.activation(
                                        szb[:, :], self.pmm[i][:, :], AF.Silu))
                                    self.pmm_free[i] = st_
                                    self.wait(self.V, st_)
                                    vt = self.inc("v", self.V.tensor_mul(
                                        yh[:, c * HALF + tt * 512:
                                           c * HALF + (tt + 1) * 512],
                                        yh[:, c * HALF + tt * 512:
                                           c * HALF + (tt + 1) * 512],
                                        szb[:, :]))
                                    self.wait(self.S, vt)
                                vt = self.tok("v")
                                self.wait(self.S, vt)
                                self.inc("s", self.S.activation(
                                    ybf[:, c * HALF:(c + 1) * HALF],
                                    yh[:, c * HALF:(c + 1) * HALF], AF.Identity))
                            st_ = self.tok("s")
                            self.wait(self.Tn, st_)
                            self.wait(self.V, htok)
                            for tt in range(2):
                                for m in range(4):
                                    i = self.pmm_next()
                                    last = None
                                    for c in range(8):
                                        last = self.Tn.matmul(
                                            self.pmm[i][:, :],
                                            moWT[:, c * 512 + m * 128:
                                                 c * 512 + (m + 1) * 128],
                                            ybf[:, c * HALF + tt * 512:
                                                c * HALF + (tt + 1) * 512],
                                            start=(c == 0), stop=(c == 7))
                                    t_ = self.inc("t", last)
                                    self.wait(self.V, t_)
                                    vt = self.inc("v", self.V.tensor_add(
                                        hseg[:, m * HALF + tt * 512:
                                             m * HALF + (tt + 1) * 512],
                                        hseg[:, m * HALF + tt * 512:
                                             m * HALF + (tt + 1) * 512],
                                        self.pmm[i][:, :]))
                                    self.pmm_free[i] = vt
                            vt = self.tok("v")
                            self.wait(self.S, vt)
                            htok = self.stdma(
                                self.v_h[:, :, hf * HALF:(hf + 1) * HALF],
                                hseg[:, :].rearrange("p (c x) -> p c x", c=4))
                            self.wait(self.SY, htok)
            self.bar()

    def relay(self, l):
        vt = self.tok("v")
        self.wait(self.G, vt)
        self.wait(self.G, self.tok("s"))
        g1 = self.gld(self.d_ag1i[:, 0:128], self.hstate[:, :])
        g2 = self.gld(self.d_ag1i[:, 128:256], self.pfull[:, :])
        self.wait(self.G, g1)
        self.wait(self.G, g2)
        ct = self.inc("cc", self.G.collective_compute(
            "AllGather", ALU.bypass,
            replica_groups=[list(range(NCORE))],
            ins=[self.d_ag1i.ap().opt()],
            outs=[self.d_ag1o.ap().opt()]), 1)
        self.wait(self.G, ct)
        with ExitStack() as es:
            gat = self.sb("relay_gat", [128, NCORE * 256], stack=es)
            tmp = self.sb("relay_tmp", [128, 128], stack=es)
            gt = self.gld(gat[:, :].rearrange("p (j c) -> p j c", j=NCORE),
                          self.d_ag1o.ap().rearrange("(j p) c -> p j c", p=128))
            self.wait(self.V, gt)
            self.V.memset(self.hin[:, :], 0.0)
            for j in range(NCORE):
                self.V.tensor_scalar(tmp[:, :],
                                     gat[:, j * 256 + 128:j * 256 + 256],
                                     1.0, self.col("mselp", j),
                                     ALU.subtract, ALU.mult)
                self.V.tensor_scalar(tmp[:, :], tmp[:, :], 1.0, None, ALU.add)
                self.V.tensor_mul(self.hin[:, :], self.hin[:, :], tmp[:, :])
                self.V.tensor_scalar(tmp[:, :], gat[:, j * 256:j * 256 + 128],
                                     self.col("mselp", j), None, ALU.mult)
                self.V.tensor_add(self.hin[:, :], self.hin[:, :], tmp[:, :])
            self.vrt(self.V.engine_nop())
            self.bar()

    def halo_relay(self):
        self.dma_quiesce(self.G, ("ds",))
        gt = self.gld(self.d_ag2i[:, :], self.v_h[:, :, T - 3:T])
        self.wait(self.G, gt)
        ct = self.inc("cc", self.G.collective_compute(
            "AllGather", ALU.bypass,
            replica_groups=[list(range(NCORE))],
            ins=[self.d_ag2i.ap().opt()],
            outs=[self.d_ag2o.ap().opt()]), 1)
        self.wait(self.G, ct)
        with ExitStack() as es:
            gat = self.sb("halo_gat", [128, NCORE * 12], stack=es)
            gt = self.gld(gat[:, :].rearrange("p (j c) -> p j c", j=NCORE),
                          self.d_ag2o.ap().rearrange("(j p) c -> p j c", p=128))
            self.wait(self.V, gt)
            self.V.memset(self.halo_h[:, :], 0.0)
            vt = None
            for j in range(NCORE):
                vt = self.inc("v", self.V.scalar_tensor_tensor(
                    self.halo_h[:, :], gat[:, j * 12:(j + 1) * 12],
                    self.col("msel", j), self.halo_h[:, :], ALU.mult, ALU.add))
            self.bar()

    def phase_po_ln0(self):
        self.xffn = self.sb("xffn_bf", [128, 8 * T], BF16)
        self.acc = self.sb("acc_tm", [128, 16 * 1024], F32)
        self.coeff = self.sb("coeff_tm", [128, 16 * NE], F32)
        self.b2sb = self.sb("b2_sb", [NE, 1024], F32)
        with ExitStack() as es:
            poWT = self.sb("poWT", [128, 8 * 1024], BF16, es)
            gateWT = self.sb("gateWT", [128, 8 * 14], F32, es)
            outbf = self.sb("outbf", [128, 8 * T], BF16, es)
            stage = self.sb("po_stage", [128, T], stack=es)
            gt = self.gld(poWT[:, :], self.d_poWT[:, :])
            gt2 = self.ld(gateWT[:, :], self.d_gateWT[:, :])
            with ExitStack() as es0:
                stage2 = self.sb("po_stage2", [128, T], stack=es0)
                for c in range(4):
                    dt1 = self.ld(stage[:, :], self.v_x1[:, c, :])
                    self.wait(self.S, dt1)
                    st_ = self.inc("s", self.S.activation(
                        outbf[:, c * T:(c + 1) * T], stage[:, :], AF.Identity))
                    self.wait(self.SY, st_)
                for c in range(4):
                    dt1 = self.ld(stage[:, :], self.v_x2[:, c, :])
                    dt2 = self.ld(stage2[:, :], self.v_h[:, c, :])
                    self.wait(self.V, dt2)
                    self.wait(self.V, dt1)
                    vt = self.inc("v", self.V.tensor_add(stage2[:, :], stage2[:, :],
                                                         stage[:, :]))
                    self.wait(self.S, vt)
                    st_ = self.inc("s", self.S.activation(
                        outbf[:, (4 + c) * T:(5 + c) * T], stage2[:, :], AF.Identity))
                    self.wait(self.SY, st_)
                self.bar("vs")
            bt = self.ld(self.b2sb[:, :], self.d_b2r[:, :])
            xab = self.sb("xa_b", [128, 8 * 512], stack=es)
            sqb = self.sb("po_sq", [128, 512], stack=es)
            xnf = self.sb("po_xnf", [128, 512], stack=es)
            srcs = self.sb("po_srcs", [128, D], stack=es)
            rows = self.sb("po_rows", [1, 3 * 512], stack=es)
            with ExitStack() as pses:
                ptr = self.ps("po_ptr", [128, 512], pses)
                pgate = self.ps("po_pgate", [128, 512], pses)
                pst = self.ps("po_pst", [1, 512], pses)
                pst2 = self.ps("po_pst2", [1, 512], pses)
                pbc = self.ps("po_pbc", [128, 512], pses)
                pbc2 = self.ps("po_pbc2", [128, 512], pses)
                self.wait(self.Tn, gt)
                self.wait(self.Tn, gt2)
                self.wait(self.Tn, self.tok("s"))
                for tt in range(4):
                    t0 = tt * 512
                    for m in range(8):
                        i = self.pmm_next()
                        last = None
                        for kc in range(8):
                            last = self.Tn.matmul(
                                self.pmm[i][:, :],
                                poWT[:, kc * 1024 + m * 128:kc * 1024 + (m + 1) * 128],
                                outbf[:, kc * T + t0:kc * T + t0 + 512],
                                start=(kc == 0), stop=(kc == 7))
                        t_ = self.inc("t", last)
                        vsrc = self.v_x1 if m < 4 else self.v_x2
                        dt1 = self.ld(stage[:, 0:512], vsrc[:, m % 4, t0:t0 + 512])
                        self.wait(self.S, t_)
                        st_ = self.inc("s", self.S.activation(
                            xab[:, m * 512:(m + 1) * 512], self.pmm[i][:, :],
                            AF.Identity, bias=self.col("po_b", m)))
                        self.pmm_free[i] = st_
                        self.wait(self.V, st_)
                        self.wait(self.V, dt1)
                        vt = self.inc("v", self.V.tensor_add(
                            xab[:, m * 512:(m + 1) * 512],
                            xab[:, m * 512:(m + 1) * 512], stage[:, 0:512]))
                        self.wait(self.SY, vt)
                    for b in range(4):
                        dt1 = self.ld(srcs[:, :],
                                      self.d_src[t0 + b * 128:t0 + (b + 1) * 128, :])
                        self.wait(self.Tn, dt1)
                        for m in range(8):
                            tt_ = self.inc("t", self.Tn.transpose(
                                ptr[:, b * 128:(b + 1) * 128],
                                srcs[:, m * 128:(m + 1) * 128], self.idt[:, :]))
                            self.wait(self.V, tt_)
                            vt = self.inc("v", self.V.tensor_add(
                                xab[:, m * 512 + b * 128:m * 512 + (b + 1) * 128],
                                xab[:, m * 512 + b * 128:m * 512 + (b + 1) * 128],
                                ptr[:, b * 128:(b + 1) * 128]))
                            self.wait(self.Tn, vt)
                        self.wait(self.SY, self.tok("v"))
                    vt = self.tok("v")
                    self.wait(self.Tn, vt)
                    self.wait(self.S, vt)
                    last = None
                    for m in range(8):
                        last = self.Tn.matmul(pst[:1, :], self.ones_c[:, :1],
                                              xab[:, m * 512:(m + 1) * 512],
                                              start=(m == 0), stop=(m == 7))
                    t_sum = self.inc("t", last)
                    sq_t = None
                    for m in range(8):
                        if sq_t is not None:
                            self.wait(self.S, self.tok("t"))
                        st_ = self.inc("s", self.S.activation(
                            sqb[:, :], xab[:, m * 512:(m + 1) * 512], AF.Square))
                        self.wait(self.Tn, st_)
                        sq_t = self.inc("t", self.Tn.matmul(
                            pst2[:1, :], self.ones_c[:, :1], sqb[:, :],
                            start=(m == 0), stop=(m == 7)))
                    self.wait(self.V, t_sum)
                    self.wait(self.V, sq_t)
                    self.vrt(self.V.tensor_single_scalar(
                        rows[:1, 0:512], pst[:1, :], 1.0 / D, ALU.mult))
                    self.vrt(self.V.tensor_single_scalar(
                        rows[:1, 512:1024], pst2[:1, :], 1.0 / D, ALU.mult))
                    self.vrt(self.V.tensor_mul(rows[:1, 1024:1536],
                                               rows[:1, 0:512], rows[:1, 0:512]))
                    vt = self.vrt(self.V.tensor_sub(rows[:1, 512:1024],
                                                    rows[:1, 512:1024],
                                                    rows[:1, 1024:1536]))
                    self.wait(self.S, vt)
                    st_ = self.inc("s", self.S.activation(
                        rows[:1, 1024:1536], rows[:1, 512:1024], AF.Sqrt,
                        bias=self.epsc[0:1, :1]))
                    self.wait(self.V, st_)
                    vt = self.vrt(self.V.reciprocal(rows[:1, 1024:1536],
                                                    rows[:1, 1024:1536]))
                    self.wait(self.Tn, vt)
                    bt1 = self.inc("t", self.Tn.matmul(
                        pbc[:, :], self.ones_r[:1, :], rows[:1, 0:512],
                        start=True, stop=True))
                    bt2 = self.inc("t", self.Tn.matmul(
                        pbc2[:, :], self.ones_r[:1, :], rows[:1, 1024:1536],
                        start=True, stop=True))
                    self.wait(self.V, bt2)
                    for m in range(8):
                        self.V.tensor_sub(xnf[:, :], xab[:, m * 512:(m + 1) * 512],
                                          pbc[:, :])
                        vt = self.inc("v", self.V.tensor_mul(xnf[:, :], xnf[:, :],
                                                             pbc2[:, :]))
                        self.wait(self.S, vt)
                        self.inc("s", self.S.activation(
                            self.xffn[:, m * T + t0:m * T + t0 + 512],
                            xnf[:, :], AF.Identity))
                        self.wait(self.Tn, vt)
                        glast = self.Tn.matmul(
                            pgate[0:14, 0:512], gateWT[:, m * 14:(m + 1) * 14],
                            xnf[:, :], start=(m == 0), stop=(m == 7))
                        if m == 7:
                            self.inc("t", glast)
                        for b in range(4):
                            tt_ = self.inc("t", self.Tn.transpose(
                                ptr[:, b * 128:(b + 1) * 128],
                                xnf[:, b * 128:(b + 1) * 128], self.idt[:, :]))
                            self.wait(self.V, tt_)
                            vt2 = self.inc("v", self.V.tensor_copy(
                                self.acc[:, (tt * 4 + b) * 1024 + m * 128:
                                         (tt * 4 + b) * 1024 + (m + 1) * 128],
                                ptr[:, b * 128:(b + 1) * 128]))
                            self.wait(self.Tn, vt2)
                    vt = self.tok("v")
                    self.wait(self.S, vt)
                    for b in range(4):
                        self.stdma(self.d_xn[t0 + b * 128:t0 + (b + 1) * 128, :],
                                   self.acc[:, (tt * 4 + b) * 1024:
                                            (tt * 4 + b) * 1024 + 1024])
                    self.gating_tile(tt, stage, pgate, pbc2)
            self.bar()

    def gating_tile(self, tt, lgsb, pgf, ptf):
        t0 = tt * 512
        pg = pgf[0:14, 0:512]
        pt2 = ptf[0:128, 0:14]
        with ExitStack() as es:
            ltm = self.sb("gt_ltm", [128, 14], stack=es)
            prb = self.sb("gt_prb", [128, 14], stack=es)
            m8 = self.sb("gt_m8", [128, 8], stack=es)
            nmx = self.sb("gt_nmx", [128, 1], stack=es)
            den = self.sb("gt_den", [128, 1], stack=es)
            msk = self.sb("gt_msk", [128, 14], stack=es)
            self.wait(self.S, self.tok("t"))
            st_ = self.inc("s", self.S.activation(lgsb[0:14, 0:512], pg,
                                                  AF.Identity))
            self.wait(self.Tn, st_)
            for b in range(4):
                tt_ = self.inc("t", self.Tn.transpose(
                    pt2, lgsb[0:14, b * 128:(b + 1) * 128], self.idt[0:14, 0:14]))
                self.wait(self.V, tt_)
                self.vrt(self.V.tensor_copy(ltm[:, :], pt2))
                self.wait(self.Tn, self.tok("v"))
                self.vrt(self.V.reduce_max(nmx[:, :1], ltm[:, :], axis=AX,
                                           negate=True))
                vt = self.tok("v")
                self.wait(self.S, vt)
                st_ = self.inc("s", self.S.activation(prb[:, :], ltm[:, :], AF.Exp,
                                                      bias=nmx[:, :1]))
                self.wait(self.V, st_)
                self.vrt(self.V.max(out=m8[:, :], in_=prb[:, :]))
                self.vrt(self.V.tensor_add(den[:, :1], m8[:, 0:1], m8[:, 1:2]))
                self.vrt(self.V.reciprocal(den[:, :1], den[:, :1]))
                self.vrt(self.V.tensor_scalar(msk[:, :], prb[:, :], m8[:, 1:2],
                                              None, ALU.is_ge))
                self.V.tensor_mul(msk[:, :], msk[:, :], prb[:, :])
                vt = self.vrt(self.V.tensor_scalar(
                    self.coeff[:, (tt * 4 + b) * NE:(tt * 4 + b + 1) * NE],
                    msk[:, :], den[:, :1], None, ALU.mult))
            self.wait(self.SY, self.tok("t"))

    def phase_ffn(self):
        self.dma_quiesce(self.V, ("ds",))
        with ExitStack() as es:
            cfm_es = ExitStack()
            cfm = self.sb("coeff_fm", [14, 16 * 128], stack=cfm_es)
            with ExitStack() as es2:
                pt2 = self.ps("ffn_pt", [14, 128], es2)
                vt = self.tok("v")
                self.wait(self.Tn, vt)
                for tm in range(16):
                    tt_ = self.inc("t", self.Tn.transpose(
                        pt2[:, :], self.coeff[:, tm * NE:(tm + 1) * NE],
                        self.idt[:, :]))
                    self.wait(self.S, tt_)
                    st_ = self.inc("s", self.S.activation(
                        cfm[0:14, tm * 128:(tm + 1) * 128], pt2[0:14, :],
                        AF.Identity))
                    self.wait(self.Tn, st_)
                pb2 = self.ps("ffn_pb2", [128, 512], es2)
                for tm in range(16):
                    for dn in range(2):
                        t_ = self.inc("t", self.Tn.matmul(
                            pb2[:, :], cfm[0:14, tm * 128:(tm + 1) * 128],
                            self.b2sb[0:14, dn * 512:(dn + 1) * 512],
                            start=True, stop=True))
                        self.wait(self.V, t_)
                        vt = self.inc("v", self.V.tensor_add(
                            self.acc[:, tm * 1024 + dn * 512:
                                     tm * 1024 + (dn + 1) * 512],
                            self.acc[:, tm * 1024 + dn * 512:
                                     tm * 1024 + (dn + 1) * 512],
                            pb2[:, :]))
                        self.wait(self.Tn, vt)
            self.bar("vst")
            cfm_es.close()
            wsl = [self.sb(f"wsl{j}", [128, 16384], BF16, es) for j in range(2)]
            h1 = self.sb("ffn_h1", [128, 16 * 1024], BF16, es)
            pW = [self.ps(f"ffn_pw{j}", [128, 512], es) for j in range(2)]
            wtoks = [self.gld(wsl[0][:, :], self.d_w1[0][:, :]),
                     self.gld(wsl[1][:, :], self.d_w2[0][:, :])]
            w1_use = None
            w2_use = None
            for e in range(NE):
                self.wait(self.Tn, wtoks[0])
                for sp in range(2):
                    for fm in range(16):
                        i = self.pmm_next()
                        if self.pmm_free[i ^ 1] is not None:
                            pass
                        last = None
                        for kc in range(8):
                            lhs = wsl[0][:, kc * FF + fm * 128:kc * FF + (fm + 1) * 128]
                            self.Tn.matmul(
                                self.pmm[i][:, :], lhs,
                                self.xffn[:, kc * T + sp * 1024:kc * T + sp * 1024 + 512],
                                start=(kc == 0), stop=(kc == 7))
                            last = self.Tn.matmul(
                                pW[i][:, :], lhs,
                                self.xffn[:, kc * T + sp * 1024 + 512:
                                          kc * T + sp * 1024 + 1024],
                                start=(kc == 0), stop=(kc == 7))
                        t_ = self.inc("t", last)
                        self.wait(self.S, t_)
                        self.S.activation(
                            h1[:, fm * 1024:fm * 1024 + 512], self.pmm[i][:, :],
                            AF.Gelu, bias=self.col("b1", e * 16 + fm))
                        st_ = self.inc("s", self.S.activation(
                            h1[:, fm * 1024 + 512:(fm + 1) * 1024], pW[i][:, :],
                            AF.Gelu, bias=self.col("b1", e * 16 + fm)))
                        self.pmm_free[i] = st_
                    if sp == 1:
                        w1_use = self.tok("t")
                        if e + 1 < NE:
                            self.wait(self.G, w1_use)
                            wtoks[0] = self.gld(wsl[0][:, :], self.d_w1[e + 1][:, :])
                    st_ = self.tok("s")
                    self.wait(self.Tn, st_)
                    self.wait(self.Tn, wtoks[1])
                    w2_free = [None, None]
                    w2_banks = [self.pmm[0], self.pmm[1], pW[0], pW[1]]
                    for tm4 in range(8):
                        tm = sp * 8 + tm4
                        b0 = w2_banks[(tm4 % 2) * 2]
                        b1 = w2_banks[(tm4 % 2) * 2 + 1]
                        if tm4 >= 2:
                            self.wait(self.Tn, w2_free[tm4 % 2])
                        else:
                            if self.pmm_free[0] is not None:
                                self.wait(self.Tn, self.pmm_free[0])
                            if self.pmm_free[1] is not None:
                                self.wait(self.Tn, self.pmm_free[1])
                        last = None
                        for kf in range(16):
                            lhs = h1[:, kf * 1024 + tm4 * 128:kf * 1024 + (tm4 + 1) * 128]
                            self.Tn.matmul(
                                b0[:, :], lhs,
                                wsl[1][:, kf * 1024:kf * 1024 + 512],
                                start=(kf == 0), stop=(kf == 15))
                            last = self.Tn.matmul(
                                b1[:, :], lhs,
                                wsl[1][:, kf * 1024 + 512:(kf + 1) * 1024],
                                start=(kf == 0), stop=(kf == 15))
                        t_ = self.inc("t", last)
                        self.wait(self.V, t_)
                        self.V.scalar_tensor_tensor(
                            self.acc[:, tm * 1024:tm * 1024 + 512], b0[:, :],
                            self.coeff[:, tm * NE + e:tm * NE + e + 1],
                            self.acc[:, tm * 1024:tm * 1024 + 512],
                            ALU.mult, ALU.add)
                        vt = self.inc("v", self.V.scalar_tensor_tensor(
                            self.acc[:, tm * 1024 + 512:(tm + 1) * 1024], b1[:, :],
                            self.coeff[:, tm * NE + e:tm * NE + e + 1],
                            self.acc[:, tm * 1024 + 512:(tm + 1) * 1024],
                            ALU.mult, ALU.add))
                        w2_free[tm4 % 2] = vt
                    self.pmm_free = [vt, vt]
                    self.pmm_i = 0
                w2_use = self.tok("t")
                if e + 1 < NE:
                    self.wait(self.G, w2_use)
                    wtoks[1] = self.gld(wsl[1][:, :], self.d_w2[e + 1][:, :])
            self.bar()

    def phase_lnf(self):
        with ExitStack() as es:
            xna = self.sb("lnf_xna", [128, NTOK * 1024], stack=es)
            mu = self.sb("lnf_mu", [128, NTOK], stack=es)
            var = self.sb("lnf_var", [128, NTOK], stack=es)
            mu2 = self.sb("lnf_mu2", [128, NTOK], stack=es)
            rstd = self.sb("lnf_rstd", [128, NTOK], stack=es)
            ot = [self.sb(f"lnf_out{j}", [128, 1024], stack=es) for j in range(2)]
            dts = []
            for i in range(NTOK):
                dts.append(self.ld(xna[:, i * 1024:(i + 1) * 1024],
                                   self.d_xn[i * 128:(i + 1) * 128, :]))
            a = self.acc[:, :]
            self.V.tensor_scalar(a, a, -100.0, 100.0, ALU.max, ALU.min)
            for t_ in dts:
                self.wait(self.V, t_)
            vt = self.inc("v", self.V.tensor_add(xna[:, :], xna[:, :], a))
            self.wait(self.S, vt)
            self.vrt(self.V.tensor_reduce(
                mu[:, :], xna[:, :].rearrange("p (i d) -> p i d", d=1024), AX,
                ALU.add))
            self.vrt(self.V.tensor_single_scalar(mu[:, :], mu[:, :], 1.0 / D,
                                                 ALU.mult))
            st_ = self.inc("s", self.S.activation(self.acc[:, :], xna[:, :],
                                                  AF.Square))
            self.wait(self.V, st_)
            self.vrt(self.V.tensor_reduce(
                var[:, :], self.acc[:, :].rearrange("p (i d) -> p i d", d=1024),
                AX, ALU.add))
            self.vrt(self.V.tensor_single_scalar(var[:, :], var[:, :], 1.0 / D,
                                                 ALU.mult))
            self.vrt(self.V.tensor_mul(mu2[:, :], mu[:, :], mu[:, :]))
            vt = self.vrt(self.V.tensor_sub(var[:, :], var[:, :], mu2[:, :]))
            self.wait(self.S, vt)
            st_ = self.inc("s", self.S.activation(rstd[:, :], var[:, :], AF.Sqrt,
                                                  bias=self.epsc[:, :1]))
            self.wait(self.V, st_)
            self.vrt(self.V.reciprocal(rstd[:, :], rstd[:, :]))
            ot_free = [None, None]
            for tm in range(NTOK):
                par = tm % 2
                if ot_free[par] is not None:
                    self.wait(self.V, ot_free[par])
                vt = self.inc("v", self.V.tensor_scalar(
                    ot[par][:, :], xna[:, tm * 1024:(tm + 1) * 1024],
                    mu[:, tm:tm + 1], rstd[:, tm:tm + 1],
                    ALU.subtract, ALU.mult))
                self.wait(self.S, vt)
                ot_free[par] = self.stdma(self.d_out[tm * 128:(tm + 1) * 128, :],
                                          ot[par][:, :])


_BUILT = None


def _get_built():
    global _BUILT
    if _BUILT is None:
        kb = KB()
        kb.build()
        _BUILT = kb
    return _BUILT


def _pack_shared(I):
    f32 = lambda a: np.ascontiguousarray(np.asarray(a, dtype=np.float32))
    out = {}
    out["identity"] = np.eye(128, dtype=np.float32)
    bsel = np.zeros((48, NS * 128), np.float32)
    for n in range(NS):
        bsel[n, n * 128:(n + 1) * 128] = 1.0
        bsel[32 + n, n * 128:(n + 1) * 128] = 1.0
    out["bsel"] = bsel
    for l in range(NL):
        out[f"inWT_p{l}"] = f32(np.asarray(I["L0_attn_in_W"])[l].T
                                .reshape(4, 128, FF).transpose(1, 0, 2)
                                .reshape(128, 4 * FF))
        out[f"moWT_p{l}"] = f32(np.asarray(I["L0_attn_mo_W"])[l].T
                                .reshape(8, 128, 512).transpose(1, 0, 2)
                                .reshape(128, 8 * 512))
        out[f"xpWT_p{l}"] = f32(np.asarray(I["L0_attn_xp_W"])[l].T
                                .reshape(8, 128, 64).transpose(1, 0, 2)
                                .reshape(128, 8 * 64))
        out[f"dtWT_p{l}"] = f32(np.asarray(I["L0_attn_dt_W"])[l].T)
    out["poWT_p"] = f32(np.asarray(I["L0_attn_po_W"]).T
                        .reshape(8, 128, 1024).transpose(1, 0, 2)
                        .reshape(128, 8 * 1024))
    out["gateWT_p"] = f32(np.asarray(I["L0_ffn_gate_w"]).T
                          .reshape(8, 128, NE).transpose(1, 0, 2)
                          .reshape(128, 8 * NE))
    w1 = np.asarray(I["L0_ffn_w1"])
    w2 = np.asarray(I["L0_ffn_w2"])
    for e in range(NE):
        out[f"w1_p{e}"] = f32(w1[e].reshape(8, 128, FF).transpose(1, 0, 2)
                              .reshape(128, 8 * FF))
        out[f"w2_p{e}"] = f32(w2[e].reshape(16, 128, 1024).transpose(1, 0, 2)
                              .reshape(128, 16 * 1024))
    out["b2_rows"] = f32(I["L0_ffn_b2"])
    return out


def _pack_pcols_base(I):
    pc = np.zeros((128, PCN), dtype=np.float32)
    def put(name, arr):
        a = np.asarray(arr, dtype=np.float32)
        pc[:, _PC[name]:_PC[name] + a.shape[1]] = a
    put("ln_g", np.asarray(I["L0_attn_ln_g"]).reshape(8, 128).T)
    put("ln_b", np.asarray(I["L0_attn_ln_b"]).reshape(8, 128).T)
    alog = np.asarray(I["L0_attn_A_log"])
    put("alog", alog.reshape(NL, 8, 128, NS).transpose(2, 0, 1, 3)
        .reshape(128, NL * 128))
    put("Dp", np.asarray(I["L0_attn_Dp"]).reshape(NL, 8, 128)
        .transpose(2, 0, 1).reshape(128, NL * 8))
    put("cb", np.asarray(I["L0_attn_conv_b"]).reshape(NL, 8, 128)
        .transpose(2, 0, 1).reshape(128, NL * 8))
    cw = np.asarray(I["L0_attn_conv_w"]).reshape(NL, 8, 128, 4)
    put("cw", cw.transpose(2, 0, 3, 1).reshape(128, NL * 32))
    put("dtb", np.asarray(I["L0_attn_dt_b"]).reshape(NL, 8, 128)
        .transpose(2, 0, 1).reshape(128, NL * 8))
    put("rmsw", np.asarray(I["L0_attn_rms_w"]).reshape(NL, 4, 128)
        .transpose(2, 0, 1).reshape(128, NL * 4))
    put("po_b", np.asarray(I["L0_attn_po_b"]).reshape(8, 128).T)
    put("b1", np.asarray(I["L0_ffn_b1"]).reshape(NE, 16, 128)
        .transpose(2, 0, 1).reshape(128, NE * 16))
    return pc


def kernel(**inputs):
    kb = _get_built()
    shared = _pack_shared(inputs)
    pc_base = _pack_pcols_base(inputs)
    src = np.ascontiguousarray(np.asarray(inputs["L0_attn_src"], np.float32))
    in_maps = []
    for k in range(NCORE):
        seq, t0 = divmod(k * T, L)
        pc = pc_base.copy()
        if k % 4 != 0:
            pc[:, _PC["msel"] + (k - 1)] = 1.0
            pc[:, _PC["hmask"]] = 1.0
        for j in range(NCORE):
            if j // 4 == k // 4 and j < k:
                pc[:, _PC["mselp"] + j] = 1.0
        halo = np.zeros((4, D), np.float32)
        if k % 4 != 0:
            halo[0:3] = src[seq, t0 - 3:t0]
        m = {"src_tm": np.ascontiguousarray(src[seq, t0:t0 + T]),
             "halo_src": halo,
             "pcols": np.ascontiguousarray(pc)}
        m.update(shared)
        in_maps.append(m)
    res = run_bass_kernel_spmd(kb.nc, in_maps, core_ids=list(range(NCORE)))
    out = np.empty((BATCH, L, D), np.float32)
    for k in range(NCORE):
        seq, t0 = divmod(k * T, L)
        out[seq, t0:t0 + T] = res.results[k]["out_tm"]
    return out
